# version 01
# baseline (speedup 1.0000x reference)
import math
import numpy as np
import concourse.bass as bass
import concourse.mybir as mybir
from concourse.bass_utils import run_bass_kernel_spmd

F32 = mybir.dt.float32
BF16 = mybir.dt.bfloat16
U32 = mybir.dt.uint32
I32 = mybir.dt.int32
AF = mybir.ActivationFunctionType
ALU = mybir.AluOpType
AX = mybir.AxisListType

N_CORES = 8
D = 1024
SEQ = 8192
BATCH = 2
CORES_PER_BATCH = 4
NTOK = 2048
NSTRIP = NTOK // 128
DC = D // 128


STREAMS = ("pe", "dve", "act", "pool", "sp")
BLOCK_ATTR = {"pe": "tensor", "dve": "vector", "act": "scalar",
              "pool": "gpsimd", "sp": "sync"}


class Op:
    __slots__ = ("stream", "fn", "deps", "sem", "amt", "need", "val", "tag", "pre_wait")

    def __init__(self, stream, fn, deps, sem, amt, tag=""):
        self.stream = stream
        self.fn = fn
        self.deps = list(deps)
        self.sem = sem
        self.amt = amt
        self.need = False
        self.val = None
        self.tag = tag
        self.pre_wait = None


N_DMA_SLOTS = 6


class Prog:
    def __init__(self, nc):
        self.nc = nc
        self.ops = {s: [] for s in STREAMS}
        self.all_ops = []
        self.sem_keys = ["pe", "dve", "act", "pool", "sp", "cc"]
        for q in ("sp", "act", "pool"):
            for i in range(N_DMA_SLOTS):
                self.sem_keys.append(f"{q}_dma{i}")
        self.dma_count = {"sp": 0, "act": 0, "pool": 0}
        self.cc_count = 0

    def op(self, stream, fn, deps=(), tag=""):
        deps = [d for d in deps if d is not None]
        o = Op(stream, fn, deps, stream, 1, tag)
        self.ops[stream].append(o)
        self.all_ops.append(o)
        return o

    def dma(self, stream, fn, deps=(), tag=""):
        assert stream in ("sp", "act", "pool")
        deps = [d for d in deps if d is not None]
        k = self.dma_count[stream]
        self.dma_count[stream] += 1
        slot = k % N_DMA_SLOTS
        use = k // N_DMA_SLOTS + 1
        key = f"{stream}_dma{slot}"
        o = Op(stream, fn, deps, key, 16, tag)
        o.need = True
        o.val = 16 * use
        if use > 1:
            o.pre_wait = (key, 16 * (use - 1))
        self.ops[stream].append(o)
        self.all_ops.append(o)
        return o

    def cc(self, fn, deps=(), tag=""):
        deps = [d for d in deps if d is not None]
        self.cc_count += 1
        o = Op("pool", fn, deps, "cc", 1, tag)
        o.need = True
        o.val = self.cc_count
        self.ops["pool"].append(o)
        self.all_ops.append(o)
        return o

    def last(self, stream):
        lst = self.ops[stream]
        return lst[-1] if lst else None

    def emit(self):
        for o in self.all_ops:
            for d in o.deps:
                d.need = True
        counters = {k: 0 for k in self.sem_keys}
        for s in STREAMS:
            for o in self.ops[s]:
                if o.need and o.val is None:
                    counters[o.sem] += o.amt
                    o.val = counters[o.sem]
        nc = self.nc
        sems = {}
        import contextlib
        self._stack = contextlib.ExitStack()
        for k in self.sem_keys:
            sems[k] = self._stack.enter_context(nc.semaphore(k + "_sem"))
        block = self._stack.enter_context(nc.Block())
        for s in STREAMS:
            if not self.ops[s]:
                continue
            watermark = {k: 0 for k in self.sem_keys}
            ops_s = self.ops[s]

            def build(eng, ops_s=ops_s, watermark=watermark):
                for o in ops_s:
                    if o.pre_wait is not None:
                        k, v = o.pre_wait
                        if v > watermark[k]:
                            eng.wait_ge(sems[k], v)
                            watermark[k] = v
                    for d in o.deps:
                        assert d.need and d.val is not None, f"dep of {o.tag} has no val"
                        if d.val > watermark[d.sem]:
                            eng.wait_ge(sems[d.sem], d.val)
                            watermark[d.sem] = d.val
                    inst = o.fn(eng)
                    if o.need:
                        assert inst is not None, f"op {o.tag} needs inc but fn returned None"
                        inst.then_inc(sems[o.sem], o.amt)
            getattr(block, BLOCK_ATTR[s])(build)
        self._stack.close()



def ceil_div(a, b):
    return (a + b - 1) // b


class Alloc:

    def __init__(self, nc, stack):
        self.nc = nc
        self.stack = stack
        self.n = 0

    def sb(self, shape, dtype, name=None):
        self.n += 1
        name = name or f"t{self.n}"
        return self.stack.enter_context(self.nc.sbuf_tensor(name, list(shape), dtype))

    def ps(self, shape, dtype=F32, name=None):
        self.n += 1
        name = name or f"p{self.n}"
        return self.stack.enter_context(self.nc.psum_tensor(name, list(shape), dtype))


def host_inputs_for_core(core, full):
    raise NotImplementedError


def emit_ln_xhatT(P, al, x_dram, xhatT, eye_bf, ln_ps, n_strips=NSTRIP, tag="ln", keep_x=True):
    ntok = n_strips * 128
    if keep_x:
        x_sb = al.sb((128, n_strips * 1024), F32, f"{tag}_x")
    else:
        x_sb = al.sb((128, 2 * 1024), F32, f"{tag}_x")
    stat = al.sb((128, 8 * n_strips), F32, f"{tag}_stat")
    xc = [al.sb((128, 1024), F32, f"{tag}_xc0")] * 2
    xh = [al.sb((128, 1024), BF16, f"{tag}_xh{i}") for i in range(2)]
    sq = None
    x_view = x_dram.rearrange("(s p) d -> p s d", p=128)
    ps_bf = ln_ps.ap().bitcast(BF16)
    x_loads = []
    xT_ops = []
    prev_tp = [None, None]
    prev_o4 = [None, None]
    prev_evac = None
    prev_x_reader = [None, None]
    for i in range(n_strips):
        xslot = i if keep_x else (i % 2)
        xi = x_sb[:, xslot * 1024:(xslot + 1) * 1024]
        ld = P.dma("sp", lambda eng, xi=xi, i=i: eng.dma_start(out=xi, in_=x_view[:, i, :]),
                   deps=([] if keep_x else [prev_x_reader[i % 2]]), tag=f"{tag}_ld{i}")
        x_loads.append(ld)
        st = i % 2
        col = 8 * i
        rs = stat[:, col:col + 1]
        nm = stat[:, col + 1:col + 2]
        ss = stat[:, col + 2:col + 3]
        vv = stat[:, col + 3:col + 4]
        o1 = P.op("dve", lambda eng, xi=xi, rs=rs: eng.tensor_reduce(out=rs, in_=xi, axis=AX.X, op=ALU.add), deps=[ld], tag=f"{tag}_rs{i}")
        o2 = P.op("dve", lambda eng, rs=rs, nm=nm: eng.tensor_scalar_mul(nm, rs, -1.0 / 1024.0), deps=[o1])
        o3 = P.op("act", lambda eng, xi=xi, nm=nm, o=xc[st]: eng.activation(out=o[:], in_=xi, func=AF.Identity, bias=nm), deps=[o2, prev_o4[st], ld], tag=f"{tag}_xc{i}")
        prev_x_reader[st] = o3
        o4 = P.op("act", lambda eng, o=xc[st], ss=ss, s=xh[st]: eng.activation(out=s[:], in_=o[:], func=AF.Square, accum_out=ss), deps=[o3, prev_tp[st]], tag=f"{tag}_sq{i}")
        o5 = P.op("dve", lambda eng, ss=ss, vv=vv: eng.tensor_scalar(out=vv, in0=ss, scalar1=1.0 / 1024.0, scalar2=1e-5, op0=ALU.mult, op1=ALU.add), deps=[o4])
        o6 = P.op("act", lambda eng, vv=vv: eng.activation(out=vv, in_=vv, func=AF.Sqrt), deps=[o5])
        o7 = P.op("dve", lambda eng, vv=vv: eng.reciprocal(out=vv, in_=vv), deps=[o6])
        o8 = P.op("dve", lambda eng, o=xc[st], vv=vv, h=xh[st]: eng.tensor_scalar(out=h[:], in0=o[:], scalar1=vv, scalar2=None, op0=ALU.mult), deps=[o7, prev_tp[st]], tag=f"{tag}_xh{i}")
        prev_o4[0] = prev_o4[1] = o8
        def tp_fn(eng, h=xh[st], ps=ps_bf):
            inst = None
            for c in range(8):
                inst = eng.transpose(ps[:, c * 128:(c + 1) * 128], h[:, c * 128:(c + 1) * 128], eye_bf[:])
            return inst
        o9 = P.op("pe", tp_fn, deps=[o8, prev_evac], tag=f"{tag}_tp{i}")
        prev_tp[st] = o9
        dst = xhatT[:].rearrange("p (c n) -> p c n", c=DC)[:, :, i * 128:(i + 1) * 128]
        o10 = P.op("dve", lambda eng, dst=dst: eng.tensor_copy(out=dst, in_=ps_bf.rearrange("p (c n) -> p c n", c=DC)), deps=[o9], tag=f"{tag}_ev{i}")
        prev_evac = o10
        xT_ops.append(o10)
    return x_sb, x_loads, xT_ops


def load_w_bf16(P, al, w_dram, name, rows=D, cols=D):
    nchunk = rows // 128
    sb = al.sb((128, nchunk * cols), BF16, name + "_sb")
    view = w_dram.rearrange("(c p) o -> p c o", p=128)
    op = P.dma("pool", lambda eng: eng.dma_start(out=sb[:].rearrange("p (c o) -> p c o", c=nchunk), in_=view), tag=f"ldw_{name}")
    return sb, op


class Banks:

    def __init__(self, al, n, shape=(128, 512), dtype=F32, name="bk"):
        self.t = [al.ps(shape, dtype, f"{name}{i}") for i in range(n)]
        self.busy = [None] * n
        self.i = 0

    def get(self):
        i = self.i % len(self.t)
        self.i += 1
        return i, self.t[i], self.busy[i]

    def done(self, i, op):
        self.busy[i] = op


def build_s0a():
    nc = bass.Bass("TRN2", target_bir_lowering=False)
    x = nc.dram_tensor("x", [NTOK, D], F32, kind="ExternalInput").ap()
    wqk = nc.dram_tensor("wqk", [D, D], F32, kind="ExternalInput").ap()
    wv = nc.dram_tensor("wv", [D, D], F32, kind="ExternalInput").ap()
    eye = nc.dram_tensor("eye", [128, 128], BF16, kind="ExternalInput").ap()
    ones = nc.dram_tensor("ones", [128, 1], BF16, kind="ExternalInput").ap()
    qk_o = nc.dram_tensor("qk_o", [NTOK, D], BF16, kind="ExternalOutput").ap()
    vT_o = nc.dram_tensor("vT_o", [D, NTOK], BF16, kind="ExternalOutput").ap()
    dotp_o = nc.dram_tensor("dotp_o", [64, D], F32, kind="ExternalOutput").ap()
    q2p_o = nc.dram_tensor("q2p_o", [1, D], F32, kind="ExternalOutput").ap()

    import contextlib
    with contextlib.ExitStack() as stack:
        al = Alloc(nc, stack)
        P = Prog(nc)
        eye_sb = al.sb((128, 128), BF16, "eye_sb")
        ones_sb = al.sb((128, 1), BF16, "ones_sb")
        ld_eye = P.dma("sp", lambda eng: eng.dma_start(out=eye_sb[:], in_=eye[:, :]))
        ld_ones = P.dma("sp", lambda eng: eng.dma_start(out=ones_sb[:], in_=ones[:, :]))
        xhatT = al.sb((128, DC * NTOK), BF16, "xhatT")
        ln_ps = al.ps((128, 512), F32, "ln_ps")
        x_sb, x_loads, xT_ops = emit_ln_xhatT(P, al, x, xhatT, eye_sb, ln_ps, keep_x=False)
        for o in P.ops["pe"]:
            o.deps.append(ld_eye)
        wqk_sb, wqk_ld = load_w_bf16(P, al, wqk, "wqk")
        wv_sb, wv_ld = load_w_bf16(P, al, wv, "wv")
        qk_sb = al.sb((128, NSTRIP * 1024), BF16, "qk_sb")
        sqst = [al.sb((128, 1024), BF16, f"sqst{i}") for i in range(2)]
        banks = Banks(al, 2, name="pj")
        psA = al.ps((128, 512), F32, "psA")
        psB = al.ps((128, 512), F32, "psB")
        dp = [psA[0:64, :], psB[0:64, :]]
        q2ps = [psA[64:65, :], psB[64:65, :]]
        dot_sb = al.sb((64, 1024), F32, "dot_sb")
        q2_sb = al.sb((1, 1024), F32, "q2_sb")

        qk_ev = [None] * NSTRIP
        sq_ev = [None] * NSTRIP
        q2_mm = [None] * NSTRIP
        for s in range(NSTRIP):
            for nch in range(2):
                bi, ps, war = banks.get()
                def mm_fn(eng, s=s, nch=nch, ps=ps):
                    inst = None
                    for c in range(DC):
                        inst = eng.matmul(ps[:, :],
                                          xhatT[:, c * NTOK + s * 128: c * NTOK + (s + 1) * 128],
                                          wqk_sb[:, c * 1024 + nch * 512: c * 1024 + (nch + 1) * 512],
                                          start=(c == 0), stop=(c == DC - 1))
                    return inst
                mm = P.op("pe", mm_fn, deps=[xT_ops[s], wqk_ld, war], tag=f"qk_mm{s}_{nch}")
                dst = qk_sb[:, s * 1024 + nch * 512: s * 1024 + (nch + 1) * 512]
                cp = P.op("act", lambda eng, dst=dst, ps=ps: eng.activation(out=dst, in_=ps[:, :], func=AF.Copy), deps=[mm], tag=f"qk_ev{s}_{nch}")
                sqdep = [mm, cp] + ([q2_mm[s - 2]] if s >= 2 else [])
                sqd = sqst[s % 2][:, nch * 512:(nch + 1) * 512]
                sq = P.op("act", lambda eng, sqd=sqd, ps=ps: eng.activation(out=sqd, in_=ps[:, :], func=AF.Square), deps=sqdep, tag=f"qk_sq{s}_{nch}")
                banks.done(bi, sq)
                if nch == 1:
                    qk_ev[s] = cp
                    sq_ev[s] = sq
            def q2_fn(eng, s=s):
                inst = None
                for j in range(2):
                    inst = eng.matmul(q2ps[j], ones_sb[:], sqst[s % 2][:, j * 512:(j + 1) * 512],
                                      start=(s == 0), stop=(s == NSTRIP - 1))
                return inst
            q2_mm[s] = P.op("pe", q2_fn, deps=[sq_ev[s], ld_ones], tag=f"q2mm{s}")
        dot_mms = []
        for h in range(16):
            def dot_fn(eng, h=h):
                inst = None
                for s in range(NSTRIP):
                    inst = eng.matmul(dp[h // 8][:, (h % 8) * 64:(h % 8 + 1) * 64],
                                      qk_sb[:, s * 1024 + h * 64: s * 1024 + h * 64 + 64],
                                      qk_sb[:, s * 1024 + h * 64: s * 1024 + h * 64 + 64],
                                      start=(s == 0), stop=(s == NSTRIP - 1))
                return inst
            dot_mms.append(P.op("pe", dot_fn, deps=qk_ev, tag=f"dot{h}"))
        ev_dot0 = P.op("dve", lambda eng: eng.tensor_copy(out=dot_sb[:, 0:512], in_=dp[0]), deps=dot_mms[:8])
        ev_dot1 = P.op("dve", lambda eng: eng.tensor_copy(out=dot_sb[:, 512:1024], in_=dp[1]), deps=dot_mms[8:])
        ev_q2 = P.op("dve", lambda eng: eng.tensor_copy(
            out=q2_sb[:].rearrange("p (j n) -> p j n", j=2),
            in_=None), deps=q2_mm, tag="ev_q2")
        P.ops["dve"].remove(ev_q2); P.all_ops.remove(ev_q2)
        ev_q2a = P.op("dve", lambda eng: eng.tensor_copy(out=q2_sb[:, 0:512], in_=q2ps[0]), deps=q2_mm)
        ev_q2b = P.op("dve", lambda eng: eng.tensor_copy(out=q2_sb[:, 512:1024], in_=q2ps[1]), deps=q2_mm)

        vbanks = Banks(al, 2, name="vb")
        vT_sb = al.sb((128, DC * NTOK), BF16, "vT_sb")
        vT_ev = []
        for co in range(DC):
            for nch in range(4):
                bi, ps, war = vbanks.get()
                def mm_fn(eng, co=co, nch=nch, ps=ps):
                    inst = None
                    for c in range(DC):
                        inst = eng.matmul(ps[:, :],
                                          wv_sb[:, c * 1024 + co * 128: c * 1024 + (co + 1) * 128],
                                          xhatT[:, c * NTOK + nch * 512: c * NTOK + (nch + 1) * 512],
                                          start=(c == 0), stop=(c == DC - 1))
                    return inst
                mm = P.op("pe", mm_fn, deps=xT_ops + [wv_ld, war], tag=f"vT_mm{co}_{nch}")
                dst = vT_sb[:, co * NTOK + nch * 512: co * NTOK + (nch + 1) * 512]
                ev = P.op("act", lambda eng, dst=dst, ps=ps: eng.activation(out=dst, in_=ps[:, :], func=AF.Copy), deps=[mm], tag=f"vT_ev{co}_{nch}")
                vbanks.done(bi, ev)
                vT_ev.append(ev)

        st1 = P.dma("sp", lambda eng: eng.dma_start(out=qk_o.rearrange("(s p) d -> p s d", p=128), in_=qk_sb[:].rearrange("p (s d) -> p s d", s=NSTRIP)), deps=qk_ev, tag="st_qk")
        st2 = P.dma("sp", lambda eng: eng.dma_start(out=vT_o.rearrange("(c p) n -> p c n", p=128), in_=vT_sb[:].rearrange("p (c n) -> p c n", c=DC)), deps=vT_ev, tag="st_vT")
        st3 = P.dma("sp", lambda eng: eng.dma_start(out=dotp_o[:, :], in_=dot_sb[:]), deps=[ev_dot0, ev_dot1])
        st4 = P.dma("sp", lambda eng: eng.dma_start(out=q2p_o[:, :], in_=q2_sb[:]), deps=[ev_q2a, ev_q2b])
        fin = P.op("pool", lambda eng: eng.nop(), deps=[st1, st2, st3, st4], tag="fin")
        fin.need = False
        P.emit()
    return nc

def build_s0b():
    nc = bass.Bass("TRN2", target_bir_lowering=False)
    x = nc.dram_tensor("x", [NTOK, D], F32, kind="ExternalInput").ap()
    vT_i = nc.dram_tensor("vT_i", [D, NTOK], BF16, kind="ExternalInput").ap()
    dps = [nc.dram_tensor(f"dot{i}", [64, D], F32, kind="ExternalInput").ap() for i in range(4)]
    q2s_i = [nc.dram_tensor(f"q2_{i}", [1, D], F32, kind="ExternalInput").ap() for i in range(4)]
    tau = nc.dram_tensor("tau", [16, 1], F32, kind="ExternalInput").ap()
    wo = nc.dram_tensor("wo", [D, D], F32, kind="ExternalInput").ap()
    eyeb = nc.dram_tensor("eyeb", [128, 128], BF16, kind="ExternalInput").ap()
    eyef = nc.dram_tensor("eyef", [128, 128], F32, kind="ExternalInput").ap()
    onesr = nc.dram_tensor("onesr", [1, 128], BF16, kind="ExternalInput").ap()
    scr = nc.dram_tensor("scr", [16, 64], F32, kind="Internal").ap()
    x_o = nc.dram_tensor("x_o", [NTOK, D], F32, kind="ExternalOutput").ap()

    import contextlib
    with contextlib.ExitStack() as stack:
        al = Alloc(nc, stack)
        P = Prog(nc)
        sb = lambda shape, dt, nm: al.sb(shape, dt, nm)
        eyeb_sb = sb((128, 128), BF16, "eyeb_sb")
        eyef_sb = sb((128, 128), F32, "eyef_sb")
        onesr_sb = sb((1, 128), BF16, "onesr_sb")
        l_eb = P.dma("sp", lambda eng: eng.dma_start(out=eyeb_sb[:], in_=eyeb[:, :]))
        l_ef = P.dma("sp", lambda eng: eng.dma_start(out=eyef_sb[:], in_=eyef[:, :]))
        l_on = P.dma("sp", lambda eng: eng.dma_start(out=onesr_sb[:], in_=onesr[:, :]))
        x_sb = sb((128, NSTRIP * 1024), F32, "x_sb")
        x_ld = P.dma("sp", lambda eng: eng.dma_start(out=x_sb[:].rearrange("p (s d) -> p s d", s=NSTRIP), in_=x.rearrange("(s p) d -> p s d", p=128)), tag="x_ld")
        vT_sb = sb((128, DC * NTOK), BF16, "vT_sb")
        vT_ld = P.dma("sp", lambda eng: eng.dma_start(out=vT_sb[:].rearrange("p (c n) -> p c n", c=DC), in_=vT_i.rearrange("(c p) n -> p c n", p=128)), tag="vT_ld")
        wo_sb, wo_ld = load_w_bf16(P, al, wo, "wo")
        dot4 = sb((64, 4096), F32, "dot4")
        q24 = sb((1, 4096), F32, "q24")
        dl = [P.dma("sp", lambda eng, i=i: eng.dma_start(out=dot4[:, i * 1024:(i + 1) * 1024], in_=dps[i][:, :])) for i in range(4)]
        ql = [P.dma("sp", lambda eng, i=i: eng.dma_start(out=q24[0:1, i * 1024:(i + 1) * 1024], in_=q2s_i[i][:, :])) for i in range(4)]
        dots = sb((64, 1024), F32, "dots")
        q2r = sb((1, 1024), F32, "q2r_sb")
        a1 = P.op("dve", lambda eng: eng.tensor_add(dots[:], dot4[:, 0:1024], dot4[:, 1024:2048]), deps=dl[:2])
        a2 = P.op("dve", lambda eng: eng.tensor_add(dots[:], dots[:], dot4[:, 2048:3072]), deps=[a1, dl[2]])
        a3 = P.op("dve", lambda eng: eng.tensor_add(dots[:], dots[:], dot4[:, 3072:4096]), deps=[a2, dl[3]])
        b1 = P.op("dve", lambda eng: eng.tensor_add(q2r[:], q24[0:1, 0:1024], q24[0:1, 1024:2048]), deps=ql[:2])
        b2 = P.op("dve", lambda eng: eng.tensor_add(q2r[:], q2r[:], q24[0:1, 2048:3072]), deps=[b1, ql[2]])
        b3 = P.op("dve", lambda eng: eng.tensor_add(q2r[:], q2r[:], q24[0:1, 3072:4096]), deps=[b2, ql[3]])
        gm = sb((1, 2), F32, "gm")
        c1 = P.op("dve", lambda eng: eng.tensor_reduce(out=gm[:, 0:1], in_=q2r[:], axis=AX.X, op=ALU.add), deps=[b3])
        c2 = P.op("dve", lambda eng: eng.tensor_scalar_mul(gm[:, 1:2], gm[:, 0:1], -1.0 / 1024.0), deps=[c1])
        c3 = P.op("dve", lambda eng: eng.tensor_scalar(out=q2r[:], in0=q2r[:], scalar1=gm[:, 1:2], scalar2=None, op0=ALU.add), deps=[c2], tag="q2shift")
        q2rb = sb((1, 1024), BF16, "q2rb")
        c4 = P.op("act", lambda eng: eng.activation(out=q2rb[:], in_=q2r[:], func=AF.Copy), deps=[c3])
        d1 = P.dma("sp", lambda eng: eng.dma_start(out=scr.rearrange("a b -> (a b)").rearrange("(a d) -> a d", a=1), in_=q2r[:]), deps=[c3], tag="q2bounce_o")
        q2m = sb((16, 64), F32, "q2m")
        d2 = P.dma("sp", lambda eng: eng.dma_start(out=q2m[:], in_=scr[:, :]), deps=[d1], tag="q2bounce_i")
        pst = al.ps((128, 512), F32, "pst")
        d3 = P.op("pe", lambda eng: eng.transpose(pst[0:64, 0:16], q2m[:], eyef_sb[0:16, 0:16]), deps=[d2, l_ef], tag="q2t_tp")
        q2t = sb((64, 16), F32, "q2t")
        d4 = P.op("dve", lambda eng: eng.tensor_copy(out=q2t[:], in_=pst[0:64, 0:16]), deps=[d3])
        taur = sb((1, 16), F32, "taur")
        taurb = sb((1, 16), BF16, "taurb")
        t1 = P.dma("sp", lambda eng: eng.dma_start(out=taur[:], in_=tau.rearrange("h a -> a h")), tag="tau_ld")
        t2 = P.op("dve", lambda eng: eng.tensor_scalar_mul(taur[:], taur[:], 1.0 / math.sqrt(8192.0)), deps=[t1])
        t3 = P.op("act", lambda eng: eng.activation(out=taurb[:], in_=taur[:], func=AF.Copy), deps=[t2])
        pstb = pst
        t4 = P.op("pe", lambda eng: eng.matmul(pstb[0:64, 16:32], onesr_sb[0:1, 0:64], taurb[:], start=True, stop=True), deps=[t3, l_on, d4], tag="taub_mm")
        taub = sb((64, 16), F32, "taub")
        t5 = P.op("dve", lambda eng: eng.tensor_copy(out=taub[:], in_=pstb[0:64, 16:32]), deps=[t4])
        q2rp = [al.ps((128, 512), F32, f"q2rp{j}") for j in range(2)]
        e1 = [P.op("pe", lambda eng, j=j: eng.matmul(q2rp[j][0:64, :], onesr_sb[0:1, 0:64], q2rb[0:1, j * 512:(j + 1) * 512], start=True, stop=True), deps=[c4, l_on], tag=f"q2rp{j}") for j in range(2)]
        lt = sb((64, 1024), F32, "lt")
        f1 = [P.op("dve", lambda eng, j=j: eng.scalar_tensor_tensor(out=lt[:, j * 512:(j + 1) * 512], in0=dots[:, j * 512:(j + 1) * 512], scalar=2.0, in1=q2rp[j][0:64, :], op0=ALU.mult, op1=ALU.subtract), deps=[a3, e1[j]], tag=f"lt{j}") for j in range(2)]
        lt3 = lt[:].rearrange("p (h e) -> p h e", h=16)
        f2 = P.op("dve", lambda eng: eng.tensor_tensor(out=lt3, in0=lt3, in1=q2t[:].unsqueeze(2).broadcast_to([64, 16, 64]), op=ALU.subtract), deps=f1 + [d4], tag="lt_sub")
        f3 = P.op("dve", lambda eng: eng.tensor_tensor(out=lt3, in0=lt3, in1=taub[:].unsqueeze(2).broadcast_to([64, 16, 64]), op=ALU.mult), deps=[f2, t5], tag="lt_tau")
        mx = sb((64, 16), F32, "mx")
        sm1 = P.op("dve", lambda eng: eng.tensor_reduce(out=mx[:], in_=lt3, axis=AX.X, op=ALU.max), deps=[f3])
        sm2 = P.op("dve", lambda eng: eng.tensor_tensor(out=lt3, in0=lt3, in1=mx[:].unsqueeze(2).broadcast_to([64, 16, 64]), op=ALU.subtract), deps=[sm1])
        sm3 = P.op("act", lambda eng: eng.activation(out=lt[:], in_=lt[:], func=AF.Exp), deps=[sm2], tag="exp")
        sms = sb((64, 16), F32, "sms")
        sm4 = P.op("dve", lambda eng: eng.tensor_reduce(out=sms[:], in_=lt3, axis=AX.X, op=ALU.add), deps=[sm3])
        sm5 = P.op("dve", lambda eng: eng.reciprocal(out=sms[:], in_=sms[:]), deps=[sm4])
        Pb = sb((64, 1024), BF16, "Pb")
        sm6 = P.op("dve", lambda eng: eng.tensor_tensor(out=Pb[:].rearrange("p (h e) -> p h e", h=16), in0=lt3, in1=sms[:].unsqueeze(2).broadcast_to([64, 16, 64]), op=ALU.mult), deps=[sm5], tag="probs")
        psa = q2rp[0]
        psa_bf = psa.ap().bitcast(BF16)
        def atp_fn(eng):
            inst = None
            for h in range(16):
                r0 = (h % 2) * 64
                c0 = (h // 2) * 64
                inst = eng.transpose(psa_bf[r0:r0 + 64, c0:c0 + 64], Pb[0:64, h * 64:(h + 1) * 64], eyeb_sb[0:64, 0:64])
            return inst
        g1 = P.op("pe", atp_fn, deps=[sm6, l_eb, f1[0]], tag="aT_tp")
        aT = sb((128, 512), BF16, "aT")
        g2 = P.op("dve", lambda eng: eng.tensor_copy(out=aT[:], in_=psa_bf[:, 0:512]), deps=[g1])
        wT = sb((128, DC * NTOK), BF16, "wT")
        wbanks = Banks(al, 2, name="wb")
        w_ev = []
        for h in range(16):
            for nch in range(4):
                bi, ps, war = wbanks.get()
                def mm_fn(eng, h=h, nch=nch, ps=ps):
                    r0 = (h % 2) * 64
                    c0 = (h // 2) * 64
                    return eng.matmul(ps[r0:r0 + 64, :], aT[r0:r0 + 64, c0:c0 + 64],
                                      vT_sb[(h % 2) * 64:(h % 2) * 64 + 64, (h // 2) * NTOK + nch * 512:(h // 2) * NTOK + (nch + 1) * 512],
                                      start=True, stop=True)
                mm = P.op("pe", mm_fn, deps=[g2, vT_ld, war], tag=f"w{h}_{nch}")
                dst = wT[(h % 2) * 64:(h % 2) * 64 + 64, (h // 2) * NTOK + nch * 512:(h // 2) * NTOK + (nch + 1) * 512]
                ev = P.op("act", lambda eng, dst=dst, ps=ps, h=h: eng.activation(out=dst, in_=ps[(h % 2) * 64:(h % 2) * 64 + 64, :], func=AF.Copy), deps=[mm], tag=f"wev{h}_{nch}")
                wbanks.done(bi, ev)
                w_ev.append(ev)
        obanks = Banks(al, 2, name="ob")
        out_ops = []
        for s in range(NSTRIP):
            for nch in range(2):
                bi, ps, war = obanks.get()
                def mm_fn(eng, s=s, nch=nch, ps=ps):
                    inst = None
                    for c in range(DC):
                        inst = eng.matmul(ps[:, :], wT[:, c * NTOK + s * 128:c * NTOK + (s + 1) * 128],
                                          wo_sb[:, c * 1024 + nch * 512:c * 1024 + (nch + 1) * 512],
                                          start=(c == 0), stop=(c == DC - 1))
                    return inst
                mm = P.op("pe", mm_fn, deps=w_ev + [wo_ld, war], tag=f"o{s}_{nch}")
                dst = x_sb[:, s * 1024 + nch * 512:s * 1024 + (nch + 1) * 512]
                ev = P.op("dve", lambda eng, dst=dst, ps=ps: eng.tensor_tensor(out=dst, in0=ps[:, :], in1=dst, op=ALU.add), deps=[mm, x_ld], tag=f"oev{s}_{nch}")
                obanks.done(bi, ev)
                out_ops.append(ev)
        st = P.dma("sp", lambda eng: eng.dma_start(out=x_o.rearrange("(s p) d -> p s d", p=128), in_=x_sb[:].rearrange("p (s d) -> p s d", s=NSTRIP)), deps=out_ops, tag="st_x")
        fin = P.op("pool", lambda eng: eng.nop(), deps=[st], tag="fin")
        fin.need = False
        P.emit()
    return nc


def build_s5():
    nc = bass.Bass("TRN2", target_bir_lowering=False)
    x = nc.dram_tensor("x", [NTOK, D], F32, kind="ExternalInput").ap()
    gate_wT = nc.dram_tensor("gate_wT", [D, 8], F32, kind="ExternalInput").ap()
    gate_b = nc.dram_tensor("gate_b", [1, 8], F32, kind="ExternalInput").ap()
    w1T = nc.dram_tensor("w1T", [8, D, D], F32, kind="ExternalInput").ap()
    w3T = nc.dram_tensor("w3T", [8, D, D], F32, kind="ExternalInput").ap()
    w2T = nc.dram_tensor("w2T", [8, D, D], F32, kind="ExternalInput").ap()
    b1a = nc.dram_tensor("b1a", [8, 128, 8], F32, kind="ExternalInput").ap()
    b3a = nc.dram_tensor("b3a", [8, 128, 8], F32, kind="ExternalInput").ap()
    b2a = nc.dram_tensor("b2a", [8, D], F32, kind="ExternalInput").ap()
    ws1T = nc.dram_tensor("ws1T", [D, 2048], F32, kind="ExternalInput").ap()
    ws3T = nc.dram_tensor("ws3T", [D, 2048], F32, kind="ExternalInput").ap()
    ws2T = nc.dram_tensor("ws2T", [2048, D], F32, kind="ExternalInput").ap()
    bs1 = nc.dram_tensor("bs1", [128, 16], F32, kind="ExternalInput").ap()
    bs3 = nc.dram_tensor("bs3", [128, 16], F32, kind="ExternalInput").ap()
    bs2 = nc.dram_tensor("bs2", [1, D], F32, kind="ExternalInput").ap()
    eyeb = nc.dram_tensor("eyeb", [128, 128], BF16, kind="ExternalInput").ap()
    onesr = nc.dram_tensor("onesr", [1, 128], BF16, kind="ExternalInput").ap()
    x_o = nc.dram_tensor("x_o", [NTOK, D], F32, kind="ExternalOutput").ap()

    import contextlib
    with contextlib.ExitStack() as stack:
        al = Alloc(nc, stack)
        P = Prog(nc)
        emit_moe(P, al, dict(x=x, gate_wT=gate_wT, gate_b=gate_b, w1T=w1T, w3T=w3T,
                             w2T=w2T, b1a=b1a, b3a=b3a, b2a=b2a, ws1T=ws1T, ws3T=ws3T,
                             ws2T=ws2T, bs1=bs1, bs3=bs3, bs2=bs2, eyeb=eyeb,
                             onesr=onesr, x_o=x_o))
        P.emit()
    return nc


def emit_moe(P, al, io):
    nc = P.nc
    sbf = al.sb
    eyeb_sb = sbf((128, 128), BF16, "m_eyeb")
    onesr_sb = sbf((1, 128), BF16, "m_onesr")
    l_eb = P.dma("sp", lambda eng: eng.dma_start(out=eyeb_sb[:], in_=io["eyeb"][:, :]))
    l_on = P.dma("sp", lambda eng: eng.dma_start(out=onesr_sb[:], in_=io["onesr"][:, :]))
    xhatT = sbf((128, DC * NTOK), BF16, "m_xhatT")
    ln_ps = al.ps((128, 512), F32, "m_lnps")
    x_sb, x_loads, xT_ops = emit_ln_xhatT(P, al, io["x"], xhatT, eyeb_sb, ln_ps, tag="m_ln", keep_x=True)
    for o in P.ops["pe"]:
        o.deps.append(l_eb)
    gwt_sb = sbf((128, DC * 8), BF16, "m_gwt")
    gb_sb = sbf((1, 8), BF16, "m_gb")
    l_gw = P.dma("pool", lambda eng: eng.dma_start(out=gwt_sb[:].rearrange("p (c e) -> p c e", c=DC), in_=io["gate_wT"].rearrange("(c p) e -> p c e", p=128)))
    l_gb = P.dma("pool", lambda eng: eng.dma_start(out=gb_sb[:], in_=io["gate_b"][:, :]))
    gd_sb = sbf((128, 8 * NSTRIP), F32, "m_gd")
    gdT_sb = sbf((8, NTOK), BF16, "m_gdT")
    gsc = sbf((128, 8 * NSTRIP), F32, "m_gsc")
    mxc = sbf((128, 4), F32, "m_gmx")
    m8t = sbf((128, 8), F32, "m_m8")
    gdb = sbf((128, 8), BF16, "m_gdb")
    gps = al.ps((128, 512), F32, "m_gps")
    gps_bf = gps.ap().bitcast(BF16)
    gd_ops = []
    gdT_ops = []
    prev_gps_user = None
    for s in range(NSTRIP):
        def g_mm(eng, s=s):
            for c in range(DC):
                eng.matmul(gps[:, 0:8], xhatT[:, c * NTOK + s * 128:c * NTOK + (s + 1) * 128],
                           gwt_sb[:, c * 8:(c + 1) * 8], start=(c == 0), stop=False)
            return eng.matmul(gps[:, 0:8], onesr_sb[0:1, :], gb_sb[0:1, :], start=False, stop=True)
        mm = P.op("pe", g_mm, deps=[xT_ops[s], l_gw, l_gb, l_on, prev_gps_user], tag=f"m_gmm{s}")
        lg = gsc[:, s * 8:(s + 1) * 8]
        e1 = P.op("dve", lambda eng, lg=lg: eng.tensor_copy(out=lg, in_=gps[:, 0:8]), deps=[mm], tag=f"m_gev{s}")
        prior = gd_ops[-1] if gd_ops else None
        o1 = P.op("dve", lambda eng, lg=lg: eng.tensor_reduce(out=mxc[:, 0:1], in_=lg, axis=AX.X, op=ALU.max, negate=True), deps=[e1, prior], tag=f"m_gmax{s}")
        o2 = P.op("act", lambda eng, lg=lg: eng.activation(out=lg, in_=lg, func=AF.Exp, bias=mxc[:, 0:1], accum_out=mxc[:, 1:2]), deps=[o1], tag=f"m_gexp{s}")
        o3 = P.op("dve", lambda eng: eng.reciprocal(out=mxc[:, 2:3], in_=mxc[:, 1:2]), deps=[o2])
        o4 = P.op("dve", lambda eng, lg=lg: eng.max(out=m8t[:], in_=lg), deps=[o2, prior], tag=f"m_gm8{s}")
        gd = gd_sb[:, s * 8:(s + 1) * 8]
        o5 = P.op("dve", lambda eng, lg=lg, gd=gd: eng.tensor_scalar(out=gd, in0=lg, scalar1=m8t[:, 1:2], scalar2=None, op0=ALU.is_ge), deps=[o4], tag=f"m_gmask{s}")
        o6 = P.op("dve", lambda eng, lg=lg, gd=gd: eng.tensor_tensor(out=gd, in0=gd, in1=lg, op=ALU.mult), deps=[o5])
        o7 = P.op("dve", lambda eng, gd=gd: eng.tensor_scalar(out=gd, in0=gd, scalar1=mxc[:, 2:3], scalar2=None, op0=ALU.mult), deps=[o6, o3], tag=f"m_gd{s}")
        gd_ops.append(o7)
        o8 = P.op("act", lambda eng, gd=gd: eng.activation(out=gdb[:], in_=gd, func=AF.Copy), deps=[o7] + ([gdT_ops[-1]] if gdT_ops else []), tag=f"m_gdb{s}")
        o9 = P.op("pe", lambda eng: eng.transpose(gps_bf[0:8, 0:128], gdb[:], eyeb_sb[:]), deps=[o8, e1], tag=f"m_gdtp{s}")
        o10 = P.op("dve", lambda eng, s=s: eng.tensor_copy(out=gdT_sb[0:8, s * 128:(s + 1) * 128], in_=gps_bf[0:8, 0:128]), deps=[o9], tag=f"m_gdTev{s}")
        gdT_ops.append(o10)
        prev_gps_user = o10
    WA = sbf((128, 16384), BF16, "m_WA")
    WB = sbf((128, 8192), BF16, "m_WB")
    HG = sbf((128, 8192), BF16, "m_HG")
    bsc = sbf((128, 32), F32, "m_bsc")
    b2r = sbf((8, 1024), BF16, "m_b2r")
    bs2r = sbf((1, 1024), BF16, "m_bs2r")
    sa = sbf((128, 512), F32, "m_sa")
    l_b2 = P.dma("pool", lambda eng: eng.dma_start(out=b2r[:], in_=io["b2a"][:, :]))
    l_bs2 = P.dma("pool", lambda eng: eng.dma_start(out=bs2r[:], in_=io["bs2"][:, :]))
    pbank = Banks(al, 4, name="m_pb")
    state = {"sa_war": None, "hg_war": []}

    def hg_half(wa_ap, wb_ap, tok0, deps):
        ops = []
        for iio in range(8):
            for tch in range(2):
                t0 = tok0 + tch * 512
                bi1, ps1, war1 = pbank.get()
                def mm1(eng, iio=iio, t0=t0, ps=ps1, wa_ap=wa_ap):
                    inst = None
                    for c in range(DC):
                        inst = eng.matmul(ps[:, :], wa_ap[:, c * 1024 + iio * 128: c * 1024 + (iio + 1) * 128],
                                          xhatT[:, c * NTOK + t0:c * NTOK + t0 + 512],
                                          start=(c == 0), stop=(c == DC - 1))
                    return inst
                m1 = P.op("pe", mm1, deps=deps + [war1], tag=f"m_h1_{iio}_{t0}")
                bi2, ps2, war2 = pbank.get()
                def mm2(eng, iio=iio, t0=t0, ps=ps2, wb_ap=wb_ap):
                    inst = None
                    for c in range(DC):
                        inst = eng.matmul(ps[:, :], wb_ap[:, c * 1024 + iio * 128: c * 1024 + (iio + 1) * 128],
                                          xhatT[:, c * NTOK + t0:c * NTOK + t0 + 512],
                                          start=(c == 0), stop=(c == DC - 1))
                    return inst
                m2 = P.op("pe", mm2, deps=deps + [war2], tag=f"m_h3_{iio}_{t0}")
                ba = bsc[:, iio:iio + 1]
                bb = bsc[:, 8 + iio:8 + iio + 1]
                e1 = P.op("act", lambda eng, ps=ps1, ba=ba: eng.activation(out=sa[:], in_=ps[:, :], func=AF.Silu, bias=ba), deps=[m1, state["sa_war"]], tag=f"m_he1_{iio}_{t0}")
                dst = HG[:, iio * 1024 + (t0 - tok0): iio * 1024 + (t0 - tok0) + 512]
                e2 = P.op("dve", lambda eng, ps=ps2, dst=dst, bb=bb: eng.scalar_tensor_tensor(out=dst, in0=ps[:, :], scalar=bb, in1=sa[:], op0=ALU.add, op1=ALU.mult), deps=[m2, e1] + state["hg_war"], tag=f"m_he2_{iio}_{t0}")
                state["sa_war"] = e2
                pbank.done(bi1, e1)
                pbank.done(bi2, e2)
                ops.append(e2)
        return ops

    def z_half(th, add_bias_rows, gate_col, extra_deps, hg_ops, w2_ap):
        evs = []
        for st in range(8):
            s = th * 8 + st
            for nch in range(2):
                bi, ps, war = pbank.get()
                def mmz(eng, s=s, st=st, nch=nch, ps=ps, add_bias_rows=add_bias_rows, w2_ap=w2_ap):
                    inst = None
                    for iio in range(8):
                        inst = eng.matmul(ps[:, :], HG[:, iio * 1024 + st * 128:iio * 1024 + (st + 1) * 128],
                                          w2_ap[:, iio * 1024 + nch * 512:iio * 1024 + (nch + 1) * 512],
                                          start=(iio == 0), stop=(not add_bias_rows and iio == 7))
                    if add_bias_rows:
                        eng.matmul(ps[:, :], onesr_sb[0:1, :], bs2r[0:1, nch * 512:(nch + 1) * 512], start=False, stop=False)
                        inst = eng.matmul(ps[:, :], gdT_sb[0:8, s * 128:(s + 1) * 128], b2r[0:8, nch * 512:(nch + 1) * 512], start=False, stop=True)
                    return inst
                mm = P.op("pe", mmz, deps=hg_ops + extra_deps + [war], tag=f"m_z{s}_{nch}")
                dst = x_sb[:, s * 1024 + nch * 512:s * 1024 + (nch + 1) * 512]
                if gate_col is None:
                    ev = P.op("dve", lambda eng, dst=dst, ps=ps: eng.tensor_tensor(out=dst, in0=ps[:, :], in1=dst, op=ALU.add), deps=[mm, x_loads[s]], tag=f"m_zev{s}_{nch}")
                else:
                    gcol = gate_col(s)
                    ev = P.op("dve", lambda eng, dst=dst, ps=ps, gcol=gcol: eng.scalar_tensor_tensor(out=dst, in0=ps[:, :], scalar=gcol, in1=dst, op0=ALU.mult, op1=ALU.add), deps=[mm, x_loads[s], gd_ops[s]], tag=f"m_zev{s}_{nch}")
                pbank.done(bi, ev)
                evs.append(ev)
        return evs

    all_evs = []
    prev_loads_user = []
    for p in range(2):
        lw1 = P.dma("pool", lambda eng, p=p: eng.dma_start(out=WA[:, 0:8192].rearrange("p (c o) -> p c o", c=DC), in_=io["ws1T"].rearrange("(c p) o -> p c o", p=128)[:, :, p * 1024:(p + 1) * 1024]), deps=prev_loads_user, tag=f"m_lws1_{p}")
        lw3 = P.dma("pool", lambda eng, p=p: eng.dma_start(out=WB[:, 0:8192].rearrange("p (c o) -> p c o", c=DC), in_=io["ws3T"].rearrange("(c p) o -> p c o", p=128)[:, :, p * 1024:(p + 1) * 1024]), deps=prev_loads_user, tag=f"m_lws3_{p}")
        lw2 = P.dma("pool", lambda eng, p=p: eng.dma_start(out=WA[:, 8192:16384].rearrange("p (c o) -> p c o", c=DC), in_=io["ws2T"].rearrange("(i p) o -> p i o", p=128)[:, p * 8:(p + 1) * 8, :]), deps=prev_loads_user, tag=f"m_lws2_{p}")
        lb1 = P.dma("sp", lambda eng, p=p: eng.dma_start(out=bsc[:, 0:8], in_=io["bs1"][:, p * 8:(p + 1) * 8]), deps=prev_loads_user, tag=f"m_lbs1_{p}")
        lb3 = P.dma("sp", lambda eng, p=p: eng.dma_start(out=bsc[:, 8:16], in_=io["bs3"][:, p * 8:(p + 1) * 8]), deps=prev_loads_user, tag=f"m_lbs3_{p}")
        phase_evs = []
        for th in range(2):
            hg_ops = hg_half(WA[:, 0:8192], WB[:, 0:8192], th * 1024, [lw1, lw3, lb1, lb3] + xT_ops)
            state["hg_war"] = []
            evs = z_half(th, add_bias_rows=(p == 0), gate_col=None,
                         extra_deps=[lw2, l_b2, l_bs2, l_on] + ([gdT_ops[th * 8 + st] for st in range(8)] if p == 0 else []),
                         hg_ops=hg_ops, w2_ap=WA[:, 8192:16384])
            state["hg_war"] = evs
            phase_evs += evs
        prev_loads_user = phase_evs + [state["sa_war"]]
        all_evs += phase_evs
    for e in range(8):
        lw1 = P.dma("pool", lambda eng, e=e: eng.dma_start(out=WA[:, 0:8192].rearrange("p (c o) -> p c o", c=DC), in_=io["w1T"][e].rearrange("(c p) o -> p c o", p=128)), deps=prev_loads_user, tag=f"m_lw1_{e}")
        lw3 = P.dma("pool", lambda eng, e=e: eng.dma_start(out=WB[:, 0:8192].rearrange("p (c o) -> p c o", c=DC), in_=io["w3T"][e].rearrange("(c p) o -> p c o", p=128)), deps=prev_loads_user, tag=f"m_lw3_{e}")
        lw2 = P.dma("pool", lambda eng, e=e: eng.dma_start(out=WA[:, 8192:16384].rearrange("p (c o) -> p c o", c=DC), in_=io["w2T"][e].rearrange("(c p) o -> p c o", p=128)), deps=prev_loads_user, tag=f"m_lw2_{e}")
        lb1 = P.dma("sp", lambda eng, e=e: eng.dma_start(out=bsc[:, 0:8], in_=io["b1a"][e]), deps=prev_loads_user, tag=f"m_lb1_{e}")
        lb3 = P.dma("sp", lambda eng, e=e: eng.dma_start(out=bsc[:, 8:16], in_=io["b3a"][e]), deps=prev_loads_user, tag=f"m_lb3_{e}")
        phase_evs = []
        for th in range(2):
            hg_ops = hg_half(WA[:, 0:8192], WB[:, 0:8192], th * 1024, [lw1, lw3, lb1, lb3] + xT_ops)
            state["hg_war"] = []
            evs = z_half(th, add_bias_rows=False,
                         gate_col=lambda s, e=e: gd_sb[:, s * 8 + e:s * 8 + e + 1],
                         extra_deps=[lw2], hg_ops=hg_ops, w2_ap=WA[:, 8192:16384])
            state["hg_war"] = evs
            phase_evs += evs
        prev_loads_user = phase_evs + [state["sa_war"]]
        all_evs = phase_evs
    st = P.dma("sp", lambda eng: eng.dma_start(out=io["x_o"].rearrange("(s p) d -> p s d", p=128), in_=x_sb[:].rearrange("p (s d) -> p s d", s=NSTRIP)), deps=all_evs, tag="m_st")
    fin = P.op("pool", lambda eng: eng.nop(), deps=[st], tag="fin")
    fin.need = False


def build_s1a():
    nc = bass.Bass("TRN2", target_bir_lowering=False)
    x = nc.dram_tensor("x", [NTOK, D], F32, kind="ExternalInput").ap()
    node_wT = {}
    node_b = {}
    w1a = {}
    b1c = {}
    w2a = {}
    b2l = {}
    for t in ("q", "k", "v"):
        node_wT[t] = nc.dram_tensor(f"{t}_nwT", [D, 15], F32, kind="ExternalInput").ap()
        node_b[t] = nc.dram_tensor(f"{t}_nb", [1, 15], F32, kind="ExternalInput").ap()
        w1a[t] = nc.dram_tensor(f"{t}_w1a", [D, D], F32, kind="ExternalInput").ap()
        b1c[t] = nc.dram_tensor(f"{t}_b1c", [128, 8], F32, kind="ExternalInput").ap()
        w2a[t] = nc.dram_tensor(f"{t}_w2a", [D, D], F32, kind="ExternalInput").ap()
        b2l[t] = nc.dram_tensor(f"{t}_b2l", [16, D], F32, kind="ExternalInput").ap()
    eyeb = nc.dram_tensor("eyeb", [128, 128], BF16, kind="ExternalInput").ap()
    onesr = nc.dram_tensor("onesr", [1, 128], BF16, kind="ExternalInput").ap()
    iota16 = nc.dram_tensor("iota16", [128, 16], F32, kind="ExternalInput").ap()
    expm = nc.dram_tensor("expm", [16, 1024], BF16, kind="ExternalInput").ap()
    qT_o = nc.dram_tensor("qT_o", [D, NTOK], BF16, kind="ExternalOutput").ap()
    kT_o = nc.dram_tensor("kT_o", [D, NTOK], BF16, kind="ExternalOutput").ap()
    v_o = nc.dram_tensor("v_o", [NTOK, D], BF16, kind="ExternalOutput").ap()

    import contextlib
    with contextlib.ExitStack() as stack:
        al = Alloc(nc, stack)
        P = Prog(nc)
        sbf = al.sb
        eyeb_sb = sbf((128, 128), BF16, "f_eyeb")
        onesr_sb = sbf((1, 128), BF16, "f_onesr")
        iota_sb = sbf((128, 16), F32, "f_iota")
        expm_sb = sbf((16, 1024), BF16, "f_expm")
        l_eb = P.dma("sp", lambda eng: eng.dma_start(out=eyeb_sb[:], in_=eyeb[:, :]))
        l_on = P.dma("sp", lambda eng: eng.dma_start(out=onesr_sb[:], in_=onesr[:, :]))
        l_io = P.dma("sp", lambda eng: eng.dma_start(out=iota_sb[:], in_=iota16[:, :]))
        l_ex = P.dma("sp", lambda eng: eng.dma_start(out=expm_sb[:], in_=expm[:, :]))
        xhatT = sbf((128, DC * NTOK), BF16, "f_xhatT")
        ln_ps = al.ps((128, 512), F32, "f_lnps")
        x_sb, x_loads, xT_ops = emit_ln_xhatT(P, al, x, xhatT, eyeb_sb, ln_ps, tag="f_ln", keep_x=False)
        for o in P.ops["pe"]:
            o.deps.append(l_eb)
        W1 = sbf((128, 8192), BF16, "f_W1")
        W2 = sbf((128, 8192), BF16, "f_W2")
        NWT = sbf((128, DC * 16), BF16, "f_nwt")
        NB = sbf((1, 16), BF16, "f_nb")
        B1C = sbf((128, 8), F32, "f_b1c")
        B2L = sbf((16, 1024), BF16, "f_b2l")
        HM = sbf((128, 16384), BF16, "f_hm")
        OHT = sbf((16, NTOK), BF16, "f_ohT")
        TW = sbf((128, 64 * 2), F32, "f_tw")
        TWI = sbf((128, 8 * 2), I32, "f_twi")
        SS = sbf((128, 16 * 2), F32, "f_ss")
        OST = [sbf((128, 2048), BF16, f"f_ost{i}") for i in range(2)]
        OHB = sbf((128, 16), BF16, "f_ohbt")
        SA1 = sbf((128, 512), F32, "f_sa1")
        pbank = Banks(al, 4, name="f_pb")
        tbank = al.ps((128, 512), F32, "f_tp")
        tbank_bf = tbank.ap().bitcast(BF16)

        prev_tree_ops = []
        store_ops = []
        for t, out_kind in (("q", "fT"), ("k", "fT"), ("v", "tm")):
            lw1 = P.dma("pool", lambda eng, t=t: eng.dma_start(out=W1[:].rearrange("p (c o) -> p c o", c=DC), in_=w1a[t].rearrange("(c p) o -> p c o", p=128)), deps=prev_tree_ops, tag=f"f_lw1{t}")
            lw2 = P.dma("pool", lambda eng, t=t: eng.dma_start(out=W2[:].rearrange("p (c o) -> p c o", c=DC), in_=w2a[t].rearrange("(c p) o -> p c o", p=128)), deps=prev_tree_ops, tag=f"f_lw2{t}")
            lnw = P.dma("pool", lambda eng, t=t: eng.dma_start(out=NWT[:].rearrange("p (c e) -> p c e", c=DC)[:, :, 0:15], in_=node_wT[t].rearrange("(c p) e -> p c e", p=128)), deps=prev_tree_ops, tag=f"f_lnw{t}")
            lnb = P.dma("pool", lambda eng, t=t: eng.dma_start(out=NB[0:1, 0:15], in_=node_b[t][:, :]), deps=prev_tree_ops, tag=f"f_lnb{t}")
            lb1 = P.dma("sp", lambda eng, t=t: eng.dma_start(out=B1C[:], in_=b1c[t][:, :]), deps=prev_tree_ops, tag=f"f_lb1{t}")
            lb2 = P.dma("pool", lambda eng, t=t: eng.dma_start(out=B2L[:], in_=b2l[t][:, :]), deps=prev_tree_ops, tag=f"f_lb2{t}")
            oh_ops = []
            prev_ss = [None, None]
            for s in range(NSTRIP):
                par = s % 2
                ssl = SS[:, par * 16:par * 16 + 16]
                bi, ps, war = pbank.get()
                def nmm(eng, s=s, ps=ps):
                    for c in range(DC):
                        eng.matmul(ps[:, 0:16], xhatT[:, c * NTOK + s * 128:c * NTOK + (s + 1) * 128],
                                   NWT[:, c * 16:(c + 1) * 16], start=(c == 0), stop=False)
                    return eng.matmul(ps[:, 0:16], onesr_sb[0:1, :], NB[0:1, :], start=False, stop=True)
                mm = P.op("pe", nmm, deps=[xT_ops[s], lnw, lnb, l_on, war], tag=f"f_nmm{t}{s}")
                ev = P.op("dve", lambda eng, ssl=ssl, ps=ps: eng.tensor_copy(out=ssl, in_=ps[:, 0:16]), deps=[mm, prev_ss[par]], tag=f"f_nev{t}{s}")
                pbank.done(bi, ev)
                tw = TW[:, par * 64:par * 64 + 64]
                twi = TWI[:, par * 8:par * 8 + 8]
                def col(j):
                    return tw[:, j:j + 1]
                def icol(j):
                    return twi[:, j:j + 1]
                o = ev
                def emit_dve(fn, deps, tg):
                    return P.op("dve", fn, deps=deps, tag=tg)
                b0 = icol(0)
                o1 = emit_dve(lambda eng, ssl=ssl, b0=b0: eng.tensor_scalar(out=b0, in0=ssl[:, 0:1], scalar1=0.0, scalar2=None, op0=ALU.is_ge), [ev, prev_ss[par]], f"f_b0{t}{s}")
                s1c = col(1)
                o2 = emit_dve(lambda eng, ssl=ssl, b0=b0, s1c=s1c: eng.select(out=s1c, mask=b0, on_true=ssl[:, 2:3], on_false=ssl[:, 1:2]), [o1], f"f_s1{t}{s}")
                b1 = icol(1)
                o3 = emit_dve(lambda eng, s1c=s1c, b1=b1: eng.tensor_scalar(out=b1, in0=s1c, scalar1=0.0, scalar2=None, op0=ALU.is_ge), [o2], f"f_b1{t}{s}")
                u0, u1, s2c = col(3), col(4), col(5)
                o4 = emit_dve(lambda eng, ssl=ssl, b1=b1, u0=u0: eng.select(out=u0, mask=b1, on_true=ssl[:, 4:5], on_false=ssl[:, 3:4]), [o3], f"f_u0{t}{s}")
                o5 = emit_dve(lambda eng, ssl=ssl, b1=b1, u1=u1: eng.select(out=u1, mask=b1, on_true=ssl[:, 6:7], on_false=ssl[:, 5:6]), [o4], f"f_u1{t}{s}")
                o6 = emit_dve(lambda eng, b0=b0, u0=u0, u1=u1, s2c=s2c: eng.select(out=s2c, mask=b0, on_true=u1, on_false=u0), [o5], f"f_s2{t}{s}")
                b2 = icol(2)
                o7 = emit_dve(lambda eng, s2c=s2c, b2=b2: eng.tensor_scalar(out=b2, in0=s2c, scalar1=0.0, scalar2=None, op0=ALU.is_ge), [o6], f"f_b2{t}{s}")
                v0, v1, v2, v3 = col(7), col(8), col(9), col(10)
                w0, w1_, s3c = col(11), col(12), col(13)
                o8 = emit_dve(lambda eng, ssl=ssl, b2=b2, v0=v0: eng.select(out=v0, mask=b2, on_true=ssl[:, 8:9], on_false=ssl[:, 7:8]), [o7], f"f_v0{t}{s}")
                o9 = emit_dve(lambda eng, ssl=ssl, b2=b2, v1=v1: eng.select(out=v1, mask=b2, on_true=ssl[:, 10:11], on_false=ssl[:, 9:10]), [o8], f"f_v1{t}{s}")
                o10 = emit_dve(lambda eng, ssl=ssl, b2=b2, v2=v2: eng.select(out=v2, mask=b2, on_true=ssl[:, 12:13], on_false=ssl[:, 11:12]), [o9], f"f_v2{t}{s}")
                o11 = emit_dve(lambda eng, ssl=ssl, b2=b2, v3=v3: eng.select(out=v3, mask=b2, on_true=ssl[:, 14:15], on_false=ssl[:, 13:14]), [o10], f"f_v3{t}{s}")
                o12 = emit_dve(lambda eng, b1=b1, v0=v0, v1=v1, w0=w0: eng.select(out=w0, mask=b1, on_true=v1, on_false=v0), [o11], f"f_w0{t}{s}")
                o13 = emit_dve(lambda eng, b1=b1, v2=v2, v3=v3, w1_=w1_: eng.select(out=w1_, mask=b1, on_true=v3, on_false=v2), [o12], f"f_w1{t}{s}")
                o14 = emit_dve(lambda eng, b0=b0, w0=w0, w1_=w1_, s3c=s3c: eng.select(out=s3c, mask=b0, on_true=w1_, on_false=w0), [o13], f"f_s3{t}{s}")
                b3 = icol(3)
                o15 = emit_dve(lambda eng, s3c=s3c, b3=b3: eng.tensor_scalar(out=b3, in0=s3c, scalar1=0.0, scalar2=None, op0=ALU.is_ge), [o14], f"f_b3{t}{s}")
                fb = [col(20 + j) for j in range(4)]
                oc0 = emit_dve(lambda eng, b0=b0, d=fb[0]: eng.tensor_copy(out=d, in_=b0), [o15], f"f_fc0{t}{s}")
                oc1 = emit_dve(lambda eng, b1=b1, d=fb[1]: eng.tensor_copy(out=d, in_=b1), [oc0], f"f_fc1{t}{s}")
                oc2 = emit_dve(lambda eng, b2=b2, d=fb[2]: eng.tensor_copy(out=d, in_=b2), [oc1], f"f_fc2{t}{s}")
                oc3 = emit_dve(lambda eng, b3=b3, d=fb[3]: eng.tensor_copy(out=d, in_=b3), [oc2], f"f_fc3{t}{s}")
                leaf = col(15)
                o16 = emit_dve(lambda eng, a=fb[0], b=fb[1], leaf=leaf: eng.scalar_tensor_tensor(out=leaf, in0=a, scalar=2.0, in1=b, op0=ALU.mult, op1=ALU.add), [oc3], f"f_lf1{t}{s}")
                o17 = emit_dve(lambda eng, b=fb[2], leaf=leaf: eng.scalar_tensor_tensor(out=leaf, in0=leaf, scalar=2.0, in1=b, op0=ALU.mult, op1=ALU.add), [o16], f"f_lf2{t}{s}")
                o18 = emit_dve(lambda eng, b=fb[3], leaf=leaf: eng.scalar_tensor_tensor(out=leaf, in0=leaf, scalar=2.0, in1=b, op0=ALU.mult, op1=ALU.add), [o17], f"f_lf3{t}{s}")
                ohb = tw[:, 16:32]
                oh16 = TW[:, par * 64 + 32:par * 64 + 48]
                o19 = emit_dve(lambda eng, leaf=leaf, oh16=oh16: eng.tensor_tensor(out=oh16, in0=leaf.broadcast_to([128, 16]), in1=iota_sb[:], op=ALU.is_equal), [o18, l_io], f"f_oh{t}{s}")
                o20 = P.op("act", lambda eng, oh16=oh16: eng.activation(out=OHB[:], in_=oh16, func=AF.Copy), deps=[o19, oh_ops[-1] if oh_ops else None], tag=f"f_ohb{t}{s}")
                o21 = P.op("pe", lambda eng: eng.transpose(tbank_bf[0:16, 0:128], OHB[:], eyeb_sb[:]), deps=[o20, l_eb] + ([oh_ops[-1]] if oh_ops else []), tag=f"f_ohtp{t}{s}")
                o22 = P.op("dve", lambda eng, s=s: eng.tensor_copy(out=OHT[0:16, s * 128:(s + 1) * 128], in_=tbank_bf[0:16, 0:128]), deps=[o21], tag=f"f_ohev{t}{s}")
                oh_ops.append(o22)
                prev_ss[par] = o19
            hm_ops = []
            for iio in range(8):
                for tch in range(4):
                    t0 = tch * 512
                    bi1, ps1, war1 = pbank.get()
                    def hmm(eng, iio=iio, t0=t0, ps=ps1):
                        inst = None
                        for c in range(DC):
                            inst = eng.matmul(ps[:, :], W1[:, c * 1024 + iio * 128:c * 1024 + (iio + 1) * 128],
                                              xhatT[:, c * NTOK + t0:c * NTOK + t0 + 512],
                                              start=(c == 0), stop=(c == DC - 1))
                        return inst
                    m1 = P.op("pe", hmm, deps=[lw1] + xT_ops + [war1], tag=f"f_h{t}{iio}_{t0}")
                    bi2, ps2, war2 = pbank.get()
                    def mmm(eng, iio=iio, t0=t0, ps=ps2):
                        return eng.matmul(ps[:, :], expm_sb[0:16, iio * 128:(iio + 1) * 128],
                                          OHT[0:16, t0:t0 + 512], start=True, stop=True)
                    m2 = P.op("pe", mmm, deps=oh_ops + [l_ex, war2], tag=f"f_msk{t}{iio}_{t0}")
                    e1 = P.op("act", lambda eng, ps=ps1, iio=iio: eng.activation(out=SA1[:], in_=ps[:, :], func=AF.Relu, bias=B1C[:, iio:iio + 1]), deps=[m1, lb1, hm_ops[-1] if hm_ops else None], tag=f"f_he{t}{iio}_{t0}")
                    dst = HM[:, iio * 2048 + t0:iio * 2048 + t0 + 512]
                    e2 = P.op("dve", lambda eng, ps=ps2, dst=dst: eng.tensor_tensor(out=dst, in0=ps[:, :], in1=SA1[:], op=ALU.mult), deps=[m2, e1] + prev_tree_ops, tag=f"f_hm{t}{iio}_{t0}")
                    pbank.done(bi1, e1)
                    pbank.done(bi2, e2)
                    hm_ops.append(e2)
            tree_out_ops = []
            if out_kind == "fT":
                dram_o = qT_o if t == "q" else kT_o
                for co in range(DC):
                    ost = OST[co % 2]
                    ev4 = []
                    for nch in range(4):
                        bi, ps, war = pbank.get()
                        def omm(eng, co=co, nch=nch, ps=ps):
                            for lh in range(DC):
                                eng.matmul(ps[:, :], W2[:, lh * 1024 + co * 128:lh * 1024 + (co + 1) * 128],
                                           HM[:, lh * 2048 + nch * 512:lh * 2048 + (nch + 1) * 512],
                                           start=(lh == 0), stop=False)
                            return eng.matmul(ps[:, :], B2L[0:16, co * 128:(co + 1) * 128],
                                              OHT[0:16, nch * 512:(nch + 1) * 512], start=False, stop=True)
                        mm = P.op("pe", omm, deps=hm_ops + oh_ops + [lw2, lb2, war, store_ops[-1] if store_ops else None], tag=f"f_o{t}{co}_{nch}")
                        tree_out_ops.append(mm)
                        ev = P.op("dve", lambda eng, ps=ps, ost=ost, nch=nch: eng.tensor_copy(out=ost[:, nch * 512:(nch + 1) * 512], in_=ps[:, :]), deps=[mm], tag=f"f_oev{t}{co}_{nch}")
                        pbank.done(bi, ev)
                        ev4.append(ev)
                    so = P.dma("sp", lambda eng, co=co, dram_o=dram_o, ost=ost: eng.dma_start(out=dram_o.rearrange("(c p) n -> p c n", p=128)[:, co, :], in_=ost[:]), deps=ev4, tag=f"f_st{t}{co}")
                    store_ops.append(so)
            else:
                for s2 in range(NSTRIP // 2):
                    ost = OST[s2 % 2]
                    ev4 = []
                    for sj in range(2):
                        s = s2 * 2 + sj
                        for nch in range(2):
                            bi, ps, war = pbank.get()
                            def vmm(eng, s=s, nch=nch, ps=ps):
                                for lh in range(DC):
                                    eng.matmul(ps[:, :], HM[:, lh * 2048 + s * 128:lh * 2048 + (s + 1) * 128],
                                               W2[:, lh * 1024 + nch * 512:lh * 1024 + (nch + 1) * 512],
                                               start=(lh == 0), stop=False)
                                return eng.matmul(ps[:, :], OHT[0:16, s * 128:(s + 1) * 128],
                                                  B2L[0:16, nch * 512:(nch + 1) * 512], start=False, stop=True)
                            mm = P.op("pe", vmm, deps=hm_ops + oh_ops + [lw2, lb2, war, store_ops[-1] if store_ops else None], tag=f"f_v{s}_{nch}")
                            tree_out_ops.append(mm)
                            ev = P.op("dve", lambda eng, ps=ps, ost=ost, sj=sj, nch=nch: eng.tensor_copy(out=ost[:, sj * 1024 + nch * 512:sj * 1024 + (nch + 1) * 512], in_=ps[:, :]), deps=[mm], tag=f"f_vev{s}_{nch}")
                            pbank.done(bi, ev)
                            ev4.append(ev)
                    so = P.dma("sp", lambda eng, s2=s2, ost=ost: eng.dma_start(out=v_o.rearrange("(s p) d -> p s d", p=128)[:, s2 * 2:s2 * 2 + 2, :], in_=ost[:].rearrange("p (a b) -> p a b", a=2)), deps=ev4, tag=f"f_vst{s2}")
                    store_ops.append(so)
            prev_tree_ops = hm_ops + oh_ops + tree_out_ops + [store_ops[-1]]
        fin = P.op("pool", lambda eng: eng.nop(), deps=[store_ops[-1]], tag="fin")
        fin.need = False
        P.emit()
    return nc

def build_s1b(scale=1.0 / 32.0):
    nc = bass.Bass("TRN2", target_bir_lowering=False)
    x = nc.dram_tensor("x", [NTOK, D], F32, kind="ExternalInput").ap()
    qT = nc.dram_tensor("qT", [D, NTOK], BF16, kind="ExternalInput").ap()
    kT = nc.dram_tensor("kT", [D, SEQ], BF16, kind="ExternalInput").ap()
    v = nc.dram_tensor("v", [SEQ, D], BF16, kind="ExternalInput").ap()
    eyeb = nc.dram_tensor("eyeb", [128, 128], BF16, kind="ExternalInput").ap()
    x_o = nc.dram_tensor("x_o", [NTOK, D], F32, kind="ExternalOutput").ap()

    import contextlib
    with contextlib.ExitStack() as stack:
        al = Alloc(nc, stack)
        P = Prog(nc)
        sbf = al.sb
        eyeb_sb = sbf((128, 128), BF16, "a_eyeb")
        l_eb = P.dma("sp", lambda eng: eng.dma_start(out=eyeb_sb[:], in_=eyeb[:, :]))
        SC = [sbf((128, SEQ), F32, f"a_sc{i}") for i in range(4)]
        SCB = [t.ap().bitcast(BF16) for t in SC]
        QTS = sbf((128, 4096), BF16, "a_qts")
        KC = sbf((128, 4096), BF16, "a_kc")
        VC = [sbf((128, 8192), BF16, f"a_vc{i}") for i in range(2)]
        ACC = sbf((128, 4 * 1024), F32, "a_acc")
        STT = sbf((128, 16), F32, "a_stt")
        XST = [sbf((128, 1024), F32, f"a_xst{i}") for i in range(2)]
        pA = Banks(al, 4, name="a_pA")
        pB = Banks(al, 4, name="a_pB")
        n_blk = 4
        slot_reader = [None] * 4
        vc_reader = [None, None]
        qts_reader = None
        prev_fo = [None] * 4
        out_ops = []
        for blk in range(n_blk):
            lq = P.dma("sp", lambda eng, blk=blk: eng.dma_start(
                out=QTS[:].rearrange("p (c n) -> p c n", c=DC),
                in_=qT.rearrange("(c p) n -> p c n", p=128)[:, :, blk * 512:(blk + 1) * 512]),
                deps=[qts_reader], tag=f"a_lq{blk}")
            sc_evs = [[] for _ in range(4)]
            kc_reader = None
            for kc in range(16):
                lk = P.dma("act", lambda eng, kc=kc: eng.dma_start(
                    out=KC[:].rearrange("p (c n) -> p c n", c=DC),
                    in_=kT.rearrange("(c p) n -> p c n", p=128)[:, :, kc * 512:(kc + 1) * 512]),
                    deps=[kc_reader], tag=f"a_lk{blk}_{kc}")
                mms = []
                for si in range(4):
                    bi, ps, war = pA.get()
                    def smm(eng, si=si, ps=ps):
                        inst = None
                        for c in range(DC):
                            inst = eng.matmul(ps[:, :], QTS[:, c * 512 + si * 128:c * 512 + (si + 1) * 128],
                                              KC[:, c * 512:(c + 1) * 512], start=(c == 0), stop=(c == DC - 1))
                        return inst
                    mm = P.op("pe", smm, deps=[lq, lk, war, slot_reader[si]], tag=f"a_smm{blk}_{kc}_{si}")
                    mms.append(mm)
                    dst = SC[si][:, kc * 512:(kc + 1) * 512]
                    ev = P.op("act", lambda eng, dst=dst, ps=ps: eng.activation(out=dst, in_=ps[:, :], func=AF.Copy), deps=[mm], tag=f"a_sev{blk}_{kc}_{si}")
                    pA.done(bi, ev)
                    sc_evs[si].append(ev)
                kc_reader = mms[-1]
            exp_ops = []
            rc_ops = []
            for si in range(4):
                c0 = si * 4
                o1 = P.op("dve", lambda eng, si=si, c0=c0: eng.tensor_reduce(out=STT[:, c0:c0 + 1], in_=SC[si][:], axis=AX.X, op=ALU.max), deps=sc_evs[si] + [prev_fo[si]], tag=f"a_mx{blk}_{si}")
                o2 = P.op("dve", lambda eng, c0=c0: eng.tensor_scalar_mul(STT[:, c0 + 1:c0 + 2], STT[:, c0:c0 + 1], -scale), deps=[o1])
                o3 = P.op("act", lambda eng, si=si, c0=c0: eng.activation(out=SCB[si][:, 0:SEQ], in_=SC[si][:], func=AF.Exp, scale=scale, bias=STT[:, c0 + 1:c0 + 2], accum_out=STT[:, c0 + 2:c0 + 3]), deps=[o2] + sc_evs[si], tag=f"a_exp{blk}_{si}")
                o4 = P.op("dve", lambda eng, c0=c0: eng.reciprocal(out=STT[:, c0 + 3:c0 + 4], in_=STT[:, c0 + 2:c0 + 3]), deps=[o3])
                exp_ops.append(o3)
                rc_ops.append(o4)
            acc_ops = [None] * 4
            for kg in range(8):
                lv = P.dma("sp", lambda eng, kg=kg: eng.dma_start(
                    out=VC[kg % 2][:].rearrange("p (g d) -> p g d", g=8),
                    in_=v.rearrange("(g p) d -> p g d", p=128)[:, kg * 8:(kg + 1) * 8, :]),
                    deps=[vc_reader[kg % 2]], tag=f"a_lv{blk}_{kg}")
                for si in range(4):
                    bi_t, ps_t, war_t = pA.get()
                    ps_t_bf = ps_t.ap().bitcast(BF16)
                    def tpf(eng, si=si, kg=kg, ps_t_bf=ps_t_bf):
                        inst = None
                        for j in range(8):
                            inst = eng.transpose(ps_t_bf[:, j * 128:(j + 1) * 128],
                                                 SCB[si][:, kg * 1024 + j * 128:kg * 1024 + (j + 1) * 128],
                                                 eyeb_sb[:])
                        return inst
                    tp = P.op("pe", tpf, deps=[exp_ops[si], l_eb, war_t], tag=f"a_tp{blk}_{kg}_{si}")
                    ptb = SCB[si][:, 8192 + (kg % 2) * 1024: 8192 + (kg % 2) * 1024 + 1024]
                    pt_ev = P.op("dve", lambda eng, ptb=ptb, ps_t_bf=ps_t_bf: eng.tensor_copy(out=ptb, in_=ps_t_bf[:, 0:1024]), deps=[tp] + ([slot_reader[si]] if kg >= 2 else []), tag=f"a_ptev{blk}_{kg}_{si}")
                    pA.done(bi_t, pt_ev)
                    bi1, ps1, war1 = pB.get()
                    bi2, ps2, war2 = pB.get()
                    def pvmm(eng, ptb=ptb, kg=kg, ps1=ps1, ps2=ps2):
                        inst = None
                        for j in range(8):
                            for dv in range(2):
                                inst = eng.matmul((ps1 if dv == 0 else ps2)[:, :], ptb[:, j * 128:(j + 1) * 128],
                                                  VC[kg % 2][:, j * 1024 + dv * 512:j * 1024 + (dv + 1) * 512],
                                                  start=(j == 0), stop=(j == 7))
                        return inst
                    mm = P.op("pe", pvmm, deps=[pt_ev, lv, war1, war2], tag=f"a_pv{blk}_{kg}_{si}")
                    slot_reader[si] = mm
                    if kg == 7:
                        vc_reader[kg % 2] = mm
                    else:
                        vc_reader[kg % 2] = mm
                    accd = ACC[:, si * 1024:(si + 1) * 1024]
                    if kg == 0:
                        a1 = P.op("dve", lambda eng, accd=accd, ps1=ps1: eng.tensor_copy(out=accd[:, 0:512], in_=ps1[:, :]), deps=[mm, acc_ops[si], prev_fo[si]], tag=f"a_ac{blk}_{kg}_{si}_0")
                        a2 = P.op("dve", lambda eng, accd=accd, ps2=ps2: eng.tensor_copy(out=accd[:, 512:1024], in_=ps2[:, :]), deps=[mm, prev_fo[si]], tag=f"a_ac{blk}_{kg}_{si}_1")
                    else:
                        a1 = P.op("dve", lambda eng, accd=accd, ps1=ps1: eng.tensor_tensor(out=accd[:, 0:512], in0=ps1[:, :], in1=accd[:, 0:512], op=ALU.add), deps=[mm, acc_ops[si]], tag=f"a_ac{blk}_{kg}_{si}_0")
                        a2 = P.op("dve", lambda eng, accd=accd, ps2=ps2: eng.tensor_tensor(out=accd[:, 512:1024], in0=ps2[:, :], in1=accd[:, 512:1024], op=ALU.add), deps=[mm], tag=f"a_ac{blk}_{kg}_{si}_1")
                    pB.done(bi1, a1)
                    pB.done(bi2, a2)
                    acc_ops[si] = a2
            for si in range(4):
                s = blk * 4 + si
                xs = XST[si % 2]
                lx = P.dma("sp", lambda eng, s=s, xs=xs: eng.dma_start(out=xs[:], in_=x.rearrange("(s p) d -> p s d", p=128)[:, s, :]), deps=[out_ops[-2] if len(out_ops) >= 2 else None], tag=f"a_lx{s}")
                c0 = si * 4
                fo = P.op("dve", lambda eng, si=si, xs=xs, c0=c0: eng.scalar_tensor_tensor(out=xs[:], in0=ACC[:, si * 1024:(si + 1) * 1024], scalar=STT[:, c0 + 3:c0 + 4], in1=xs[:], op0=ALU.mult, op1=ALU.add), deps=[acc_ops[si], rc_ops[si], lx], tag=f"a_fo{s}")
                so = P.dma("sp", lambda eng, s=s, xs=xs: eng.dma_start(out=x_o.rearrange("(s p) d -> p s d", p=128)[:, s, :], in_=xs[:]), deps=[fo], tag=f"a_so{s}")
                out_ops.append(so)
                prev_fo[si] = fo
            qts_reader = kc_reader
        fin = P.op("pool", lambda eng: eng.nop(), deps=out_ops[-2:], tag="fin")
        fin.need = False
        P.emit()
    return nc


def build_s6a():
    nc = bass.Bass("TRN2", target_bir_lowering=False)
    x = nc.dram_tensor("x", [NTOK, D], F32, kind="ExternalInput").ap()
    wq12 = nc.dram_tensor("wq12", [D, 2048], F32, kind="ExternalInput").ap()
    wk12 = nc.dram_tensor("wk12", [D, 2048], F32, kind="ExternalInput").ap()
    wv = nc.dram_tensor("wv", [D, D], F32, kind="ExternalInput").ap()
    eyeb = nc.dram_tensor("eyeb", [128, 128], BF16, kind="ExternalInput").ap()
    q12T_o = nc.dram_tensor("q12T_o", [2048, NTOK], BF16, kind="ExternalOutput").ap()
    k12T_o = nc.dram_tensor("k12T_o", [2048, NTOK], BF16, kind="ExternalOutput").ap()
    v_o = nc.dram_tensor("v_o", [NTOK, D], BF16, kind="ExternalOutput").ap()
    import contextlib
    with contextlib.ExitStack() as stack:
        al = Alloc(nc, stack)
        P = Prog(nc)
        sbf = al.sb
        eyeb_sb = sbf((128, 128), BF16, "d_eyeb")
        l_eb = P.dma("sp", lambda eng: eng.dma_start(out=eyeb_sb[:], in_=eyeb[:, :]))
        xhatT = sbf((128, DC * NTOK), BF16, "d_xhatT")
        ln_ps = al.ps((128, 512), F32, "d_lnps")
        x_sb, x_loads, xT_ops = emit_ln_xhatT(P, al, x, xhatT, eyeb_sb, ln_ps, tag="d_ln", keep_x=False)
        for o in P.ops["pe"]:
            o.deps.append(l_eb)
        W = sbf((128, 16384), BF16, "d_W")
        WVs = sbf((128, 8192), BF16, "d_WV")
        OST = [sbf((128, 2048), BF16, f"d_ost{i}") for i in range(2)]
        pbank = Banks(al, 4, name="d_pb")
        store_ops = []
        prev_w_readers = []
        for t, wdram, odram, ncols in (("q", wq12, q12T_o, 2048), ("k", wk12, k12T_o, 2048)):
            lw = P.dma("pool", lambda eng, wdram=wdram: eng.dma_start(out=W[:].rearrange("p (c o) -> p c o", c=DC), in_=wdram.rearrange("(c p) o -> p c o", p=128)), deps=prev_w_readers, tag=f"d_lw{t}")
            readers = []
            for co in range(16):
                ost = OST[co % 2]
                ev4 = []
                for nch in range(4):
                    bi, ps, war = pbank.get()
                    def omm(eng, co=co, nch=nch, ps=ps):
                        inst = None
                        for c in range(DC):
                            inst = eng.matmul(ps[:, :], W[:, c * 2048 + co * 128:c * 2048 + (co + 1) * 128],
                                              xhatT[:, c * NTOK + nch * 512:c * NTOK + (nch + 1) * 512],
                                              start=(c == 0), stop=(c == DC - 1))
                        return inst
                    mm = P.op("pe", omm, deps=xT_ops + [lw, war, store_ops[-1] if store_ops else None], tag=f"d_{t}mm{co}_{nch}")
                    readers.append(mm)
                    ev = P.op("dve", lambda eng, ps=ps, ost=ost, nch=nch: eng.tensor_copy(out=ost[:, nch * 512:(nch + 1) * 512], in_=ps[:, :]), deps=[mm], tag=f"d_{t}ev{co}_{nch}")
                    pbank.done(bi, ev)
                    ev4.append(ev)
                so = P.dma("sp", lambda eng, co=co, odram=odram, ost=ost: eng.dma_start(out=odram.rearrange("(c p) n -> p c n", p=128)[:, co, :], in_=ost[:]), deps=ev4, tag=f"d_st{t}{co}")
                store_ops.append(so)
            prev_w_readers = readers
        lwv = P.dma("pool", lambda eng: eng.dma_start(out=WVs[:].rearrange("p (c o) -> p c o", c=DC), in_=wv.rearrange("(c p) o -> p c o", p=128)), tag="d_lwv")
        for s2 in range(NSTRIP // 2):
            ost = OST[s2 % 2]
            ev4 = []
            for sj in range(2):
                s = s2 * 2 + sj
                for nch in range(2):
                    bi, ps, war = pbank.get()
                    def vmm(eng, s=s, nch=nch, ps=ps):
                        inst = None
                        for c in range(DC):
                            inst = eng.matmul(ps[:, :], xhatT[:, c * NTOK + s * 128:c * NTOK + (s + 1) * 128],
                                              WVs[:, c * 1024 + nch * 512:c * 1024 + (nch + 1) * 512],
                                              start=(c == 0), stop=(c == DC - 1))
                        return inst
                    mm = P.op("pe", vmm, deps=xT_ops + [lwv, war, store_ops[-1] if store_ops else None], tag=f"d_vmm{s}_{nch}")
                    ev = P.op("dve", lambda eng, ps=ps, ost=ost, sj=sj, nch=nch: eng.tensor_copy(out=ost[:, sj * 1024 + nch * 512:sj * 1024 + (nch + 1) * 512], in_=ps[:, :]), deps=[mm], tag=f"d_vev{s}_{nch}")
                    pbank.done(bi, ev)
                    ev4.append(ev)
            so = P.dma("sp", lambda eng, s2=s2, ost=ost: eng.dma_start(out=v_o.rearrange("(s p) d -> p s d", p=128)[:, s2 * 2:s2 * 2 + 2, :], in_=ost[:].rearrange("p (a b) -> p a b", a=2)), deps=ev4, tag=f"d_vst{s2}")
            store_ops.append(so)
        fin = P.op("pool", lambda eng: eng.nop(), deps=[store_ops[-1]], tag="fin")
        fin.need = False
        P.emit()
    return nc


def build_s6b():
    nc = bass.Bass("TRN2", target_bir_lowering=False)
    x = nc.dram_tensor("x", [NTOK, D], F32, kind="ExternalInput").ap()
    qT = nc.dram_tensor("qT", [2048, NTOK], BF16, kind="ExternalInput").ap()
    kT = nc.dram_tensor("kT", [2048, SEQ], BF16, kind="ExternalInput").ap()
    v = nc.dram_tensor("v", [SEQ, D], BF16, kind="ExternalInput").ap()
    lams = nc.dram_tensor("lams", [4, D], F32, kind="ExternalInput").ap()
    eyeb = nc.dram_tensor("eyeb", [128, 128], BF16, kind="ExternalInput").ap()
    onesr = nc.dram_tensor("onesr", [1, 128], BF16, kind="ExternalInput").ap()
    x_o = nc.dram_tensor("x_o", [NTOK, D], F32, kind="ExternalOutput").ap()
    scale = 1024.0 ** (-0.25)
    import contextlib
    with contextlib.ExitStack() as stack:
        al = Alloc(nc, stack)
        P = Prog(nc)
        sbf = al.sb
        eyeb_sb = sbf((128, 128), BF16, "e_eyeb")
        onesr_sb = sbf((1, 128), BF16, "e_onesr")
        l_eb = P.dma("sp", lambda eng: eng.dma_start(out=eyeb_sb[:], in_=eyeb[:, :]))
        l_on = P.dma("sp", lambda eng: eng.dma_start(out=onesr_sb[:], in_=onesr[:, :]))
        lam_sb = sbf((1, 4 * D), F32, "e_lam")
        lsc = sbf((1, 16), F32, "e_lsc")
        lscb = sbf((1, 2), BF16, "e_lscb")
        lcol = sbf((128, 1), F32, "e_lcol")
        l_lm = P.dma("sp", lambda eng: eng.dma_start(out=lam_sb[:].rearrange("a (k d) -> a k d", k=4), in_=lams[:, :].rearrange("k d -> (k d)").rearrange("(a n) -> a n", a=1).rearrange("a (k d) -> a k d", k=4)))
        lm1 = P.op("dve", lambda eng: eng.tensor_tensor(out=lam_sb[0:1, 0:D], in0=lam_sb[0:1, 0:D], in1=lam_sb[0:1, D:2 * D], op=ALU.mult), deps=[l_lm], tag="e_l1")
        lm2 = P.op("dve", lambda eng: eng.tensor_reduce(out=lsc[:, 0:1], in_=lam_sb[0:1, 0:D], axis=AX.X, op=ALU.add), deps=[lm1])
        lm3 = P.op("dve", lambda eng: eng.tensor_tensor(out=lam_sb[0:1, 2 * D:3 * D], in0=lam_sb[0:1, 2 * D:3 * D], in1=lam_sb[0:1, 3 * D:4 * D], op=ALU.mult), deps=[l_lm], tag="e_l2")
        lm4 = P.op("dve", lambda eng: eng.tensor_reduce(out=lsc[:, 1:2], in_=lam_sb[0:1, 2 * D:3 * D], axis=AX.X, op=ALU.add), deps=[lm3, lm2])
        lm5 = P.op("act", lambda eng: eng.activation(out=lsc[:, 2:4], in_=lsc[:, 0:2], func=AF.Exp), deps=[lm4], tag="e_exp")
        lm6 = P.op("dve", lambda eng: eng.tensor_sub(lsc[:, 4:5], lsc[:, 2:3], lsc[:, 3:4]), deps=[lm5])
        lm7 = P.op("dve", lambda eng: eng.tensor_scalar(out=lsc[:, 5:6], in0=lsc[:, 4:5], scalar1=0.8, scalar2=None, op0=ALU.add), deps=[lm6], tag="e_lam_full")
        lm8 = P.op("act", lambda eng: eng.activation(out=lscb[0:1, 0:1], in_=lsc[:, 5:6], func=AF.Copy), deps=[lm7])
        lps = al.ps((128, 512), F32, "e_lps")
        lm9 = P.op("pe", lambda eng: eng.matmul(lps[:, 0:1], onesr_sb[0:1, :], lscb[0:1, 0:1], start=True, stop=True), deps=[lm8, l_on], tag="e_lbc")
        lm10 = P.op("dve", lambda eng: eng.tensor_copy(out=lcol[:], in_=lps[:, 0:1]), deps=[lm9], tag="e_lcol")
        SC = [sbf((128, SEQ), F32, f"e_sc{i}") for i in range(4)]
        SCB = [t.ap().bitcast(BF16) for t in SC]
        QTS = sbf((128, 16 * 256), BF16, "e_qts")
        KC = sbf((128, 16 * 512), BF16, "e_kc")
        VC = sbf((128, 8192), BF16, "e_vc")
        ACC = sbf((128, 2 * 1024), F32, "e_acc")
        STT = sbf((128, 16), F32, "e_stt")
        SQS = sbf((128, 512), BF16, "e_sqs")
        XST = [sbf((128, 1024), F32, f"e_xst{i}") for i in range(2)]
        pA = Banks(al, 4, name="e_pA")
        pB = Banks(al, 2, name="e_pB")
        slot_reader = [None] * 4
        vc_reader = None
        qts_reader = None
        prev_fo = [None] * 2
        out_ops = []
        for blk in range(8):
            lq = P.dma("sp", lambda eng, blk=blk: eng.dma_start(
                out=QTS[:].rearrange("p (c n) -> p c n", c=16),
                in_=qT.rearrange("(c p) n -> p c n", p=128)[:, :, blk * 256:(blk + 1) * 256]),
                deps=[qts_reader], tag=f"e_lq{blk}")
            sc_evs = [[] for _ in range(4)]
            kc_reader = None
            for kc in range(16):
                lk = P.dma("act", lambda eng, kc=kc: eng.dma_start(
                    out=KC[:].rearrange("p (c n) -> p c n", c=16),
                    in_=kT.rearrange("(c p) n -> p c n", p=128)[:, :, kc * 512:(kc + 1) * 512]),
                    deps=[kc_reader], tag=f"e_lk{blk}_{kc}")
                mms = []
                for si in range(2):
                    for g in range(2):
                        slot = si * 2 + g
                        bi, ps, war = pA.get()
                        def smm(eng, si=si, g=g, ps=ps):
                            inst = None
                            for c in range(8):
                                inst = eng.matmul(ps[:, :], QTS[:, (g * 8 + c) * 256 + si * 128:(g * 8 + c) * 256 + (si + 1) * 128],
                                                  KC[:, (g * 8 + c) * 512:(g * 8 + c + 1) * 512], start=(c == 0), stop=(c == 7))
                            return inst
                        mm = P.op("pe", smm, deps=[lq, lk, war, slot_reader[slot]], tag=f"e_smm{blk}_{kc}_{slot}")
                        mms.append(mm)
                        dst = SC[slot][:, kc * 512:(kc + 1) * 512]
                        ev = P.op("act", lambda eng, dst=dst, ps=ps: eng.activation(out=dst, in_=ps[:, :], func=AF.Copy), deps=[mm], tag=f"e_sev{blk}_{kc}_{slot}")
                        pA.done(bi, ev)
                        sc_evs[slot].append(ev)
                kc_reader = mms[-1]
            comb_ops = [None] * 2
            for si in range(2):
                for g in range(2):
                    slot = si * 2 + g
                    c0 = slot * 4
                    o1 = P.op("dve", lambda eng, slot=slot, c0=c0: eng.tensor_reduce(out=STT[:, c0:c0 + 1], in_=SC[slot][:], axis=AX.X, op=ALU.max), deps=sc_evs[slot] + [prev_fo[si]], tag=f"e_mx{blk}_{slot}")
                    o2 = P.op("dve", lambda eng, c0=c0: eng.tensor_scalar_mul(STT[:, c0 + 1:c0 + 2], STT[:, c0:c0 + 1], -scale), deps=[o1])
                    o3 = P.op("act", lambda eng, slot=slot, c0=c0: eng.activation(out=SCB[slot][:, 0:SEQ], in_=SC[slot][:], func=AF.Exp, scale=scale, bias=STT[:, c0 + 1:c0 + 2], accum_out=STT[:, c0 + 2:c0 + 3]), deps=[o2] + sc_evs[slot], tag=f"e_exp{blk}_{slot}")
                    o4 = P.op("dve", lambda eng, c0=c0: eng.reciprocal(out=STT[:, c0 + 3:c0 + 4], in_=STT[:, c0 + 2:c0 + 3]), deps=[o3], tag=f"e_rc{blk}_{slot}")
                    if g == 0:
                        g0_ops = (o3, o4)
                    else:
                        s1b_ = si * 2
                        s2b_ = si * 2 + 1
                        m1 = P.op("dve", lambda eng, s1b_=s1b_, c0=s1b_ * 4: eng.tensor_scalar(out=SCB[s1b_][:, 0:SEQ], in0=SCB[s1b_][:, 0:SEQ], scalar1=STT[:, c0 + 3:c0 + 4], scalar2=None, op0=ALU.mult), deps=[g0_ops[0], g0_ops[1]], tag=f"e_n1{blk}_{si}")
                        m2 = P.op("dve", lambda eng, c0=c0: eng.tensor_tensor(out=STT[:, c0 + 3:c0 + 4], in0=STT[:, c0 + 3:c0 + 4], in1=lcol[:], op=ALU.mult), deps=[o4, lm10], tag=f"e_lr{blk}_{si}")
                        m3 = P.op("dve", lambda eng, c0=c0: eng.tensor_scalar_mul(STT[:, c0 + 3:c0 + 4], STT[:, c0 + 3:c0 + 4], -1.0), deps=[m2])
                        m4 = P.op("dve", lambda eng, s1b_=s1b_, s2b_=s2b_, c0=c0: eng.scalar_tensor_tensor(out=SCB[s1b_][:, 0:SEQ], in0=SCB[s2b_][:, 0:SEQ], scalar=STT[:, c0 + 3:c0 + 4], in1=SCB[s1b_][:, 0:SEQ], op0=ALU.mult, op1=ALU.add), deps=[m1, m3, o3], tag=f"e_comb{blk}_{si}")
                        comb_ops[si] = m4
            acc_ops = [None] * 2
            for kg in range(8):
                lv = P.dma("sp", lambda eng, kg=kg: eng.dma_start(
                    out=VC[:].rearrange("p (g d) -> p g d", g=8),
                    in_=v.rearrange("(g p) d -> p g d", p=128)[:, kg * 8:(kg + 1) * 8, :]),
                    deps=[vc_reader], tag=f"e_lv{blk}_{kg}")
                for si in range(2):
                    slot = si * 2
                    bi_t, ps_t, war_t = pA.get()
                    ps_t_bf = ps_t.ap().bitcast(BF16)
                    def tpf(eng, slot=slot, kg=kg, ps_t_bf=ps_t_bf):
                        inst = None
                        for j in range(8):
                            inst = eng.transpose(ps_t_bf[:, j * 128:(j + 1) * 128],
                                                 SCB[slot][:, kg * 1024 + j * 128:kg * 1024 + (j + 1) * 128],
                                                 eyeb_sb[:])
                        return inst
                    tp = P.op("pe", tpf, deps=[comb_ops[si], l_eb, war_t], tag=f"e_tp{blk}_{kg}_{si}")
                    ptb = SCB[slot][:, 8192 + (kg % 2) * 1024: 8192 + (kg % 2) * 1024 + 1024]
                    pt_ev = P.op("dve", lambda eng, ptb=ptb, ps_t_bf=ps_t_bf: eng.tensor_copy(out=ptb, in_=ps_t_bf[:, 0:1024]), deps=[tp] + ([slot_reader[slot]] if kg >= 2 else []), tag=f"e_ptev{blk}_{kg}_{si}")
                    pA.done(bi_t, pt_ev)
                    bi1, ps1, war1 = pB.get()
                    bi2, ps2, war2 = pB.get()
                    def pvmm(eng, ptb=ptb, ps1=ps1, ps2=ps2):
                        inst = None
                        for j in range(8):
                            for dv in range(2):
                                inst = eng.matmul((ps1 if dv == 0 else ps2)[:, :], ptb[:, j * 128:(j + 1) * 128],
                                                  VC[:, j * 1024 + dv * 512:j * 1024 + (dv + 1) * 512],
                                                  start=(j == 0), stop=(j == 7))
                        return inst
                    mm = P.op("pe", pvmm, deps=[pt_ev, lv, war1, war2], tag=f"e_pv{blk}_{kg}_{si}")
                    slot_reader[slot] = mm
                    slot_reader[slot + 1] = mm
                    vc_reader = mm
                    accd = ACC[:, si * 1024:(si + 1) * 1024]
                    if kg == 0:
                        a1 = P.op("dve", lambda eng, accd=accd, ps1=ps1: eng.tensor_copy(out=accd[:, 0:512], in_=ps1[:, :]), deps=[mm, acc_ops[si], prev_fo[si]], tag=f"e_ac{blk}_{kg}_{si}_0")
                        a2 = P.op("dve", lambda eng, accd=accd, ps2=ps2: eng.tensor_copy(out=accd[:, 512:1024], in_=ps2[:, :]), deps=[mm, prev_fo[si]], tag=f"e_ac{blk}_{kg}_{si}_1")
                    else:
                        a1 = P.op("dve", lambda eng, accd=accd, ps1=ps1: eng.tensor_tensor(out=accd[:, 0:512], in0=ps1[:, :], in1=accd[:, 0:512], op=ALU.add), deps=[mm, acc_ops[si]], tag=f"e_ac{blk}_{kg}_{si}_0")
                        a2 = P.op("dve", lambda eng, accd=accd, ps2=ps2: eng.tensor_tensor(out=accd[:, 512:1024], in0=ps2[:, :], in1=accd[:, 512:1024], op=ALU.add), deps=[mm], tag=f"e_ac{blk}_{kg}_{si}_1")
                    pB.done(bi1, a1)
                    pB.done(bi2, a2)
                    acc_ops[si] = a2
            for si in range(2):
                s = blk * 2 + si
                c0 = si * 4
                accd = ACC[:, si * 1024:(si + 1) * 1024]
                r1 = P.op("act", lambda eng, accd=accd, c0=c0: eng.activation(out=SQS[:], in_=accd[:, 0:512], func=AF.Square, accum_out=STT[:, c0:c0 + 1]), deps=[acc_ops[si]], tag=f"e_sq1{s}")
                r2 = P.op("act", lambda eng, accd=accd, c0=c0: eng.activation(out=SQS[:], in_=accd[:, 512:1024], func=AF.Square, accum_out=STT[:, c0 + 1:c0 + 2]), deps=[r1], tag=f"e_sq2{s}")
                r3 = P.op("dve", lambda eng, c0=c0: eng.tensor_tensor(out=STT[:, c0:c0 + 1], in0=STT[:, c0:c0 + 1], in1=STT[:, c0 + 1:c0 + 2], op=ALU.add), deps=[r2], tag=f"e_ssum{s}")
                r4 = P.op("dve", lambda eng, c0=c0: eng.tensor_scalar(out=STT[:, c0:c0 + 1], in0=STT[:, c0:c0 + 1], scalar1=1.0 / 1024.0, scalar2=1e-5, op0=ALU.mult, op1=ALU.add), deps=[r3])
                r5 = P.op("act", lambda eng, c0=c0: eng.activation(out=STT[:, c0:c0 + 1], in_=STT[:, c0:c0 + 1], func=AF.Sqrt), deps=[r4])
                r6 = P.op("dve", lambda eng, c0=c0: eng.reciprocal(out=STT[:, c0:c0 + 1], in_=STT[:, c0:c0 + 1]), deps=[r5])
                r7 = P.op("dve", lambda eng, c0=c0: eng.tensor_scalar_mul(STT[:, c0:c0 + 1], STT[:, c0:c0 + 1], 0.2), deps=[r6], tag=f"e_rs{s}")
                xs = XST[si]
                lx = P.dma("sp", lambda eng, s=s, xs=xs: eng.dma_start(out=xs[:], in_=x.rearrange("(s p) d -> p s d", p=128)[:, s, :]), deps=[out_ops[-2] if len(out_ops) >= 2 else None], tag=f"e_lx{s}")
                fo = P.op("dve", lambda eng, accd=accd, xs=xs, c0=c0: eng.scalar_tensor_tensor(out=xs[:], in0=accd, scalar=STT[:, c0:c0 + 1], in1=xs[:], op0=ALU.mult, op1=ALU.add), deps=[r7, lx], tag=f"e_fo{s}")
                so = P.dma("sp", lambda eng, s=s, xs=xs: eng.dma_start(out=x_o.rearrange("(s p) d -> p s d", p=128)[:, s, :], in_=xs[:]), deps=[fo], tag=f"e_so{s}")
                out_ops.append(so)
                prev_fo[si] = fo
        fin = P.op("pool", lambda eng: eng.nop(), deps=out_ops[-2:], tag="fin")
        fin.need = False
        P.emit()
    return nc


def emit_decay_tables(P, al, decay_row_ap, faaaa_row_ap, onesr_sb, consts, ps, tag="r"):
    sbf = al.sb
    dr = sbf((1, 64), F32, f"{tag}_dr")
    l_d = P.dma("sp", lambda eng: eng.dma_start(out=dr[0:1, 0:16], in_=decay_row_ap), tag=f"{tag}_ld")
    l_f = P.dma("sp", lambda eng: eng.dma_start(out=dr[0:1, 16:32], in_=faaaa_row_ap), tag=f"{tag}_lf")
    o1 = P.op("act", lambda eng: eng.activation(out=dr[0:1, 32:48], in_=dr[0:1, 0:16], func=AF.Exp), deps=[l_d], tag=f"{tag}_exp")
    o2 = P.op("dve", lambda eng: eng.tensor_scalar_mul(dr[0:1, 32:48], dr[0:1, 32:48], -1.0), deps=[o1])
    row = sbf((1, 2048), BF16, f"{tag}_rows")
    o3 = P.op("dve", lambda eng: eng.tensor_copy(out=row[0:1, 0:1024].rearrange("a (h n) -> a h n", h=16), in_=dr[0:1, 32:48].unsqueeze(2).broadcast_to([1, 16, 64])), deps=[o2], tag=f"{tag}_negaexp")
    o4 = P.op("dve", lambda eng: eng.tensor_copy(out=row[0:1, 1024:2048].rearrange("a (h n) -> a h n", h=16), in_=dr[0:1, 16:32].unsqueeze(2).broadcast_to([1, 16, 64])), deps=[l_f], tag=f"{tag}_uexp")
    mm = P.op("pe", lambda eng: [eng.matmul(ps[:, 0:512], onesr_sb[0:1, :], row[0:1, 0:512], start=True, stop=True),
                                 eng.matmul(ps[:, 512:1024], onesr_sb[0:1, :], row[0:1, 512:1024], start=True, stop=True)][-1],
               deps=[o3, o4], tag=f"{tag}_bc1")
    nega128 = sbf((128, 1024), F32, f"{tag}_nega128")
    e1 = P.op("dve", lambda eng: eng.tensor_copy(out=nega128[:], in_=ps[:, 0:1024].rearrange("p (a b) -> p a b", a=2) if False else ps[:, 0:1024]), deps=[mm], tag=f"{tag}_nega_ev")
    mm2 = P.op("pe", lambda eng: [eng.matmul(ps[:, 0:512], onesr_sb[0:1, :], row[0:1, 1024:1536], start=True, stop=True),
                                  eng.matmul(ps[:, 512:1024], onesr_sb[0:1, :], row[0:1, 1536:2048], start=True, stop=True)][-1],
                deps=[o3, o4, e1], tag=f"{tag}_bc2")
    u128 = sbf((128, 1024), F32, f"{tag}_u128")
    e2 = P.op("dve", lambda eng: eng.tensor_copy(out=u128[:], in_=ps[:, 0:1024]), deps=[mm2], tag=f"{tag}_u_ev")
    WtT = sbf((64, 1024), BF16, f"{tag}_WtT")
    t1 = sbf((64, 1024), BF16, f"{tag}_t1")
    w1 = P.op("dve", lambda eng: eng.tensor_tensor(out=t1[:], in0=nega128[0:64, :], in1=consts["ET"][:], op=ALU.mult), deps=[e1], tag=f"{tag}_wt1")
    w2 = P.op("act", lambda eng: eng.activation(out=t1[:], in_=t1[:], func=AF.Exp), deps=[w1])
    w3 = P.op("dve", lambda eng: eng.tensor_tensor(out=t1[:], in0=t1[:], in1=consts["MGT"][:], op=ALU.mult), deps=[w2], tag=f"{tag}_wt3")
    w4 = P.op("dve", lambda eng: eng.scalar_tensor_tensor(out=t1[:], in0=u128[0:64, :], scalar=1.0, in1=t1[:], op0=ALU.bypass, op1=ALU.add) if False else None, deps=[], tag="")
    P.ops["dve"].remove(w4); P.all_ops.remove(w4)
    w5 = P.op("dve", lambda eng: eng.tensor_tensor(out=t1[:].rearrange("p (h n) -> p h n", h=16), in0=u128[0:64, :].rearrange("p (h n) -> p h n", h=16), in1=consts["MEQ"][:].rearrange("p (h n) -> p h n", h=16), op=ALU.mult) if False else None, deps=[], tag="")
    P.ops["dve"].remove(w5); P.all_ops.remove(w5)
    t2 = sbf((64, 1024), BF16, f"{tag}_t2")
    w6 = P.op("dve", lambda eng: eng.tensor_tensor(out=t2[:], in0=u128[0:64, :], in1=consts["MEQ"][:], op=ALU.mult), deps=[e2, w3], tag=f"{tag}_wt6")
    w7 = P.op("dve", lambda eng: eng.tensor_tensor(out=t1[:], in0=t1[:], in1=t2[:], op=ALU.add), deps=[w6], tag=f"{tag}_wt7")
    w8 = P.op("act", lambda eng: eng.activation(out=WtT[:], in_=t1[:], func=AF.Copy), deps=[w7], tag=f"{tag}_wt8")
    wk = sbf((128, 1024), BF16, f"{tag}_wk")
    t3 = sbf((128, 1024), BF16, f"{tag}_t3")
    k1 = P.op("dve", lambda eng: eng.tensor_scalar(out=t3[:], in0=nega128[:], scalar1=consts["E2"][:, 0:1], scalar2=None, op0=ALU.mult), deps=[e1], tag=f"{tag}_wk1")
    k2 = P.op("act", lambda eng: eng.activation(out=wk[:], in_=t3[:], func=AF.Exp), deps=[k1], tag=f"{tag}_wk2")
    wb = sbf((64, 1024), F32, f"{tag}_wb")
    b1 = P.op("dve", lambda eng: eng.tensor_scalar(out=t3[:], in0=nega128[:], scalar1=consts["TC"][:, 0:1], scalar2=None, op0=ALU.mult), deps=[k2, e1], tag=f"{tag}_wb1")
    b2 = P.op("act", lambda eng: eng.activation(out=wb[:], in_=t3[0:64, :], func=AF.Exp), deps=[b1], tag=f"{tag}_wb2")
    ws = sbf((128, 1024), F32, f"{tag}_ws")
    s1 = P.op("dve", lambda eng: eng.tensor_scalar_mul(t3[:], nega128[:], 64.0), deps=[b2, e1], tag=f"{tag}_ws1")
    s2 = P.op("act", lambda eng: eng.activation(out=ws[:], in_=t3[:], func=AF.Exp), deps=[s1], tag=f"{tag}_ws2")
    done = [w8, k2, b2, s2]
    return dict(nega128=nega128, WtT=WtT, wk=wk, wb=wb, ws=ws, negarow=dr[0:1, 32:48],
                done=done, t3=t3, t3_war=s2)

def build_s4a():
    nc = bass.Bass("TRN2", target_bir_lowering=False)
    x = nc.dram_tensor("x", [NTOK, D], F32, kind="ExternalInput").ap()
    xprev = nc.dram_tensor("xprev", [1, D], F32, kind="ExternalInput").ap()
    maa = {m: nc.dram_tensor(f"maa_{m}", [128, 8], F32, kind="ExternalInput").ap() for m in "kvrg"}
    decay_row = nc.dram_tensor("decay_row", [1, 16], F32, kind="ExternalInput").ap()
    faaaa_row = nc.dram_tensor("faaaa_row", [1, 16], F32, kind="ExternalInput").ap()
    W_T = {m: nc.dram_tensor(f"W{m}T", [D, D], F32, kind="ExternalInput").ap() for m in "rkvg"}
    eyeb = nc.dram_tensor("eyeb", [128, 128], BF16, kind="ExternalInput").ap()
    onesr = nc.dram_tensor("onesr", [1, 128], BF16, kind="ExternalInput").ap()
    ET = nc.dram_tensor("ET", [64, 1024], BF16, kind="ExternalInput").ap()
    MGT = nc.dram_tensor("MGT", [64, 1024], BF16, kind="ExternalInput").ap()
    MEQ = nc.dram_tensor("MEQ", [64, 1024], BF16, kind="ExternalInput").ap()
    E2C = nc.dram_tensor("E2C", [128, 2], F32, kind="ExternalInput").ap()
    kT_i = nc.dram_tensor("kT_i", [D, NTOK], BF16, kind="Internal").ap()
    ktm_i = nc.dram_tensor("ktm_i", [NTOK, D], BF16, kind="Internal").ap()
    v_i = nc.dram_tensor("v_i", [NTOK, D], BF16, kind="Internal").ap()
    scr1 = nc.dram_tensor("scr1", [1, D], BF16, kind="Internal").ap()
    rT_o = nc.dram_tensor("rT_o", [D, NTOK], BF16, kind="ExternalOutput").ap()
    g_o = nc.dram_tensor("g_o", [NTOK, D], BF16, kind="ExternalOutput").ap()
    y_o = nc.dram_tensor("y_o", [NTOK, D], F32, kind="ExternalOutput").ap()
    A_o = nc.dram_tensor("A_o", [64, D], F32, kind="ExternalOutput").ap()

    import contextlib
    with contextlib.ExitStack() as stack:
        al = Alloc(nc, stack)
        P = Prog(nc)
        sbf = al.sb
        eyeb_sb = sbf((128, 128), BF16, "r_eyeb")
        onesr_sb = sbf((1, 128), BF16, "r_onesr")
        l_eb = P.dma("sp", lambda eng: eng.dma_start(out=eyeb_sb[:], in_=eyeb[:, :]))
        l_on = P.dma("sp", lambda eng: eng.dma_start(out=onesr_sb[:], in_=onesr[:, :]))
        consts = {}
        for nm, dr_ in (("ET", ET), ("MGT", MGT), ("MEQ", MEQ)):
            tl = sbf((64, 1024), BF16, f"r_c{nm}")
            consts[nm] = tl
            P.dma("sp", lambda eng, tl=tl, dr_=dr_: eng.dma_start(out=tl[:], in_=dr_[:, :]), tag=f"r_lc{nm}")
        e2c = sbf((128, 2), F32, "r_e2c")
        l_e2 = P.dma("sp", lambda eng: eng.dma_start(out=e2c[:], in_=E2C[:, :]))
        consts["E2"] = e2c[:, 0:1].tensor if False else None
        PS2 = al.ps((128, 1024), F32, "r_PS2")
        consts2 = dict(ET=consts["ET"], MGT=consts["MGT"], MEQ=consts["MEQ"], E2=e2c[:, 0:2], TC=e2c[:, 1:2])
        consts2["E2"] = e2c
        class _C: pass
        cl = {"ET": consts["ET"], "MGT": consts["MGT"], "MEQ": consts["MEQ"], "E2": e2c, "TC": e2c}
        cobj = {"ET": consts["ET"], "MGT": consts["MGT"], "MEQ": consts["MEQ"],
                "E2": e2c[:, 0:2], "TC": e2c[:, 1:2]}
        tbl = emit_decay_tables(P, al, decay_row[:, :], faaaa_row[:, :], onesr_sb, cobj, PS2, tag="r")
        for o in P.ops["pe"]:
            o.deps.append(l_eb)
        for o in P.ops["dve"]:
            if o.tag.startswith("r_wk1") or o.tag.startswith("r_wb1"):
                o.deps.append(l_e2)
        xhatT = sbf((128, DC * NTOK), BF16, "r_xhatT")
        ln_ps = al.ps((128, 512), F32, "r_lnps")
        x_sb, x_loads, xT_ops = emit_ln_xhatT(P, al, x, xhatT, eyeb_sb, ln_ps, tag="r_ln", keep_x=False)
        pr = sbf((1, 1024), F32, "r_prev")
        prb = sbf((1, 1024), BF16, "r_prevb")
        prs = sbf((1, 4), F32, "r_prevs")
        lp = P.dma("sp", lambda eng: eng.dma_start(out=pr[:], in_=xprev[:, :]), tag="r_lp")
        p1 = P.op("dve", lambda eng: eng.tensor_reduce(out=prs[:, 0:1], in_=pr[:], axis=AX.X, op=ALU.add), deps=[lp])
        p2 = P.op("dve", lambda eng: eng.tensor_scalar_mul(prs[:, 0:1], prs[:, 0:1], -1.0 / 1024.0), deps=[p1])
        p3 = P.op("act", lambda eng: eng.activation(out=pr[:], in_=pr[:], func=AF.Identity, bias=prs[:, 0:1]), deps=[p2], tag="r_pxc")
        p4 = P.op("act", lambda eng: eng.activation(out=prb[:], in_=pr[:], func=AF.Square, accum_out=prs[:, 1:2]), deps=[p3])
        p5 = P.op("dve", lambda eng: eng.tensor_scalar(out=prs[:, 1:2], in0=prs[:, 1:2], scalar1=1.0 / 1024.0, scalar2=1e-5, op0=ALU.mult, op1=ALU.add), deps=[p4])
        p6 = P.op("act", lambda eng: eng.activation(out=prs[:, 1:2], in_=prs[:, 1:2], func=AF.Sqrt), deps=[p5])
        p7 = P.op("dve", lambda eng: eng.reciprocal(out=prs[:, 1:2], in_=prs[:, 1:2]), deps=[p6])
        p8 = P.op("dve", lambda eng: eng.tensor_scalar(out=prb[:], in0=pr[:], scalar1=prs[:, 1:2], scalar2=None, op0=ALU.mult), deps=[p7, p4], tag="r_pxh")
        b1 = P.dma("sp", lambda eng: eng.dma_start(out=scr1[:, :], in_=prb[:]), deps=[p8], tag="r_pso")
        xpT = sbf((128, 8), BF16, "r_xpT")
        b2 = P.dma("sp", lambda eng: eng.dma_start(out=xpT[:], in_=scr1.rearrange("a (c p) -> p (a c)", p=128), allow_slow_non_contiguous=True), deps=[b1], tag="r_psi")
        dT = sbf((128, DC * NTOK), BF16, "r_dT")
        d_ops = []
        for c in range(DC):
            o1 = P.op("dve", lambda eng, c=c: eng.tensor_tensor(out=dT[:, c * NTOK + 1:(c + 1) * NTOK],
                                                                in0=xhatT[:, c * NTOK:(c + 1) * NTOK - 1],
                                                                in1=xhatT[:, c * NTOK + 1:(c + 1) * NTOK], op=ALU.subtract),
                      deps=[xT_ops[-1]] + xT_ops, tag=f"r_d{c}")
            o2 = P.op("dve", lambda eng, c=c: eng.tensor_tensor(out=dT[:, c * NTOK:c * NTOK + 1], in0=xpT[:, c:c + 1], in1=xhatT[:, c * NTOK:c * NTOK + 1], op=ALU.subtract), deps=[b2, xT_ops[0], o1], tag=f"r_d0{c}")
            d_ops.append(o2)
        MIX = sbf((128, DC * NTOK), BF16, "r_MIX")
        WB = sbf((128, 8192), BF16, "r_WB")
        OST = [sbf((128, 2048), BF16, f"r_ost{i}") for i in range(2)]
        pbank = Banks(al, 2, name="r_pb")
        maac = {}
        maal = {}
        for m in "kvrg":
            maac[m] = sbf((128, 8), F32, f"r_maa{m}")
            maal[m] = P.dma("sp", lambda eng, m=m: eng.dma_start(out=maac[m][:], in_=maa[m][:, :]), tag=f"r_lmaa{m}")
        store_ops = []
        mix_ops_prev = []

        def emit_mix(m, deps_extra):
            ops = []
            for c in range(DC):
                for tch in range(4):
                    sl = slice(c * NTOK + tch * 512, c * NTOK + (tch + 1) * 512)
                    o = P.op("dve", lambda eng, sl=sl, m=m, c=c: eng.scalar_tensor_tensor(
                        out=MIX[:, sl], in0=dT[:, sl], scalar=maac[m][:, c:c + 1], in1=xhatT[:, sl],
                        op0=ALU.mult, op1=ALU.add), deps=d_ops + [maal[m]] + deps_extra, tag=f"r_mix{m}{c}_{tch}")
                    ops.append(o)
            return ops

        def emit_proj(wname, orient, dram_out, deps_extra, tagp):
            lw = P.dma("pool", lambda eng: eng.dma_start(out=WB[:].rearrange("p (c o) -> p c o", c=DC), in_=W_T[wname].rearrange("(c p) o -> p c o", p=128)), deps=deps_extra, tag=f"r_lw{tagp}")
            evs_all = []
            mms_all = []
            if orient == "fT":
                for co in range(DC):
                    ost = OST[co % 2]
                    ev4 = []
                    for nch in range(4):
                        bi, ps, war = pbank.get()
                        def omm(eng, co=co, nch=nch, ps=ps):
                            inst = None
                            for c in range(DC):
                                inst = eng.matmul(ps[:, :], WB[:, c * 1024 + co * 128:c * 1024 + (co + 1) * 128],
                                                  MIX[:, c * NTOK + nch * 512:c * NTOK + (nch + 1) * 512],
                                                  start=(c == 0), stop=(c == DC - 1))
                            return inst
                        mm = P.op("pe", omm, deps=[lw, war, store_ops[-1] if store_ops else None], tag=f"r_{tagp}mm{co}_{nch}")
                        mms_all.append(mm)
                        ev = P.op("dve", lambda eng, ps=ps, ost=ost, nch=nch: eng.tensor_copy(out=ost[:, nch * 512:(nch + 1) * 512], in_=ps[:, :]), deps=[mm], tag=f"r_{tagp}ev{co}_{nch}")
                        pbank.done(bi, ev)
                        ev4.append(ev)
                    so = P.dma("sp", lambda eng, co=co, ost=ost: eng.dma_start(out=dram_out.rearrange("(c p) n -> p c n", p=128)[:, co, :], in_=ost[:]), deps=ev4, tag=f"r_{tagp}st{co}")
                    store_ops.append(so)
            else:
                act_fn = AF.Gelu if tagp == "g" else AF.Copy
                for s2 in range(NSTRIP // 2):
                    ost = OST[s2 % 2]
                    ev4 = []
                    for sj in range(2):
                        s = s2 * 2 + sj
                        for nch in range(2):
                            bi, ps, war = pbank.get()
                            def vmm(eng, s=s, nch=nch, ps=ps):
                                inst = None
                                for c in range(DC):
                                    inst = eng.matmul(ps[:, :], MIX[:, c * NTOK + s * 128:c * NTOK + (s + 1) * 128],
                                                      WB[:, c * 1024 + nch * 512:c * 1024 + (nch + 1) * 512],
                                                      start=(c == 0), stop=(c == DC - 1))
                                return inst
                            mm = P.op("pe", vmm, deps=[lw, war, store_ops[-1] if store_ops else None], tag=f"r_{tagp}m{s}_{nch}")
                            mms_all.append(mm)
                            ev = P.op("act", lambda eng, ps=ps, ost=ost, sj=sj, nch=nch, act_fn=act_fn: eng.activation(out=ost[:, sj * 1024 + nch * 512:sj * 1024 + (nch + 1) * 512], in_=ps[:, :], func=act_fn), deps=[mm], tag=f"r_{tagp}e{s}_{nch}")
                            pbank.done(bi, ev)
                            ev4.append(ev)
                    so = P.dma("sp", lambda eng, s2=s2, ost=ost: eng.dma_start(out=dram_out.rearrange("(s p) d -> p s d", p=128)[:, s2 * 2:s2 * 2 + 2, :], in_=ost[:].rearrange("p (a b) -> p a b", a=2)), deps=ev4, tag=f"r_{tagp}vst{s2}")
                    store_ops.append(so)
            return mms_all

        mix_ops = emit_mix("k", [])
        mm1 = emit_proj("k", "fT", kT_i, mix_ops, "kf")
        mm2 = emit_proj("k", "tm", ktm_i, mix_ops, "kt")
        prev = mm1 + mm2
        mix_ops = emit_mix("v", prev)
        mmv = emit_proj("v", "tm", v_i, mix_ops, "v")
        mix_ops = emit_mix("r", mmv)
        mmr = emit_proj("r", "fT", rT_o, mix_ops, "r")
        mix_ops = emit_mix("g", mmr)
        mmg = emit_proj("g", "tm", g_o, mix_ops, "g")
        KTC = [sbf((64, 1024), BF16, f"r_ktc{i}") for i in range(2)]
        RTC = [sbf((64, 1024), BF16, f"r_rtc{i}") for i in range(2)]
        KMC = [sbf((64, 1024), BF16, f"r_kmc{i}") for i in range(2)]
        VCC = [sbf((64, 1024), BF16, f"r_vcc{i}") for i in range(2)]
        ATW = OST[0][0:64, 0:1024]
        VCS = OST[0][0:64, 1024:2048]
        ost1_f32 = OST[1].ap().bitcast(F32)
        T1 = ost1_f32[0:64, 0:1024]
        YST = ost1_f32[64:128, 0:1024]
        ST = sbf((64, 1024), F32, "r_state")
        STB_t = sbf((64, 1024), BF16, "r_stb")
        STB = STB_t[:, :]
        z0a = P.op("dve", lambda eng: eng.memset(ST[:], 0.0), deps=[store_ops[-1]], tag="r_st0a")
        z0 = P.op("dve", lambda eng: eng.memset(STB, 0.0), deps=[z0a], tag="r_st0")
        attp = [PS2[:, 0:512], PS2[:, 512:1024]]
        rs_extra = al.ps((128, 512), F32, "r_rsx")
        rsb = [ln_ps, rs_extra]
        last_store = store_ops[-1]
        chunk_loads_prev = [None, None]
        prev_scan = {"att_mm": None, "atw_ev": None, "vcs_ev": None, "kv_mm": None,
                     "yc_mm": None, "y_ev": None, "st_upd": z0, "rs_mm": None, "t1": None}
        tbl_done = tbl["done"]
        y_st_ops = []
        for cch in range(32):
            rot = cch % 2
            ldeps = [chunk_loads_prev[rot], last_store] + (tbl_done if cch == 0 else [])
            lk1 = P.dma("sp", lambda eng, cch=cch, rot=rot: eng.dma_start(out=KTC[rot][:].rearrange("p (h t) -> p h t", h=16), in_=kT_i.rearrange("(h n) tok -> n h tok", n=64)[:, :, cch * 64:(cch + 1) * 64]), deps=ldeps, tag=f"r_slk{cch}")
            lr1 = P.dma("sp", lambda eng, cch=cch, rot=rot: eng.dma_start(out=RTC[rot][:].rearrange("p (h t) -> p h t", h=16), in_=rT_o.rearrange("(h n) tok -> n h tok", n=64)[:, :, cch * 64:(cch + 1) * 64]), deps=ldeps, tag=f"r_slr{cch}")
            lk2 = P.dma("act", lambda eng, cch=cch, rot=rot: eng.dma_start(out=KMC[rot][:], in_=ktm_i[cch * 64:(cch + 1) * 64, :]), deps=ldeps, tag=f"r_slkm{cch}")
            lv2 = P.dma("act", lambda eng, cch=cch, rot=rot: eng.dma_start(out=VCC[rot][:], in_=v_i[cch * 64:(cch + 1) * 64, :]), deps=ldeps, tag=f"r_slv{cch}")
            def amm(eng, rot=rot):
                inst = None
                for h in range(16):
                    inst = eng.matmul(attp[h // 8][0:64, (h % 8) * 64:(h % 8 + 1) * 64],
                                      KTC[rot][0:64, h * 64:(h + 1) * 64], RTC[rot][0:64, h * 64:(h + 1) * 64],
                                      start=True, stop=True)
                return inst
            att_mm = P.op("pe", amm, deps=[lk1, lr1, prev_scan["atw_ev"], prev_scan["st_upd"]] + (tbl_done if cch == 0 else []), tag=f"r_att{cch}")
            ae1 = P.op("dve", lambda eng: eng.tensor_tensor(out=ATW[:, 0:512], in0=attp[0][0:64, :], in1=tbl["WtT"][0:64, 0:512], op=ALU.mult), deps=[att_mm, prev_scan["yc_mm"]], tag=f"r_ae1_{cch}")
            ae2 = P.op("dve", lambda eng: eng.tensor_tensor(out=ATW[:, 512:1024], in0=attp[1][0:64, :], in1=tbl["WtT"][0:64, 512:1024], op=ALU.mult), deps=[att_mm, prev_scan["yc_mm"]], tag=f"r_ae2_{cch}")
            vcs_ev = P.op("dve", lambda eng, rot=rot: eng.tensor_tensor(out=VCS[:, :], in0=VCC[rot][:], in1=tbl["wk"][0:64, :], op=ALU.mult), deps=[lv2, prev_scan["kv_mm"]] + (tbl_done if cch == 0 else []), tag=f"r_vcs{cch}")
            rs_mm = None
            def rsmm(eng, rot=rot):
                inst = None
                for h in range(16):
                    inst = eng.matmul(rsb[h // 8][0:64, (h % 8) * 64:(h % 8 + 1) * 64],
                                      RTC[rot][0:64, h * 64:(h + 1) * 64], STB[:, h * 64:(h + 1) * 64],
                                      start=True, stop=True)
                return inst
            rs_mm = P.op("pe", rsmm, deps=[lr1, prev_scan["st_upd"], prev_scan["t1"]], tag=f"r_rs{cch}")
            t1a = P.op("dve", lambda eng: eng.tensor_tensor(out=T1[:, 0:512], in0=rsb[0][0:64, :], in1=tbl["wb"][0:64, 0:512], op=ALU.mult), deps=[rs_mm], tag=f"r_t1a{cch}")
            t1b = P.op("dve", lambda eng: eng.tensor_tensor(out=T1[:, 512:1024], in0=rsb[1][0:64, :], in1=tbl["wb"][0:64, 512:1024], op=ALU.mult), deps=[rs_mm], tag=f"r_t1b{cch}")
            bi1, ps1, war1 = pbank.get()
            bi2, ps2, war2 = pbank.get()
            def ycmm(eng, rot=rot, ps1=ps1, ps2=ps2):
                inst = None
                for h in range(16):
                    pj = (ps1 if h < 8 else ps2)
                    inst = eng.matmul(pj[0:64, (h % 8) * 64:(h % 8 + 1) * 64],
                                      ATW[:, h * 64:(h + 1) * 64], VCC[rot][0:64, h * 64:(h + 1) * 64],
                                      start=True, stop=True)
                return inst
            yc_mm = P.op("pe", ycmm, deps=[ae1, ae2, lv2, war1, war2], tag=f"r_yc{cch}")
            y1 = P.op("dve", lambda eng, ps1=ps1: eng.tensor_tensor(out=YST[:, 0:512], in0=ps1[0:64, :], in1=T1[:, 0:512], op=ALU.add), deps=[yc_mm, t1a, y_st_ops[-1] if y_st_ops else None], tag=f"r_y1{cch}")
            y2 = P.op("dve", lambda eng, ps2=ps2: eng.tensor_tensor(out=YST[:, 512:1024], in0=ps2[0:64, :], in1=T1[:, 512:1024], op=ALU.add), deps=[yc_mm, t1b, y_st_ops[-1] if y_st_ops else None], tag=f"r_y2{cch}")
            pbank.done(bi1, y1)
            pbank.done(bi2, y2)
            ys = P.dma("sp", lambda eng, cch=cch: eng.dma_start(out=y_o[cch * 64:(cch + 1) * 64, :], in_=YST[:, :]), deps=[y1, y2], tag=f"r_yso{cch}")
            y_st_ops.append(ys)
            def kvmm(eng, rot=rot):
                inst = None
                for h in range(16):
                    inst = eng.matmul(attp[h // 8][0:64, (h % 8) * 64:(h % 8 + 1) * 64],
                                      KMC[rot][0:64, h * 64:(h + 1) * 64], VCS[:, h * 64:(h + 1) * 64],
                                      start=True, stop=True)
                return inst
            kv_mm = P.op("pe", kvmm, deps=[ae1, ae2, vcs_ev, lk2, rs_mm], tag=f"r_kv{cch}")
            su1 = P.op("dve", lambda eng: eng.tensor_tensor(out=ST[:], in0=ST[:], in1=tbl["ws"][0:64, :], op=ALU.mult), deps=[rs_mm] + (tbl_done if cch == 0 else []), tag=f"r_su1_{cch}")
            su2 = P.op("dve", lambda eng: eng.tensor_tensor(out=ST[0:64, 0:512], in0=attp[0][0:64, :], in1=ST[0:64, 0:512], op=ALU.add), deps=[su1, kv_mm], tag=f"r_su2_{cch}")
            su3 = P.op("dve", lambda eng: eng.tensor_tensor(out=ST[0:64, 512:1024], in0=attp[1][0:64, :], in1=ST[0:64, 512:1024], op=ALU.add), deps=[su2, kv_mm], tag=f"r_su3_{cch}")
            su4 = P.op("act", lambda eng: eng.activation(out=STB, in_=ST[:], func=AF.Copy), deps=[su3, rs_mm], tag=f"r_su4_{cch}")
            chunk_loads_prev[rot] = kv_mm
            prev_scan = {"atw_ev": ae2, "kv_mm": kv_mm, "yc_mm": yc_mm, "st_upd": su4, "t1": t1b}
        sA = P.dma("sp", lambda eng: eng.dma_start(out=A_o[:, :], in_=ST[0:64, :]), deps=[prev_scan["st_upd"]], tag="r_sA")
        fin = P.op("pool", lambda eng: eng.nop(), deps=[sA, y_st_ops[-1]], tag="fin")
        fin.need = False
        P.emit()
    return nc

def build_s4b():
    nc = bass.Bass("TRN2", target_bir_lowering=False)
    x = nc.dram_tensor("x", [NTOK, D], F32, kind="ExternalInput").ap()
    y_l = nc.dram_tensor("y_l", [NTOK, D], F32, kind="ExternalInput").ap()
    rT = nc.dram_tensor("rT", [D, NTOK], BF16, kind="ExternalInput").ap()
    g = nc.dram_tensor("g", [NTOK, D], BF16, kind="ExternalInput").ap()
    As = [nc.dram_tensor(f"A{i}", [64, D], F32, kind="ExternalInput").ap() for i in range(4)]
    steps = nc.dram_tensor("steps", [64, 4], F32, kind="ExternalInput").ap()
    decay_row = nc.dram_tensor("decay_row", [1, 16], F32, kind="ExternalInput").ap()
    faaaa_row = nc.dram_tensor("faaaa_row", [1, 16], F32, kind="ExternalInput").ap()
    ln_w = nc.dram_tensor("ln_w", [1, D], F32, kind="ExternalInput").ap()
    ln_b = nc.dram_tensor("ln_b", [1, D], F32, kind="ExternalInput").ap()
    WoT = nc.dram_tensor("WoT", [D, D], F32, kind="ExternalInput").ap()
    eyeb = nc.dram_tensor("eyeb", [128, 128], BF16, kind="ExternalInput").ap()
    onesr = nc.dram_tensor("onesr", [1, 128], BF16, kind="ExternalInput").ap()
    ET = nc.dram_tensor("ET", [64, 1024], BF16, kind="ExternalInput").ap()
    MGT = nc.dram_tensor("MGT", [64, 1024], BF16, kind="ExternalInput").ap()
    MEQ = nc.dram_tensor("MEQ", [64, 1024], BF16, kind="ExternalInput").ap()
    E2C = nc.dram_tensor("E2C", [128, 2], F32, kind="ExternalInput").ap()
    x_o = nc.dram_tensor("x_o", [NTOK, D], F32, kind="ExternalOutput").ap()
    EPS = 1e-5 * 64

    import contextlib
    with contextlib.ExitStack() as stack:
        al = Alloc(nc, stack)
        P = Prog(nc)
        sbf = al.sb
        eyeb_sb = sbf((128, 128), BF16, "b_eyeb")
        onesr_sb = sbf((1, 128), BF16, "b_onesr")
        l_eb = P.dma("sp", lambda eng: eng.dma_start(out=eyeb_sb[:], in_=eyeb[:, :]))
        l_on = P.dma("sp", lambda eng: eng.dma_start(out=onesr_sb[:], in_=onesr[:, :]))
        consts = {}
        for nm, dr_ in (("ET", ET), ("MGT", MGT), ("MEQ", MEQ)):
            tl = sbf((64, 1024), BF16, f"b_c{nm}")
            consts[nm] = tl
            P.dma("sp", lambda eng, tl=tl, dr_=dr_: eng.dma_start(out=tl[:], in_=dr_[:, :]), tag=f"b_lc{nm}")
        e2c = sbf((128, 2), F32, "b_e2c")
        l_e2 = P.dma("sp", lambda eng: eng.dma_start(out=e2c[:], in_=E2C[:, :]))
        PS2 = al.ps((128, 1024), F32, "b_PS2")
        cobj = {"ET": consts["ET"], "MGT": consts["MGT"], "MEQ": consts["MEQ"],
                "E2": e2c[:, 0:2], "TC": e2c[:, 1:2]}
        tbl = emit_decay_tables(P, al, decay_row[:, :], faaaa_row[:, :], onesr_sb, cobj, PS2, tag="b")
        for o in P.ops["pe"]:
            o.deps.append(l_eb)
        for o in P.ops["dve"]:
            if o.tag.startswith("b_wk1") or o.tag.startswith("b_wb1"):
                o.deps.append(l_e2)
        tbl_done = tbl["done"]
        nega = tbl["nega128"]
        lnr = sbf((1, 2048), BF16, "b_lnr")
        llw = P.dma("pool", lambda eng: eng.dma_start(out=lnr[0:1, 0:1024], in_=ln_w[:, :]))
        llb = P.dma("pool", lambda eng: eng.dma_start(out=lnr[0:1, 1024:2048], in_=ln_b[:, :]))
        lnwb = sbf((128, 1024), F32, "b_lnwb")
        lnbb = sbf((128, 1024), F32, "b_lnbb")
        bc1 = P.op("pe", lambda eng: [eng.matmul(PS2[:, j * 512:(j + 1) * 512], onesr_sb[0:1, :], lnr[0:1, j * 512:(j + 1) * 512], start=True, stop=True) for j in range(2)][-1], deps=[llw, l_on] + tbl_done, tag="b_bc1")
        bv1 = P.op("dve", lambda eng: eng.tensor_copy(out=lnwb[:], in_=PS2[:, 0:1024]), deps=[bc1], tag="b_bv1")
        bc2 = P.op("pe", lambda eng: [eng.matmul(PS2[:, j * 512:(j + 1) * 512], onesr_sb[0:1, :], lnr[0:1, 1024 + j * 512:1024 + (j + 1) * 512], start=True, stop=True) for j in range(2)][-1], deps=[llb, l_on, bv1], tag="b_bc2")
        bv2 = P.op("dve", lambda eng: eng.tensor_copy(out=lnbb[:], in_=PS2[:, 0:1024]), deps=[bc2], tag="b_bv2")
        A_sb = sbf((64, 1024), F32, "b_A")
        SIN = sbf((64, 1024), F32, "b_sin")
        SINB = sbf((64, 1024), BF16, "b_sinb")
        stp = sbf((64, 4), F32, "b_steps")
        scl = sbf((64, 1024), BF16, "b_scl")
        l_st = P.dma("sp", lambda eng: eng.dma_start(out=stp[:], in_=steps[:, :]))
        zi = P.op("dve", lambda eng: eng.memset(SIN[:], 0.0), deps=[], tag="b_zi")
        prev = zi
        for i in range(4):
            la = P.dma("sp", lambda eng, i=i: eng.dma_start(out=A_sb[:], in_=As[i][:, :]), deps=[prev], tag=f"b_lA{i}")
            s1 = P.op("dve", lambda eng, i=i: eng.tensor_scalar(out=scl[:], in0=nega[0:64, :], scalar1=stp[:, i:i + 1], scalar2=None, op0=ALU.mult), deps=[l_st, prev] + tbl_done, tag=f"b_ss{i}")
            s2 = P.op("act", lambda eng: eng.activation(out=scl[:], in_=scl[:], func=AF.Exp), deps=[s1], tag=f"b_se{i}")
            s3 = P.op("dve", lambda eng: eng.tensor_tensor(out=scl[:], in0=scl[:], in1=A_sb[:], op=ALU.mult), deps=[s2, la], tag=f"b_sm{i}")
            s4 = P.op("dve", lambda eng: eng.tensor_tensor(out=SIN[:], in0=SIN[:], in1=scl[:], op=ALU.add), deps=[s3], tag=f"b_sa{i}")
            prev = s4
        sc = P.op("act", lambda eng: eng.activation(out=SINB[:], in_=SIN[:], func=AF.Copy), deps=[prev], tag="b_sinb")
        RTC = [sbf((64, 1024), BF16, f"b_rtc{i}") for i in range(2)]
        YC = sbf((64, 1024), F32, "b_ycf")
        GC = sbf((64, 1024), BF16, "b_gc")
        WBC = sbf((64, 1024), BF16, "b_wbc")
        SQ = sbf((64, 1024), BF16, "b_sq")
        STT = sbf((64, 64), F32, "b_stt")
        ygT = sbf((128, 16384), BF16, "b_ygT")
        pbank = Banks(al, 2, name="b_pb")
        tpps = al.ps((128, 512), F32, "b_tpps")
        tpps_bf = tpps.ap().bitcast(BF16)
        chunk_prev = [None, None]
        prev_tp = None
        prev_chunk_dve = None
        ygT_ops = []
        for cch in range(32):
            rot = cch % 2
            lr1 = P.dma("sp", lambda eng, cch=cch, rot=rot: eng.dma_start(out=RTC[rot][:].rearrange("p (h t) -> p h t", h=16), in_=rT.rearrange("(h n) tok -> n h tok", n=64)[:, :, cch * 64:(cch + 1) * 64]), deps=[chunk_prev[rot]], tag=f"b_lr{cch}")
            ly = P.dma("act", lambda eng, cch=cch: eng.dma_start(out=YC[:], in_=y_l[cch * 64:(cch + 1) * 64, :]), deps=[prev_chunk_dve], tag=f"b_ly{cch}")
            lg = P.dma("act", lambda eng, cch=cch: eng.dma_start(out=GC[:], in_=g[cch * 64:(cch + 1) * 64, :]), deps=[prev_chunk_dve], tag=f"b_lg{cch}")
            bi1, ps1, war1 = pbank.get()
            bi2, ps2, war2 = pbank.get()
            def rsmm(eng, rot=rot, ps1=ps1, ps2=ps2):
                inst = None
                for h in range(16):
                    pj = ps1 if h < 8 else ps2
                    inst = eng.matmul(pj[0:64, (h % 8) * 64:(h % 8 + 1) * 64],
                                      RTC[rot][0:64, h * 64:(h + 1) * 64], SINB[:, h * 64:(h + 1) * 64],
                                      start=True, stop=True)
                return inst
            rs_mm = P.op("pe", rsmm, deps=[lr1, sc, war1, war2], tag=f"b_rs{cch}")
            w1 = P.op("dve", lambda eng, cch=cch: eng.tensor_scalar(out=STT[:, 0:1], in0=e2c[0:64, 1:2], scalar1=float(64 * cch), scalar2=None, op0=ALU.add), deps=[l_e2, prev_chunk_dve], tag=f"b_w1{cch}")
            w2 = P.op("dve", lambda eng: eng.tensor_scalar(out=WBC[:], in0=nega[0:64, :], scalar1=STT[:, 0:1], scalar2=None, op0=ALU.mult), deps=[w1] + tbl_done, tag=f"b_w2{cch}")
            w3 = P.op("act", lambda eng: eng.activation(out=WBC[:], in_=WBC[:], func=AF.Exp), deps=[w2], tag=f"b_w3{cch}")
            y1 = P.op("dve", lambda eng, ps1=ps1: eng.tensor_tensor(out=SQ[:, 0:512], in0=ps1[0:64, :], in1=WBC[:, 0:512], op=ALU.mult), deps=[rs_mm, w3], tag=f"b_yc1{cch}")
            y2 = P.op("dve", lambda eng, ps2=ps2: eng.tensor_tensor(out=SQ[:, 512:1024], in0=ps2[0:64, :], in1=WBC[:, 512:1024], op=ALU.mult), deps=[rs_mm, w3], tag=f"b_yc2{cch}")
            pbank.done(bi1, y1)
            pbank.done(bi2, y2)
            y3 = P.op("dve", lambda eng: eng.tensor_tensor(out=YC[:], in0=YC[:], in1=SQ[:, 0:1024], op=ALU.add), deps=[y1, y2, ly], tag=f"b_ya{cch}")
            yv = YC[:].rearrange("p (h n) -> p h n", h=16)
            g1 = P.op("dve", lambda eng: eng.tensor_reduce(out=STT[:, 16:32], in_=yv, axis=AX.X, op=ALU.add), deps=[y3], tag=f"b_gm{cch}")
            g2 = P.op("dve", lambda eng: eng.tensor_scalar_mul(STT[:, 16:32], STT[:, 16:32], 1.0 / 64.0), deps=[g1])
            g3 = P.op("dve", lambda eng: eng.tensor_tensor(out=yv, in0=yv, in1=STT[:, 16:32].unsqueeze(2).broadcast_to([64, 16, 64]), op=ALU.subtract), deps=[g2], tag=f"b_gc{cch}")
            g4 = P.op("act", lambda eng: eng.activation(out=SQ[:], in_=YC[:], func=AF.Square), deps=[g3], tag=f"b_gsq{cch}")
            g5 = P.op("dve", lambda eng: eng.tensor_reduce(out=STT[:, 32:48], in_=SQ[:].rearrange("p (h n) -> p h n", h=16), axis=AX.X, op=ALU.add), deps=[g4], tag=f"b_gss{cch}")
            g6 = P.op("dve", lambda eng: eng.tensor_scalar(out=STT[:, 32:48], in0=STT[:, 32:48], scalar1=1.0 / 64.0, scalar2=EPS, op0=ALU.mult, op1=ALU.add), deps=[g5])
            g7 = P.op("act", lambda eng: eng.activation(out=STT[:, 32:48], in_=STT[:, 32:48], func=AF.Sqrt), deps=[g6])
            g8 = P.op("dve", lambda eng: eng.reciprocal(out=STT[:, 32:48], in_=STT[:, 32:48]), deps=[g7], tag=f"b_grs{cch}")
            g9 = P.op("dve", lambda eng: eng.tensor_tensor(out=yv, in0=yv, in1=STT[:, 32:48].unsqueeze(2).broadcast_to([64, 16, 64]), op=ALU.mult), deps=[g8], tag=f"b_gn{cch}")
            g10 = P.op("dve", lambda eng: eng.tensor_tensor(out=YC[:], in0=YC[:], in1=lnwb[0:64, :], op=ALU.mult), deps=[g9, bv1], tag=f"b_glw{cch}")
            g11 = P.op("dve", lambda eng: eng.tensor_tensor(out=YC[:], in0=YC[:], in1=lnbb[0:64, :], op=ALU.add), deps=[g10, bv2], tag=f"b_glb{cch}")
            g12 = P.op("dve", lambda eng: eng.tensor_tensor(out=GC[:], in0=YC[:], in1=GC[:], op=ALU.mult), deps=[g11, lg], tag=f"b_gg{cch}")
            prev_chunk_dve = g12
            def tpf(eng):
                inst = None
                for h in range(16):
                    r0 = (h % 2) * 64
                    inst = eng.transpose(tpps_bf[r0:r0 + 64, (h // 2) * 64:(h // 2) * 64 + 64],
                                         GC[0:64, h * 64:(h + 1) * 64], eyeb_sb[0:64, 0:64])
                return inst
            tp = P.op("pe", tpf, deps=[g12, l_eb, prev_tp], tag=f"b_tp{cch}")
            def tev(eng, cch=cch):
                return eng.tensor_copy(out=ygT[:].rearrange("p (c n) -> p c n", c=DC)[:, :, cch * 64:(cch + 1) * 64],
                                       in_=tpps_bf[:, 0:512].rearrange("p (c n) -> p c n", c=DC))
            te = P.op("dve", tev, deps=[tp], tag=f"b_te{cch}")
            prev_tp = te
            ygT_ops.append(te)
            chunk_prev[rot] = rs_mm
        WOB = sbf((128, 8192), BF16, "b_wob")
        lwo = P.dma("pool", lambda eng: eng.dma_start(out=WOB[:].rearrange("p (c o) -> p c o", c=DC), in_=WoT.rearrange("(c p) o -> p c o", p=128)), tag="b_lwo")
        XS = [sbf((128, 1024), F32, f"b_xs{i}") for i in range(2)]
        out_ops = []
        for s in range(NSTRIP):
            xs = XS[s % 2]
            lx = P.dma("sp", lambda eng, s=s, xs=xs: eng.dma_start(out=xs[:], in_=x.rearrange("(s p) d -> p s d", p=128)[:, s, :]), deps=[out_ops[-2] if len(out_ops) >= 2 else None], tag=f"b_lx{s}")
            evs = []
            for nch in range(2):
                bi, ps, war = pbank.get()
                def omm(eng, s=s, nch=nch, ps=ps):
                    inst = None
                    for c in range(DC):
                        inst = eng.matmul(ps[:, :], ygT[:, c * NTOK + s * 128:c * NTOK + (s + 1) * 128],
                                          WOB[:, c * 1024 + nch * 512:c * 1024 + (nch + 1) * 512],
                                          start=(c == 0), stop=(c == DC - 1))
                    return inst
                mm = P.op("pe", omm, deps=ygT_ops + [lwo, war], tag=f"b_omm{s}_{nch}")
                ev = P.op("dve", lambda eng, xs=xs, ps=ps, nch=nch: eng.tensor_tensor(out=xs[:, nch * 512:(nch + 1) * 512], in0=ps[:, :], in1=xs[:, nch * 512:(nch + 1) * 512], op=ALU.add), deps=[mm, lx], tag=f"b_oev{s}_{nch}")
                pbank.done(bi, ev)
                evs.append(ev)
            so = P.dma("sp", lambda eng, s=s, xs=xs: eng.dma_start(out=x_o.rearrange("(s p) d -> p s d", p=128)[:, s, :], in_=xs[:]), deps=evs, tag=f"b_so{s}")
            out_ops.append(so)
        fin = P.op("pool", lambda eng: eng.nop(), deps=out_ops[-2:], tag="fin")
        fin.need = False
        P.emit()
    return nc


def build_s2a():
    nc = bass.Bass("TRN2", target_bir_lowering=False)
    x = nc.dram_tensor("x", [NTOK, D], F32, kind="ExternalInput").ap()
    Wq = nc.dram_tensor("Wq", [D, D], F32, kind="ExternalInput").ap()
    Wk = nc.dram_tensor("Wk", [D, D], F32, kind="ExternalInput").ap()
    Wv = nc.dram_tensor("Wv", [D, D], F32, kind="ExternalInput").ap()
    Wg = nc.dram_tensor("Wg", [D, 3], F32, kind="ExternalInput").ap()
    bq = nc.dram_tensor("bq", [128, 8], F32, kind="ExternalInput").ap()
    bk = nc.dram_tensor("bk", [128, 8], F32, kind="ExternalInput").ap()
    bv = nc.dram_tensor("bv", [1, D], F32, kind="ExternalInput").ap()
    bg = nc.dram_tensor("bg", [1, 3], F32, kind="ExternalInput").ap()
    eyeb = nc.dram_tensor("eyeb", [128, 128], BF16, kind="ExternalInput").ap()
    onesr = nc.dram_tensor("onesr", [1, 128], BF16, kind="ExternalInput").ap()
    qT_o = nc.dram_tensor("qT_o", [D, NTOK], BF16, kind="ExternalOutput").ap()
    kT_o = nc.dram_tensor("kT_o", [D, NTOK], BF16, kind="ExternalOutput").ap()
    ktm_o = nc.dram_tensor("ktm_o", [NTOK, D], BF16, kind="ExternalOutput").ap()
    v_o = nc.dram_tensor("v_o", [NTOK, D], BF16, kind="ExternalOutput").ap()
    ckT_o = nc.dram_tensor("ckT_o", [D, 64], BF16, kind="ExternalOutput").ap()
    cv_o = nc.dram_tensor("cv_o", [64, D], BF16, kind="ExternalOutput").ap()
    g_o = nc.dram_tensor("g_o", [NTOK, 4], F32, kind="ExternalOutput").ap()

    import contextlib
    with contextlib.ExitStack() as stack:
        al = Alloc(nc, stack)
        P = Prog(nc)
        sbf = al.sb
        eyeb_sb = sbf((128, 128), BF16, "n_eyeb")
        onesr_sb = sbf((1, 128), BF16, "n_onesr")
        l_eb = P.dma("sp", lambda eng: eng.dma_start(out=eyeb_sb[:], in_=eyeb[:, :]))
        l_on = P.dma("sp", lambda eng: eng.dma_start(out=onesr_sb[:], in_=onesr[:, :]))
        xhatT = sbf((128, DC * NTOK), BF16, "n_xhatT")
        ln_ps = al.ps((128, 512), F32, "n_lnps")
        x_sb, x_loads, xT_ops = emit_ln_xhatT(P, al, x, xhatT, eyeb_sb, ln_ps, tag="n_ln", keep_x=False)
        for o in P.ops["pe"]:
            o.deps.append(l_eb)
        WB = sbf((128, 8192), BF16, "n_WB")
        OST = [sbf((128, 2048), BF16, f"n_ost{i}") for i in range(2)]
        BQ = sbf((128, 8), F32, "n_bq")
        BVR = sbf((1, D), BF16, "n_bvr")
        pbank = Banks(al, 4, name="n_pb")
        store_ops = []
        pool_evs = {"k": [], "v": []}
        CKT = sbf((128, 512), BF16, "n_ckT")
        CVT = sbf((128, 512), BF16, "n_cvT")
        PSC = sbf((128, 64), F32, "n_psc")
        prev_w = []

        def proj(wdram, bias_mode, bias_dram, orient, dram_o, tagp, pool_to=None):
            lw = P.dma("pool", lambda eng: eng.dma_start(out=WB[:].rearrange("p (c o) -> p c o", c=DC), in_=wdram.rearrange("(c p) o -> p c o", p=128)), deps=prev_w, tag=f"n_lw{tagp}")
            lb = None
            if bias_mode == "col":
                lb = P.dma("sp", lambda eng: eng.dma_start(out=BQ[:], in_=bias_dram[:, :]), deps=prev_w, tag=f"n_lb{tagp}")
            elif bias_mode == "row":
                lb = P.dma("pool", lambda eng: eng.dma_start(out=BVR[:], in_=bias_dram[:, :]), deps=prev_w, tag=f"n_lb{tagp}")
            readers = []
            if orient == "fT":
                for co in range(DC):
                    ost = OST[co % 2]
                    ev4 = []
                    for nch in range(4):
                        bi, ps, war = pbank.get()
                        def omm(eng, co=co, nch=nch, ps=ps):
                            inst = None
                            for c in range(DC):
                                inst = eng.matmul(ps[:, :], WB[:, c * 1024 + co * 128:c * 1024 + (co + 1) * 128],
                                                  xhatT[:, c * NTOK + nch * 512:c * NTOK + (nch + 1) * 512],
                                                  start=(c == 0), stop=(c == DC - 1))
                            return inst
                        mm = P.op("pe", omm, deps=xT_ops + [lw, war, store_ops[-1] if store_ops else None], tag=f"n_{tagp}mm{co}_{nch}")
                        readers.append(mm)
                        ev = P.op("act", lambda eng, ps=ps, ost=ost, nch=nch, co=co: eng.activation(out=ost[:, nch * 512:(nch + 1) * 512], in_=ps[:, :], func=AF.Identity, bias=BQ[:, co:co + 1]), deps=[mm, lb], tag=f"n_{tagp}ev{co}_{nch}")
                        pbank.done(bi, ev)
                        ev4.append(ev)
                    if pool_to is not None:
                        pv = P.op("dve", lambda eng, ost=ost: eng.tensor_reduce(
                            out=PSC[:], in_=ost[:].rearrange("p (b t) -> p b t", b=64), axis=AX.X, op=ALU.add), deps=ev4 + pool_evs[tagp[0]], tag=f"n_pool{tagp}{co}")
                        pv2 = P.op("dve", lambda eng, pool_to=pool_to, co=co: eng.tensor_scalar(out=pool_to[:, co * 64:(co + 1) * 64], in0=PSC[:], scalar1=1.0 / 32.0, scalar2=None, op0=ALU.mult), deps=[pv], tag=f"n_poolm{tagp}{co}")
                        pool_evs[tagp[0]] = [pv2]
                    so = P.dma("sp", lambda eng, co=co, ost=ost, dram_o=dram_o: eng.dma_start(out=dram_o.rearrange("(c p) n -> p c n", p=128)[:, co, :], in_=ost[:]), deps=ev4, tag=f"n_st{tagp}{co}")
                    store_ops.append(so)
            else:
                for s2_ in range(NSTRIP // 2):
                    ost = OST[s2_ % 2]
                    ev4 = []
                    for sj in range(2):
                        s = s2_ * 2 + sj
                        for nch in range(2):
                            bi, ps, war = pbank.get()
                            def vmm(eng, s=s, nch=nch, ps=ps):
                                inst = None
                                for c in range(DC):
                                    inst = eng.matmul(ps[:, :], xhatT[:, c * NTOK + s * 128:c * NTOK + (s + 1) * 128],
                                                      WB[:, c * 1024 + nch * 512:c * 1024 + (nch + 1) * 512],
                                                      start=(c == 0), stop=False)
                                return eng.matmul(ps[:, :], onesr_sb[0:1, :], BVR[0:1, nch * 512:(nch + 1) * 512], start=False, stop=True)
                            mm = P.op("pe", vmm, deps=xT_ops + [lw, lb, l_on, war, store_ops[-1] if store_ops else None], tag=f"n_{tagp}m{s}_{nch}")
                            readers.append(mm)
                            ev = P.op("dve", lambda eng, ps=ps, ost=ost, sj=sj, nch=nch: eng.tensor_copy(out=ost[:, sj * 1024 + nch * 512:sj * 1024 + (nch + 1) * 512], in_=ps[:, :]), deps=[mm], tag=f"n_{tagp}e{s}_{nch}")
                            pbank.done(bi, ev)
                            ev4.append(ev)
                    so = P.dma("sp", lambda eng, s2_=s2_, ost=ost, dram_o=dram_o: eng.dma_start(out=dram_o.rearrange("(s p) d -> p s d", p=128)[:, s2_ * 2:s2_ * 2 + 2, :], in_=ost[:].rearrange("p (a b) -> p a b", a=2)), deps=ev4, tag=f"n_st{tagp}{s2_}")
                    store_ops.append(so)
            return readers

        prev_w = proj(Wq, "col", bq, "fT", qT_o, "q")
        prev_w = proj(Wk, "col", bk, "fT", kT_o, "kf", pool_to=CKT)
        bkr = nc.dram_tensor("bkr", [1, D], F32, kind="ExternalInput").ap()
        prev_w = proj(Wk, "row", bkr, "tm", ktm_o, "kt")
        vT_i = nc.dram_tensor("vT_i", [D, NTOK], BF16, kind="Internal").ap()
        bvc = nc.dram_tensor("bvc", [128, 8], F32, kind="ExternalInput").ap()
        prev_w = proj(Wv, "col", bvc, "fT", vT_i, "vf", pool_to=CVT)
        prev_w = proj(Wv, "row", bv, "tm", v_o, "v")
        sck = P.dma("sp", lambda eng: eng.dma_start(out=ckT_o.rearrange("(c p) n -> p c n", p=128), in_=CKT[:].rearrange("p (c n) -> p c n", c=DC)), deps=pool_evs["k"], tag="n_sck")
        store_ops.append(sck)
        cvps = al.ps((128, 512), F32, "n_cvps")
        cvps_bf = cvps.ap().bitcast(BF16)
        def cvtp(eng):
            inst = None
            for c in range(DC):
                inst = eng.transpose(cvps_bf[0:64, c * 128:(c + 1) * 128], CVT[:, c * 64:(c + 1) * 64], eyeb_sb[:])
            return inst
        tp = P.op("pe", cvtp, deps=pool_evs["v"] + [l_eb], tag="n_cvtp")
        CVM = sbf((64, 1024), BF16, "n_cvm")
        te = P.op("dve", lambda eng: eng.tensor_copy(out=CVM[:], in_=cvps_bf[0:64, 0:1024]), deps=[tp], tag="n_cvte")
        scv = P.dma("sp", lambda eng: eng.dma_start(out=cv_o[:, :], in_=CVM[:]), deps=[te], tag="n_scv")
        store_ops.append(scv)
        WGT = sbf((128, DC * 4), BF16, "n_wgt")
        BGR = sbf((1, 4), BF16, "n_bgr")
        lwg = P.dma("pool", lambda eng: eng.dma_start(out=WGT[:].rearrange("p (c e) -> p c e", c=DC)[:, :, 0:3], in_=Wg.rearrange("(c p) e -> p c e", p=128)), tag="n_lwg")
        lbg = P.dma("pool", lambda eng: eng.dma_start(out=BGR[0:1, 0:3], in_=bg[:, :]), tag="n_lbg")
        GST = sbf((128, 4 * NSTRIP), F32, "n_gst")
        g_evs = []
        gps = al.ps((128, 512), F32, "n_gps")
        prev_g = None
        for s in range(NSTRIP):
            def gmm(eng, s=s):
                for c in range(DC):
                    eng.matmul(gps[:, 0:4], xhatT[:, c * NTOK + s * 128:c * NTOK + (s + 1) * 128],
                               WGT[:, c * 4:(c + 1) * 4], start=(c == 0), stop=False)
                return eng.matmul(gps[:, 0:4], onesr_sb[0:1, :], BGR[0:1, :], start=False, stop=True)
            mm = P.op("pe", gmm, deps=[xT_ops[s], lwg, lbg, l_on, prev_g], tag=f"n_gmm{s}")
            ev = P.op("act", lambda eng, s=s: eng.activation(out=GST[:, s * 4:(s + 1) * 4], in_=gps[:, 0:4], func=AF.Sigmoid), deps=[mm], tag=f"n_gev{s}")
            prev_g = ev
            g_evs.append(ev)
        sg = P.dma("sp", lambda eng: eng.dma_start(out=g_o.rearrange("(s p) e -> p s e", p=128), in_=GST[:].rearrange("p (s e) -> p s e", e=4)), deps=g_evs, tag="n_sg")
        store_ops.append(sg)
        fin = P.op("pool", lambda eng: eng.nop(), deps=store_ops[-3:], tag="fin")
        fin.need = False
        P.emit()
    return nc

def build_s2b(parts=("bs", "pv")):
    nc = bass.Bass("TRN2", target_bir_lowering=False)
    qT = nc.dram_tensor("qT", [D, NTOK], BF16, kind="ExternalInput").ap()
    ckT = nc.dram_tensor("ckT", [D, 256], BF16, kind="ExternalInput").ap()
    cv = nc.dram_tensor("cv", [256, D], BF16, kind="ExternalInput").ap()
    eyeb = nc.dram_tensor("eyeb", [128, 128], BF16, kind="ExternalInput").ap()
    onesc = nc.dram_tensor("onesc", [128, 1], BF16, kind="ExternalInput").ap()
    co_o = nc.dram_tensor("co_o", [NTOK, D], F32, kind="ExternalOutput").ap()
    bs_o = nc.dram_tensor("bs_o", [128, 32], F32, kind="ExternalOutput").ap()
    import contextlib
    with contextlib.ExitStack() as stack:
        al = Alloc(nc, stack)
        P = Prog(nc)
        sbf = al.sb
        eyeb_sb = sbf((128, 128), BF16, "c_eyeb")
        ONESC = sbf((128, 1), BF16, "c_onesc")
        l_eb = P.dma("sp", lambda eng: eng.dma_start(out=eyeb_sb[:], in_=eyeb[:, :]))
        l_onc = P.dma("sp", lambda eng: eng.dma_start(out=ONESC[:], in_=onesc[:, :]))
        QSS = [sbf((64, 16 * 128), BF16, f"c_qss{i}") for i in range(2)]
        CKS = sbf((64, 16 * 256), BF16, "c_cks")
        CVS = sbf((128, 2 * 1024), BF16, "c_cvs")
        lk = P.dma("sp", lambda eng: eng.dma_start(out=CKS[:].rearrange("p (h n) -> p h n", h=16), in_=ckT.rearrange("(h n) c -> n h c", n=64)), tag="c_lk")
        lv = P.dma("sp", lambda eng: eng.dma_start(out=CVS[:].rearrange("p (g d) -> p g d", g=2), in_=cv.rearrange("(g p) d -> p g d", p=128)), tag="c_lv")
        SCST = sbf((128, 4096), F32, "c_scst")
        PN = sbf((128, 4096), BF16, "c_pn")
        PT = sbf((128, 4096), BF16, "c_pt")
        STT = sbf((128, 48), F32, "c_stt")
        COT = sbf((128, 1024), F32, "c_cot")
        BSE = sbf((128, 32), F32, "c_bse")
        bse_z = None
        pA = Banks(al, 4, name="c_pA")
        pB = Banks(al, 2, name="c_pB")
        bsps = al.ps((128, 32), F32, "c_bs")
        prev_pn_guard = []
        prev_pv = None
        prev_cot = None
        prev_stt = None
        prev_qss = [None, None]
        bs_mms = []
        out_ops = []
        for s in range(NSTRIP):
            rot = s % 2
            lq = P.dma("sp", lambda eng, s=s, rot=rot: eng.dma_start(out=QSS[rot][:].rearrange("p (h n) -> p h n", h=16), in_=qT.rearrange("(h n) tok -> n h tok", n=64)[:, :, s * 128:(s + 1) * 128]), deps=[prev_qss[rot]], tag=f"c_lq{s}")
            sc_evs = []
            sc_mms = []
            for hp in range(8):
                bi, ps, war = pA.get()
                def smm(eng, s=s, hp=hp, ps=ps, rot=rot):
                    inst = None
                    for hj in range(2):
                        h = hp * 2 + hj
                        inst = eng.matmul(ps[:, hj * 256:(hj + 1) * 256],
                                          QSS[rot][0:64, h * 128:(h + 1) * 128],
                                          CKS[0:64, h * 256:(h + 1) * 256],
                                          start=True, stop=True)
                    return inst
                mm = P.op("pe", smm, deps=[lq, lk, war], tag=f"c_smm{s}_{hp}")
                sc_mms.append(mm)
                dst = SCST[:, hp * 512:(hp + 1) * 512]
                ev = P.op("act", lambda eng, dst=dst, ps=ps: eng.activation(out=dst, in_=ps[:, :], func=AF.Copy), deps=[mm], tag=f"c_sev{s}_{hp}")
                pA.done(bi, ev)
                sc_evs.append(ev)
            scv = SCST[:].rearrange("p (h c) -> p h c", h=16)
            pnv = PN[:].rearrange("p (h c) -> p h c", h=16)
            o1 = P.op("dve", lambda eng: eng.tensor_reduce(out=STT[:, 0:16], in_=scv, axis=AX.X, op=ALU.max), deps=sc_evs + [prev_stt], tag=f"c_mx{s}")
            o2 = P.op("dve", lambda eng: eng.tensor_tensor(out=scv, in0=scv, in1=STT[:, 0:16].unsqueeze(2).broadcast_to([128, 16, 256]), op=ALU.subtract), deps=[o1], tag=f"c_sub{s}")
            o3 = P.op("act", lambda eng: eng.activation(out=SCST[:], in_=SCST[:], func=AF.Exp, scale=0.125), deps=[o2], tag=f"c_exp{s}")
            o4 = P.op("dve", lambda eng: eng.tensor_reduce(out=STT[:, 16:32], in_=scv, axis=AX.X, op=ALU.add), deps=[o3], tag=f"c_sum{s}")
            o5 = P.op("dve", lambda eng: eng.reciprocal(out=STT[:, 32:48], in_=STT[:, 16:32]), deps=[o4], tag=f"c_rc{s}")
            o6 = P.op("dve", lambda eng: eng.tensor_tensor(out=pnv, in0=scv, in1=STT[:, 32:48].unsqueeze(2).broadcast_to([128, 16, 256]), op=ALU.mult), deps=[o5] + prev_pn_guard, tag=f"c_pn{s}")
            prev_stt = o6
            def bsmm(eng, s=s):
                inst = None
                for h in range(16):
                    for ch in range(2):
                        inst = eng.matmul(bsps[:, 2 * h + ch:2 * h + ch + 1],
                                          PN[:, h * 256 + ch * 128:h * 256 + (ch + 1) * 128],
                                          ONESC[:, 0:1], start=(h == 0 and ch == 0),
                                          stop=(h == 15 and ch == 1))
                return inst
            if "bs" not in parts:
                class _FakeOp:
                    pass
                bs = None
            bs = P.op("pe", bsmm, deps=[o6, l_onc] + ([bs_mms[-1]] if bs_mms else []), tag=f"c_bs{s}") if "bs" in parts else None
            if "bs" in parts:
                if s == 0:
                    bsa = P.op("dve", lambda eng: eng.tensor_copy(out=BSE[:], in_=bsps[:, :]), deps=[bs], tag=f"c_bsa{s}")
                else:
                    bsa = P.op("dve", lambda eng: eng.tensor_tensor(out=BSE[:], in0=bsps[:, :], in1=BSE[:], op=ALU.add), deps=[bs, bs_mms[-1]], tag=f"c_bsa{s}")
                bs_mms.append(bsa)
            tp_evs = []
            last_tp = None
            if "pv" in parts:
                for hp in range(8):
                    bi, ps, war = pA.get()
                    ps_bf = ps.ap().bitcast(BF16)
                    def tpf(eng, hp=hp, ps_bf=ps_bf):
                        inst = None
                        for hj in range(2):
                            h = hp * 2 + hj
                            for ch in range(2):
                                inst = eng.transpose(ps_bf[:, (hj * 2 + ch) * 128:(hj * 2 + ch + 1) * 128],
                                                     PN[:, h * 256 + ch * 128:h * 256 + (ch + 1) * 128], eyeb_sb[:])
                        return inst
                    tp = P.op("pe", tpf, deps=[o6, l_eb, war], tag=f"c_tp{s}_{hp}")
                    last_tp = tp
                    dst = PT[:, hp * 512:(hp + 1) * 512]
                    ev = P.op("dve", lambda eng, dst=dst, ps_bf=ps_bf: eng.tensor_copy(out=dst, in_=ps_bf[:, 0:512]), deps=[tp, prev_pv], tag=f"c_ptev{s}_{hp}")
                    pA.done(bi, ev)
                    tp_evs.append(ev)
            prev_pn_guard = [o for o in (bs, last_tp) if o is not None] or [o6]
            if "pv" in parts:
                bi1, ps1, war1 = pB.get()
                bi2, ps2, war2 = pB.get()
                def pvmm(eng, ps1=ps1, ps2=ps2):
                    inst = None
                    for h in range(16):
                        pj = ps1 if h < 8 else ps2
                        for ch in range(2):
                            inst = eng.matmul(pj[:, (h % 8) * 64:(h % 8) * 64 + 64],
                                              PT[:, (2 * h + ch) * 128:(2 * h + ch + 1) * 128],
                                              CVS[:, ch * 1024 + h * 64:ch * 1024 + (h + 1) * 64],
                                              start=(ch == 0), stop=(ch == 1))
                    return inst
                mm = P.op("pe", pvmm, deps=tp_evs + [lv, war1, war2], tag=f"c_pv{s}")
                prev_pv = mm
                e1 = P.op("dve", lambda eng, ps1=ps1: eng.tensor_copy(out=COT[:, 0:512], in_=ps1[:, :]), deps=[mm, prev_cot], tag=f"c_ce1{s}")
                e2 = P.op("dve", lambda eng, ps2=ps2: eng.tensor_copy(out=COT[:, 512:1024], in_=ps2[:, :]), deps=[mm, prev_cot], tag=f"c_ce2{s}")
                pB.done(bi1, e1)
                pB.done(bi2, e2)
            else:
                e1 = P.op("dve", lambda eng: eng.tensor_copy(out=COT[:, 0:1024], in_=PN[:, 0:1024]), deps=[o6, prev_cot], tag=f"c_ce1{s}")
                e2 = e1
                prev_pn_guard.append(e1)
            so = P.dma("sp", lambda eng, s=s: eng.dma_start(out=co_o.rearrange("(s p) d -> p s d", p=128)[:, s, :], in_=COT[:]), deps=[e1, e2], tag=f"c_so{s}")
            prev_cot = so
            out_ops.append(so)
        sbs = P.dma("sp", lambda eng: eng.dma_start(out=bs_o[:, :], in_=BSE[:]), deps=[bs_mms[-1] if bs_mms else out_ops[-1]], tag="c_sbs")
        fin = P.op("pool", lambda eng: eng.nop(), deps=[sbs, out_ops[-1]], tag="fin")
        fin.need = False
        P.emit()
    return nc

def build_s2c():
    nc = bass.Bass("TRN2", target_bir_lowering=False)
    x = nc.dram_tensor("x", [NTOK, D], F32, kind="ExternalInput").ap()
    qT = nc.dram_tensor("qT", [D, NTOK], BF16, kind="ExternalInput").ap()
    ktm = nc.dram_tensor("ktm", [SEQ, D], BF16, kind="ExternalInput").ap()
    vfull = nc.dram_tensor("vfull", [SEQ, D], BF16, kind="ExternalInput").ap()
    kTw = nc.dram_tensor("kTw", [D, NTOK + 128], BF16, kind="ExternalInput").ap()
    vw = nc.dram_tensor("vw", [NTOK + 128, D], BF16, kind="ExternalInput").ap()
    bsp = [nc.dram_tensor(f"bsp{i}", [128, 32], F32, kind="ExternalInput").ap() for i in range(4)]
    gates = nc.dram_tensor("gates", [NTOK, 4], F32, kind="ExternalInput").ap()
    comp = nc.dram_tensor("comp", [NTOK, D], F32, kind="ExternalInput").ap()
    Wo = nc.dram_tensor("Wo", [D, D], F32, kind="ExternalInput").ap()
    bo = nc.dram_tensor("bo", [1, D], F32, kind="ExternalInput").ap()
    maskA = nc.dram_tensor("maskA", [128, 256], F32, kind="ExternalInput").ap()
    maskB = nc.dram_tensor("maskB", [128, 256], F32, kind="ExternalInput").ap()
    iota32 = nc.dram_tensor("iota32", [16, 256], F32, kind="ExternalInput").ap()
    eyeb = nc.dram_tensor("eyeb", [128, 128], BF16, kind="ExternalInput").ap()
    eyef = nc.dram_tensor("eyef", [128, 128], F32, kind="ExternalInput").ap()
    onesr = nc.dram_tensor("onesr", [1, 128], BF16, kind="ExternalInput").ap()
    scrbs = nc.dram_tensor("scrbs", [16, 256], F32, kind="Internal").ap()
    x_o = nc.dram_tensor("x_o", [NTOK, D], F32, kind="ExternalOutput").ap()


    import contextlib
    with contextlib.ExitStack() as stack:
        al = Alloc(nc, stack)
        P = Prog(nc)
        sbf = al.sb
        eyeb_sb = sbf((128, 128), BF16, "t_eyeb")
        eyef_sb = sbf((128, 128), F32, "t_eyef")
        onesr_sb = sbf((1, 128), BF16, "t_onesr")
        l_eb = P.dma("sp", lambda eng: eng.dma_start(out=eyeb_sb[:], in_=eyeb[:, :]))
        l_ef = P.dma("sp", lambda eng: eng.dma_start(out=eyef_sb[:], in_=eyef[:, :]))
        l_on = P.dma("sp", lambda eng: eng.dma_start(out=onesr_sb[:], in_=onesr[:, :]))
        BS4 = sbf((128, 128), F32, "t_bs4")
        l_bs = [P.dma("sp", lambda eng, i=i: eng.dma_start(out=BS4[:, i * 32:(i + 1) * 32], in_=bsp[i][:, :]), tag=f"t_lbs{i}") for i in range(4)]
        a1 = P.op("dve", lambda eng: eng.tensor_tensor(out=BS4[:, 0:32], in0=BS4[:, 0:32], in1=BS4[:, 32:64], op=ALU.add), deps=l_bs[:2], tag="t_a1")
        a2 = P.op("dve", lambda eng: eng.tensor_tensor(out=BS4[:, 0:32], in0=BS4[:, 0:32], in1=BS4[:, 64:96], op=ALU.add), deps=[a1, l_bs[2]], tag="t_a2")
        a3 = P.op("dve", lambda eng: eng.tensor_tensor(out=BS4[:, 0:32], in0=BS4[:, 0:32], in1=BS4[:, 96:128], op=ALU.add), deps=[a2, l_bs[3]], tag="t_a3")
        scr2 = nc.dram_tensor("scr2", [32, 128], F32, kind="Internal").ap()
        tps = al.ps((128, 512), F32, "t_tps")
        bps = tps
        btp = P.op("pe", lambda eng: eng.transpose(bps[0:32, 128:256], BS4[:, 0:32], eyef_sb[:]), deps=[a3, l_ef], tag="t_btp")
        BST = sbf((32, 128), F32, "t_bst")
        bev = P.op("dve", lambda eng: eng.tensor_copy(out=BST[:], in_=bps[0:32, 128:256]), deps=[btp], tag="t_bev")
        b1 = P.dma("sp", lambda eng: eng.dma_start(out=scr2[:, :], in_=BST[:]), deps=[bev], tag="t_bso")
        BS16 = sbf((16, 256), F32, "t_bs16")
        b2 = P.dma("sp", lambda eng: eng.dma_start(out=BS16[:], in_=scr2.rearrange("(h t) c -> h (t c)", t=2)), deps=[b1], tag="t_bsi")
        M8 = sbf((16, 8), F32, "t_m8")
        MI = sbf((16, 8), U32, "t_mi")
        MIF = sbf((16, 256), F32, "t_mif")
        o1 = P.op("dve", lambda eng: eng.max(out=M8[:], in_=BS16[:]), deps=[b2], tag="t_max8")
        o2 = P.op("dve", lambda eng: eng.max_index(out=MI[:], in_max=M8[:], in_values=BS16[:]), deps=[o1], tag="t_mi")
        o3 = P.op("dve", lambda eng: eng.tensor_copy(out=MIF[:, 0:8], in_=MI[:]), deps=[o2], tag="t_mif")
        o4 = P.op("dve", lambda eng: eng.tensor_scalar_mul(MIF[:, 0:8], MIF[:, 0:8], 32.0), deps=[o3], tag="t_mi32")
        IOT = sbf((16, 256), F32, "t_iot")
        l_io = P.dma("sp", lambda eng: eng.dma_start(out=IOT[:], in_=iota32[:, :]))
        MIFX = sbf((16, 256), F32, "t_mifx")
        o5 = P.op("dve", lambda eng: eng.tensor_tensor(out=MIFX[:].rearrange("p (j o) -> p j o", j=8), in0=MIF[:, 0:8].unsqueeze(2).broadcast_to([16, 8, 32]), in1=IOT[:].rearrange("p (j o) -> p j o", j=8), op=ALU.add), deps=[o4, l_io], tag="t_idx")
        MIF2 = sbf((16, 128), F32, "t_mif2")
        o5b = P.op("dve", lambda eng: eng.tensor_copy(out=MIF2[:], in_=MIFX[:, 128:256]), deps=[o5], tag="t_mif2cp")
        tps2 = al.ps((128, 512), F32, "t_tps2")
        o6 = P.op("pe", lambda eng: [eng.transpose(tps[:, 0:16], MIFX[0:16, 0:128], eyef_sb[0:16, 0:16]),
                                     eng.transpose(tps2[:, 0:16], MIF2[0:16, 0:128], eyef_sb[0:16, 0:16])][-1], deps=[o5, o5b, l_ef], tag="t_idxtp")
        IDXF = sbf((128, 32), F32, "t_idxf")
        IDXU = sbf((128, 32), U32, "t_idxu")
        o7a = P.op("dve", lambda eng: eng.tensor_copy(out=IDXF[:, 0:16], in_=tps[:, 0:16]), deps=[o6], tag="t_idxev_a")
        o7 = P.op("dve", lambda eng: eng.tensor_copy(out=IDXF[:, 16:32], in_=tps2[:, 0:16]), deps=[o6, o7a], tag="t_idxev")
        o8 = P.op("dve", lambda eng: eng.tensor_copy(out=IDXU[:], in_=IDXF[:]), deps=[o7], tag="t_idxu")
        SELKT = sbf((64, 16 * 256), BF16, "t_selkT")
        SELV = sbf((128, 16 * 128), BF16, "t_selv")
        GK = [sbf((128, 1024), BF16, f"t_gk{i}") for i in range(2)]
        GV = [sbf((128, 1024), BF16, f"t_gv{i}") for i in range(2)]
        gk_ops = []
        selk_tp_ops = []
        gps2 = tps
        gps2_bf = gps2.ap().bitcast(BF16)
        prev_tpev = bev
        for h in range(16):
            for half in range(2):
                rot = (h * 2 + half) % 2
                g1 = P.dma("pool", lambda eng, h=h, half=half, rot=rot: eng.indirect_dma_start(
                    out=GK[rot][:, :], out_offset=None, in_=ktm[:, :],
                    in_offset=bass.IndirectOffsetOnAxis(ap=IDXU[:, h + 16 * half:h + 16 * half + 1], axis=0)),
                    deps=[o8, gk_ops[-4] if len(gk_ops) >= 4 else None], tag=f"t_gk{h}_{half}")
                g2 = P.dma("pool", lambda eng, h=h, half=half, rot=rot: eng.indirect_dma_start(
                    out=GV[rot][:, :], out_offset=None, in_=vfull[:, :],
                    in_offset=bass.IndirectOffsetOnAxis(ap=IDXU[:, h + 16 * half:h + 16 * half + 1], axis=0)),
                    deps=[o8, gk_ops[-3] if len(gk_ops) >= 4 else None], tag=f"t_gv{h}_{half}")
                gv_cp = P.op("dve", lambda eng, h=h, half=half, rot=rot: eng.tensor_copy(
                    out=SELV[:, (2 * h + half) * 64:(2 * h + half + 1) * 64], in_=GV[rot][:, h * 64:(h + 1) * 64]), deps=[g2], tag=f"t_gvc{h}_{half}")
                tp = P.op("pe", lambda eng, h=h, half=half, rot=rot: eng.transpose(
                    gps2_bf[0:64, (half) * 128:(half + 1) * 128],
                    GK[rot][:, h * 64:(h + 1) * 64], eyeb_sb[:]), deps=[g1, l_eb, prev_tpev], tag=f"t_sktp{h}_{half}")
                ev = P.op("dve", lambda eng, h=h, half=half: eng.tensor_copy(
                    out=SELKT[0:64, h * 256 + half * 128:h * 256 + (half + 1) * 128],
                    in_=gps2_bf[0:64, half * 128:(half + 1) * 128]), deps=[tp], tag=f"t_skev{h}_{half}")
                if half == 1:
                    prev_tpev = ev
                gk_ops.extend([tp, gv_cp])
                selk_tp_ops.append(ev)
                selk_tp_ops.append(gv_cp)
        gather_done = selk_tp_ops + gk_ops
        QSS = [sbf((64, 16 * 128), BF16, f"t_qss{i}") for i in range(2)]
        KW = [sbf((64, 16 * 256), BF16, f"t_kw{i}") for i in range(2)]
        VW = [sbf((128, 2 * 1024), BF16, f"t_vw{i}") for i in range(2)]
        MSK = sbf((128, 256), F32, "t_msk")
        MSKB = sbf((128, 256), F32, "t_mskb")
        l_mA = P.dma("sp", lambda eng: eng.dma_start(out=MSK[:], in_=maskA[:, :]))
        l_mB = P.dma("sp", lambda eng: eng.dma_start(out=MSKB[:], in_=maskB[:, :]))
        WOB = sbf((128, 8192), BF16, "t_wob")
        BOR = sbf((1, 1024), BF16, "t_bor")
        lwo = P.dma("pool", lambda eng: eng.dma_start(out=WOB[:].rearrange("p (c o) -> p c o", c=DC), in_=Wo.rearrange("(c p) o -> p c o", p=128)), tag="t_lwo")
        lbo = P.dma("pool", lambda eng: eng.dma_start(out=BOR[:], in_=bo[:, :]), tag="t_lbo")
        SCST = sbf((128, 4096), F32, "t_scst")
        PN = sbf((128, 4096), BF16, "t_pn")
        PT = sbf((128, 4096), BF16, "t_pt")
        STT = sbf((128, 48), F32, "t_stt")
        OT = sbf((128, 1024), F32, "t_ot")
        OTB = sbf((128, 1024), BF16, "t_otb")
        OTT = sbf((128, 1024), BF16, "t_ott")
        GTS = sbf((128, 4), F32, "t_gts")
        CMP = sbf((128, 1024), F32, "t_cmp")
        XST = sbf((128, 1024), F32, "t_xst")
        pA = Banks(al, 4, name="t_pA")
        pB = Banks(al, 2, name="t_pB")
        prev = {"pn": [], "pv": None, "stt": None, "ot": None, "kw": [None, None], "vw": [None, None],
                "cmp": None, "gts": None, "xst": None, "otb": None, "ott": None, "qss": [None, None]}
        out_ops = []
        for s in range(NSTRIP):
            rot = s % 2
            lq = P.dma("sp", lambda eng, s=s, rot=rot: eng.dma_start(out=QSS[rot][:].rearrange("p (h n) -> p h n", h=16), in_=qT.rearrange("(h n) tok -> n h tok", n=64)[:, :, s * 128:(s + 1) * 128]), deps=[prev["qss"][rot]], tag=f"t_lq{s}")
            lkw = P.dma("act", lambda eng, s=s, rot=rot: eng.dma_start(out=KW[rot][:].rearrange("p (h n) -> p h n", h=16), in_=kTw.rearrange("(h n) tok -> n h tok", n=64)[:, :, s * 128:s * 128 + 256]), deps=[prev["kw"][rot]], tag=f"t_lkw{s}")
            lvw = P.dma("act", lambda eng, s=s, rot=rot: eng.dma_start(out=VW[rot][:].rearrange("p (g d) -> p g d", g=2), in_=vw.rearrange("(g p) d -> p g d", p=128)[:, s:s + 2, :]), deps=[prev["vw"][rot]], tag=f"t_lvw{s}")
            lcmp = P.dma("sp", lambda eng, s=s: eng.dma_start(out=CMP[:], in_=comp.rearrange("(s p) d -> p s d", p=128)[:, s, :]), deps=[prev["cmp"]], tag=f"t_lcmp{s}")
            lgts = P.dma("sp", lambda eng, s=s: eng.dma_start(out=GTS[:], in_=gates.rearrange("(s p) e -> p s e", p=128)[:, s, :]), deps=[prev["gts"]], tag=f"t_lgts{s}")
            lx = P.dma("sp", lambda eng, s=s: eng.dma_start(out=XST[:], in_=x.rearrange("(s p) d -> p s d", p=128)[:, s, :]), deps=[prev["xst"]], tag=f"t_lx{s}")
            sel_evs = []
            sel_mms = []
            for hp in range(8):
                bi, ps, war = pA.get()
                def smm(eng, s=s, hp=hp, ps=ps, rot=rot):
                    inst = None
                    for hj in range(2):
                        h = hp * 2 + hj
                        inst = eng.matmul(ps[:, hj * 256:(hj + 1) * 256],
                                          QSS[rot][0:64, h * 128:(h + 1) * 128],
                                          SELKT[0:64, h * 256:(h + 1) * 256],
                                          start=True, stop=True)
                    return inst
                mm = P.op("pe", smm, deps=[lq, war] + (gather_done if s == 0 else []), tag=f"t_smm{s}_{hp}")
                sel_mms.append(mm)
                dst = SCST[:, hp * 512:(hp + 1) * 512]
                ev = P.op("act", lambda eng, dst=dst, ps=ps: eng.activation(out=dst, in_=ps[:, :], func=AF.Copy), deps=[mm], tag=f"t_sev{s}_{hp}")
                pA.done(bi, ev)
                sel_evs.append(ev)
            scv = SCST[:].rearrange("p (h c) -> p h c", h=16)
            pnv = PN[:].rearrange("p (h c) -> p h c", h=16)
            o1 = P.op("dve", lambda eng: eng.tensor_reduce(out=STT[:, 0:16], in_=scv, axis=AX.X, op=ALU.max), deps=sel_evs + [prev["stt"]], tag=f"t_mx{s}")
            o2 = P.op("dve", lambda eng: eng.tensor_tensor(out=scv, in0=scv, in1=STT[:, 0:16].unsqueeze(2).broadcast_to([128, 16, 256]), op=ALU.subtract), deps=[o1], tag=f"t_sub{s}")
            o3 = P.op("act", lambda eng: eng.activation(out=SCST[:], in_=SCST[:], func=AF.Exp, scale=0.125), deps=[o2], tag=f"t_exp{s}")
            o4 = P.op("dve", lambda eng: eng.tensor_reduce(out=STT[:, 16:32], in_=scv, axis=AX.X, op=ALU.add), deps=[o3], tag=f"t_sum{s}")
            o5 = P.op("dve", lambda eng: eng.reciprocal(out=STT[:, 32:48], in_=STT[:, 16:32]), deps=[o4], tag=f"t_rc{s}")
            o6 = P.op("dve", lambda eng: eng.tensor_tensor(out=pnv, in0=scv, in1=STT[:, 32:48].unsqueeze(2).broadcast_to([128, 16, 256]), op=ALU.mult), deps=[o5] + prev["pn"], tag=f"t_pn{s}")
            tp_evs = []
            last_tp = None
            for hp in range(8):
                bi, ps, war = pA.get()
                ps_bf = ps.ap().bitcast(BF16)
                def tpf(eng, hp=hp, ps_bf=ps_bf):
                    inst = None
                    for hj in range(2):
                        h = hp * 2 + hj
                        for ch in range(2):
                            inst = eng.transpose(ps_bf[:, (hj * 2 + ch) * 128:(hj * 2 + ch + 1) * 128],
                                                 PN[:, h * 256 + ch * 128:h * 256 + (ch + 1) * 128], eyeb_sb[:])
                    return inst
                tp = P.op("pe", tpf, deps=[o6, l_eb, war], tag=f"t_tp{s}_{hp}")
                last_tp = tp
                dst = PT[:, hp * 512:(hp + 1) * 512]
                ev = P.op("dve", lambda eng, dst=dst, ps_bf=ps_bf: eng.tensor_copy(out=dst, in_=ps_bf[:, 0:512]), deps=[tp, prev["pv"]], tag=f"t_ptev{s}_{hp}")
                pA.done(bi, ev)
                tp_evs.append(ev)
            bi1, ps1, war1 = pB.get()
            bi2, ps2, war2 = pB.get()
            def pvmm(eng, ps1=ps1, ps2=ps2):
                inst = None
                for h in range(16):
                    pj = ps1 if h < 8 else ps2
                    for ch in range(2):
                        inst = eng.matmul(pj[:, (h % 8) * 64:(h % 8) * 64 + 64],
                                          PT[:, (2 * h + ch) * 128:(2 * h + ch + 1) * 128],
                                          SELV[:, (2 * h + ch) * 64:(2 * h + ch + 1) * 64],
                                          start=(ch == 0), stop=(ch == 1))
                return inst
            selmm = P.op("pe", pvmm, deps=tp_evs + [war1, war2] + (gather_done if s == 0 else []), tag=f"t_pv{s}")
            c1 = P.op("dve", lambda eng: eng.tensor_scalar(out=OT[:], in0=CMP[:], scalar1=GTS[:, 0:1], scalar2=None, op0=ALU.mult), deps=[lcmp, lgts, prev["ot"]], tag=f"t_c1{s}")
            c2 = P.op("dve", lambda eng, ps1=ps1: eng.scalar_tensor_tensor(out=OT[:, 0:512], in0=ps1[:, :], scalar=GTS[:, 1:2], in1=OT[:, 0:512], op0=ALU.mult, op1=ALU.add), deps=[selmm, c1], tag=f"t_c2{s}")
            c3 = P.op("dve", lambda eng, ps2=ps2: eng.scalar_tensor_tensor(out=OT[:, 512:1024], in0=ps2[:, :], scalar=GTS[:, 1:2], in1=OT[:, 512:1024], op0=ALU.mult, op1=ALU.add), deps=[selmm, c1], tag=f"t_c3{s}")
            pB.done(bi1, c2)
            pB.done(bi2, c3)
            sl_evs = []
            for hp in range(8):
                bi, ps, war = pA.get()
                def wmm(eng, s=s, hp=hp, ps=ps, rot=rot):
                    inst = None
                    for hj in range(2):
                        h = hp * 2 + hj
                        inst = eng.matmul(ps[:, hj * 256:(hj + 1) * 256],
                                          QSS[rot][0:64, h * 128:(h + 1) * 128],
                                          KW[rot][0:64, h * 256:(h + 1) * 256],
                                          start=True, stop=True)
                    return inst
                mm = P.op("pe", wmm, deps=[lq, lkw, war, o3 if hp == 0 else None, last_tp], tag=f"t_wmm{s}_{hp}")
                prev["qss"][rot] = mm
                dst = SCST[:, hp * 512:(hp + 1) * 512]
                ev = P.op("act", lambda eng, dst=dst, ps=ps: eng.activation(out=dst, in_=ps[:, :], func=AF.Copy), deps=[mm, o6], tag=f"t_wev{s}_{hp}")
                pA.done(bi, ev)
                sl_evs.append(ev)
            msk = MSKB if s == 0 else MSK
            w1 = P.op("dve", lambda eng, msk=msk: eng.tensor_tensor(out=scv, in0=scv, in1=msk[:].unsqueeze(1).broadcast_to([128, 16, 256]), op=ALU.add), deps=sl_evs + [l_mA, l_mB, o6], tag=f"t_wmsk{s}")
            w2 = P.op("dve", lambda eng: eng.tensor_reduce(out=STT[:, 0:16], in_=scv, axis=AX.X, op=ALU.max), deps=[w1], tag=f"t_wmx{s}")
            w3 = P.op("dve", lambda eng: eng.tensor_tensor(out=scv, in0=scv, in1=STT[:, 0:16].unsqueeze(2).broadcast_to([128, 16, 256]), op=ALU.subtract), deps=[w2], tag=f"t_wsub{s}")
            w4 = P.op("act", lambda eng: eng.activation(out=SCST[:], in_=SCST[:], func=AF.Exp, scale=0.125), deps=[w3], tag=f"t_wexp{s}")
            w5 = P.op("dve", lambda eng: eng.tensor_reduce(out=STT[:, 16:32], in_=scv, axis=AX.X, op=ALU.add), deps=[w4], tag=f"t_wsum{s}")
            w6 = P.op("dve", lambda eng: eng.reciprocal(out=STT[:, 32:48], in_=STT[:, 16:32]), deps=[w5], tag=f"t_wrc{s}")
            w7 = P.op("dve", lambda eng: eng.tensor_tensor(out=pnv, in0=scv, in1=STT[:, 32:48].unsqueeze(2).broadcast_to([128, 16, 256]), op=ALU.mult), deps=[w6, last_tp, o6], tag=f"t_wpn{s}")
            prev["stt"] = w7
            wtp_evs = []
            wlast_tp = None
            for hp in range(8):
                bi, ps, war = pA.get()
                ps_bf = ps.ap().bitcast(BF16)
                def tpf2(eng, hp=hp, ps_bf=ps_bf):
                    inst = None
                    for hj in range(2):
                        h = hp * 2 + hj
                        for ch in range(2):
                            inst = eng.transpose(ps_bf[:, (hj * 2 + ch) * 128:(hj * 2 + ch + 1) * 128],
                                                 PN[:, h * 256 + ch * 128:h * 256 + (ch + 1) * 128], eyeb_sb[:])
                    return inst
                tp = P.op("pe", tpf2, deps=[w7, l_eb, war], tag=f"t_wtp{s}_{hp}")
                wlast_tp = tp
                dst = PT[:, hp * 512:(hp + 1) * 512]
                ev = P.op("dve", lambda eng, dst=dst, ps_bf=ps_bf: eng.tensor_copy(out=dst, in_=ps_bf[:, 0:512]), deps=[tp, selmm], tag=f"t_wptev{s}_{hp}")
                pA.done(bi, ev)
                wtp_evs.append(ev)
            prev["pn"] = [wlast_tp]
            bi1, ps1, war1 = pB.get()
            bi2, ps2, war2 = pB.get()
            def wvmm(eng, ps1=ps1, ps2=ps2, rot=rot):
                inst = None
                for h in range(16):
                    pj = ps1 if h < 8 else ps2
                    for ch in range(2):
                        inst = eng.matmul(pj[:, (h % 8) * 64:(h % 8) * 64 + 64],
                                          PT[:, (2 * h + ch) * 128:(2 * h + ch + 1) * 128],
                                          VW[rot][:, ch * 1024 + h * 64:ch * 1024 + (h + 1) * 64],
                                          start=(ch == 0), stop=(ch == 1))
                return inst
            slimm = P.op("pe", wvmm, deps=wtp_evs + [lvw, war1, war2], tag=f"t_wpv{s}")
            prev["pv"] = slimm
            prev["kw"][rot] = slimm
            prev["vw"][rot] = slimm
            c4 = P.op("dve", lambda eng, ps1=ps1: eng.scalar_tensor_tensor(out=OT[:, 0:512], in0=ps1[:, :], scalar=GTS[:, 2:3], in1=OT[:, 0:512], op0=ALU.mult, op1=ALU.add), deps=[slimm, c2], tag=f"t_c4{s}")
            c5 = P.op("dve", lambda eng, ps2=ps2: eng.scalar_tensor_tensor(out=OT[:, 512:1024], in0=ps2[:, :], scalar=GTS[:, 2:3], in1=OT[:, 512:1024], op0=ALU.mult, op1=ALU.add), deps=[slimm, c3], tag=f"t_c5{s}")
            pB.done(bi1, c4)
            pB.done(bi2, c5)
            prev["ot"] = c5
            prev["cmp"] = c1
            prev["gts"] = c5
            cb = P.op("act", lambda eng: eng.activation(out=OTB[:], in_=OT[:], func=AF.Copy), deps=[c4, c5, prev["otb"]], tag=f"t_cb{s}")
            prev["otb"] = cb
            bi, ps, war = pA.get()
            ps_bf = ps.ap().bitcast(BF16)
            def otpf(eng, ps_bf=ps_bf):
                inst = None
                for c in range(DC):
                    inst = eng.transpose(ps_bf[:, c * 128:(c + 1) * 128], OTB[:, c * 128:(c + 1) * 128], eyeb_sb[:])
                return inst
            otp = P.op("pe", otpf, deps=[cb, l_eb, war], tag=f"t_otp{s}")
            ote = P.op("dve", lambda eng, ps_bf=ps_bf: eng.tensor_copy(out=OTT[:], in_=ps_bf[:, 0:1024]), deps=[otp, prev["ott"]], tag=f"t_ote{s}")
            pA.done(bi, ote)
            evs = []
            for nch in range(2):
                bi2_, ps2_, war2_ = pB.get()
                def omm(eng, nch=nch, ps2_=ps2_):
                    for c in range(DC):
                        eng.matmul(ps2_[:, :], OTT[:, c * 128:(c + 1) * 128],
                                   WOB[:, c * 1024 + nch * 512:c * 1024 + (nch + 1) * 512],
                                   start=(c == 0), stop=False)
                    return eng.matmul(ps2_[:, :], onesr_sb[0:1, :], BOR[0:1, nch * 512:(nch + 1) * 512], start=False, stop=True)
                mm = P.op("pe", omm, deps=[ote, lwo, lbo, l_on, war2_], tag=f"t_omm{s}_{nch}")
                ev = P.op("dve", lambda eng, ps2_=ps2_, nch=nch: eng.tensor_tensor(out=XST[:, nch * 512:(nch + 1) * 512], in0=ps2_[:, :], in1=XST[:, nch * 512:(nch + 1) * 512], op=ALU.add), deps=[mm, lx], tag=f"t_oev{s}_{nch}")
                pB.done(bi2_, ev)
                evs.append(ev)
            prev["ott"] = mm
            so = P.dma("sp", lambda eng, s=s: eng.dma_start(out=x_o.rearrange("(s p) d -> p s d", p=128)[:, s, :], in_=XST[:]), deps=evs, tag=f"t_so{s}")
            prev["xst"] = so
            out_ops.append(so)
        fin = P.op("pool", lambda eng: eng.nop(), deps=out_ops[-2:], tag="fin")
        fin.need = False
        P.emit()
    return nc


def _consts():
    import ml_dtypes
    c = {}
    c["eye_b"] = np.eye(128).astype(ml_dtypes.bfloat16)
    c["eye_f"] = np.eye(128).astype(np.float32)
    c["ones_r"] = np.ones((1, 128)).astype(ml_dtypes.bfloat16)
    c["ones_c"] = np.ones((128, 1)).astype(ml_dtypes.bfloat16)
    c["iota16"] = np.tile(np.arange(16, dtype=np.float32), (128, 1))
    expm = np.zeros((16, 1024), np.float32)
    for io in range(8):
        for r in range(128):
            expm[2 * io + r // 64, io * 128 + r] = 1.0
    c["expm"] = expm.astype(ml_dtypes.bfloat16)
    ET = np.zeros((64, 1024), np.float32)
    MGT = np.zeros((64, 1024), np.float32)
    MEQ = np.zeros((64, 1024), np.float32)
    for t in range(64):
        for s in range(64):
            if t > s:
                ET[s, t::64] = t - s - 1
    for h in range(16):
        for t in range(64):
            for s in range(64):
                if t > s:
                    ET[s, h * 64 + t] = t - s - 1
                    MGT[s, h * 64 + t] = 1.0
                if t == s:
                    MEQ[s, h * 64 + t] = 1.0
    c["ET"] = ET.astype(ml_dtypes.bfloat16)
    c["MGT"] = MGT.astype(ml_dtypes.bfloat16)
    c["MEQ"] = MEQ.astype(ml_dtypes.bfloat16)
    c["E2C"] = np.stack([63 - np.arange(128) % 64, np.arange(128) % 64], axis=1).astype(np.float32)
    maskA = np.full((128, 256), -1e9, np.float32)
    for qi in range(128):
        maskA[qi, qi + 1:qi + 129] = 0.0
    maskB = maskA.copy()
    maskB[:, :128] = -1e9
    c["maskA"] = maskA
    c["maskB"] = maskB
    c["iota32"] = np.tile(np.arange(32, dtype=np.float32), (16, 8))
    return c


_NC_CACHE = {}


def _get_nc(name):
    if name not in _NC_CACHE:
        _NC_CACHE[name] = globals()["build_" + name]()
    return _NC_CACHE[name]


def _launch(name, in_maps):
    nc = _get_nc(name)
    return run_bass_kernel_spmd(nc, in_maps, core_ids=list(range(8))).results


def _colform(b):
    return np.ascontiguousarray(b.reshape(8, 128).T)


def kernel(**I):
    import ml_dtypes
    C = _consts()
    x0 = np.asarray(I["L0_attn_a_x"], dtype=np.float32)

    def shards(X):
        return [np.ascontiguousarray(X[c // 4, (c % 4) * 2048:(c % 4 + 1) * 2048]) for c in range(8)]

    def gather(res, key):
        b0 = np.concatenate([res[c][key] for c in range(4)], axis=0)
        b1 = np.concatenate([res[c][key] for c in range(4, 8)], axis=0)
        return np.stack([b0, b1], axis=0)

    X = x0

    xs = shards(X)
    im = [{"x": xs[c], "wqk": I["L0_attn_a_Wqk"], "wv": I["L0_attn_a_Wv"],
           "eye": C["eye_b"], "ones": C["ones_c"]} for c in range(8)]
    ra = _launch("s0a", im)
    tau2 = np.asarray(I["L0_attn_a_tau"], np.float32).reshape(16, 1)
    im = []
    for c in range(8):
        g0 = (c // 4) * 4
        ins = {"x": xs[c], "vT_i": ra[c]["vT_o"], "tau": tau2, "wo": I["L0_attn_a_Wo"],
               "eyeb": C["eye_b"], "eyef": C["eye_f"], "onesr": C["ones_r"]}
        for i in range(4):
            ins[f"dot{i}"] = ra[g0 + i]["dotp_o"]
            ins[f"q2_{i}"] = ra[g0 + i]["q2p_o"]
        im.append(ins)
    rb = _launch("s0b", im)
    X = gather(rb, "x_o")

    def ffn_launch(X):
        xs = shards(X)
        im = []
        for c in range(8):
            ins = {"x": xs[c], "eyeb": C["eye_b"], "onesr": C["ones_r"],
                   "iota16": C["iota16"], "expm": C["expm"]}
            for t in ("q", "k", "v"):
                ins[f"{t}_nwT"] = np.ascontiguousarray(I[f"L0_ffn_{t}_node_w"].T)
                ins[f"{t}_nb"] = np.asarray(I[f"L0_ffn_{t}_node_b"], np.float32)[None, :]
                ins[f"{t}_w1a"] = np.ascontiguousarray(np.asarray(I[f"L0_ffn_{t}_w1"]).transpose(1, 0, 2).reshape(1024, 1024))
                ins[f"{t}_b1c"] = _colform(np.asarray(I[f"L0_ffn_{t}_b1"], np.float32).reshape(1024))
                ins[f"{t}_w2a"] = np.ascontiguousarray(np.asarray(I[f"L0_ffn_{t}_w2"]).reshape(1024, 1024))
                ins[f"{t}_b2l"] = np.asarray(I[f"L0_ffn_{t}_b2"], np.float32)
            im.append(ins)
        ra = _launch("s1a", im)
        kT = [np.concatenate([ra[g * 4 + i]["kT_o"] for i in range(4)], axis=1) for g in range(2)]
        vv = [np.concatenate([ra[g * 4 + i]["v_o"] for i in range(4)], axis=0) for g in range(2)]
        im = []
        for c in range(8):
            g = c // 4
            im.append({"x": xs[c], "qT": ra[c]["qT_o"], "kT": np.ascontiguousarray(kT[g]),
                       "v": np.ascontiguousarray(vv[g]), "eyeb": C["eye_b"]})
        rb = _launch("s1b", im)
        return gather(rb, "x_o")

    X = ffn_launch(X)

    xs = shards(X)
    bq_, bk_, bv_ = [np.asarray(I[f"L0_attn_b_b{t}"], np.float32) for t in ("q", "k", "v")]
    im = [{"x": xs[c], "Wq": I["L0_attn_b_Wq"], "Wk": I["L0_attn_b_Wk"], "Wv": I["L0_attn_b_Wv"],
           "Wg": I["L0_attn_b_Wg"], "bq": _colform(bq_), "bk": _colform(bk_), "bkr": bk_[None, :],
           "bv": bv_[None, :], "bvc": _colform(bv_), "bg": np.asarray(I["L0_attn_b_bg"], np.float32)[None, :],
           "eyeb": C["eye_b"], "onesr": C["ones_r"]} for c in range(8)]
    ra = _launch("s2a", im)
    ckT = [np.concatenate([ra[g * 4 + i]["ckT_o"] for i in range(4)], axis=1) for g in range(2)]
    cv = [np.concatenate([ra[g * 4 + i]["cv_o"] for i in range(4)], axis=0) for g in range(2)]
    im = []
    for c in range(8):
        g = c // 4
        im.append({"qT": ra[c]["qT_o"], "ckT": np.ascontiguousarray(ckT[g]),
                   "cv": np.ascontiguousarray(cv[g]), "eyeb": C["eye_b"], "onesc": C["ones_c"]})
    rb = _launch("s2b", im)
    kT = [np.concatenate([ra[g * 4 + i]["kT_o"] for i in range(4)], axis=1) for g in range(2)]
    ktm = [np.concatenate([ra[g * 4 + i]["ktm_o"] for i in range(4)], axis=0) for g in range(2)]
    vv = [np.concatenate([ra[g * 4 + i]["v_o"] for i in range(4)], axis=0) for g in range(2)]
    im = []
    for c in range(8):
        g = c // 4
        cc = c % 4
        if cc == 0:
            kTw = np.concatenate([np.zeros((1024, 128), ml_dtypes.bfloat16), kT[g][:, 0:2048]], axis=1)
            vw = np.concatenate([np.zeros((128, 1024), ml_dtypes.bfloat16), vv[g][0:2048]], axis=0)
        else:
            kTw = kT[g][:, cc * 2048 - 128:(cc + 1) * 2048]
            vw = vv[g][cc * 2048 - 128:(cc + 1) * 2048]
        ins = {"x": xs[c], "qT": ra[c]["qT_o"], "ktm": np.ascontiguousarray(ktm[g]),
               "vfull": np.ascontiguousarray(vv[g]), "kTw": np.ascontiguousarray(kTw),
               "vw": np.ascontiguousarray(vw), "gates": ra[c]["g_o"], "comp": rb[c]["co_o"],
               "Wo": I["L0_attn_b_Wo"], "bo": np.asarray(I["L0_attn_b_bo"], np.float32)[None, :],
               "maskA": C["maskA"], "maskB": C["maskB"] if cc == 0 else C["maskA"],
               "iota32": C["iota32"], "eyeb": C["eye_b"], "eyef": C["eye_f"], "onesr": C["ones_r"]}
        for i in range(4):
            ins[f"bsp{i}"] = rb[(c // 4) * 4 + i]["bs_o"]
        im.append(ins)
    rc = _launch("s2c", im)
    X = gather(rc, "x_o")

    X = ffn_launch(X)

    xs = shards(X)
    decay_row = np.asarray(I["L1_attn_a_time_decay"], np.float32).reshape(1, 16)
    faaaa_row = np.asarray(I["L1_attn_a_time_faaaa"], np.float32).reshape(1, 16)
    im = []
    for c in range(8):
        g = c // 4
        cc = c % 4
        xprev = np.zeros((1, 1024), np.float32) if cc == 0 else np.ascontiguousarray(X[g, cc * 2048 - 1:cc * 2048])
        ins = {"x": xs[c], "xprev": xprev, "decay_row": decay_row, "faaaa_row": faaaa_row,
               "eyeb": C["eye_b"], "onesr": C["ones_r"], "ET": C["ET"], "MGT": C["MGT"],
               "MEQ": C["MEQ"], "E2C": C["E2C"]}
        for m in "kvrg":
            ins[f"maa_{m}"] = _colform(np.asarray(I[f"L1_attn_a_time_maa_{m}"], np.float32).reshape(1024))
        for m in "rkvg":
            ins[f"W{m}T"] = np.ascontiguousarray(np.asarray(I[f"L1_attn_a_W{m}"]).T)
        im.append(ins)
    ra = _launch("s4a", im)
    im = []
    for c in range(8):
        g = c // 4
        cc = c % 4
        stepsv = np.array([2048.0 * (cc - 1 - i) if i < cc else 1e9 for i in range(4)], np.float32)
        ins = {"x": xs[c], "y_l": ra[c]["y_o"], "rT": ra[c]["rT_o"], "g": ra[c]["g_o"],
               "steps": np.tile(stepsv, (64, 1)), "decay_row": decay_row, "faaaa_row": faaaa_row,
               "ln_w": np.asarray(I["L1_attn_a_ln_w"], np.float32)[None, :],
               "ln_b": np.asarray(I["L1_attn_a_ln_b"], np.float32)[None, :],
               "WoT": np.ascontiguousarray(np.asarray(I["L1_attn_a_Wo"]).T),
               "eyeb": C["eye_b"], "onesr": C["ones_r"], "ET": C["ET"], "MGT": C["MGT"],
               "MEQ": C["MEQ"], "E2C": C["E2C"]}
        for i in range(4):
            ins[f"A{i}"] = ra[g * 4 + i]["A_o"]
        im.append(ins)
    rb = _launch("s4b", im)
    X = gather(rb, "x_o")

    def moe_launch(X):
        xs = shards(X)
        im = []
        base = {"gate_wT": np.ascontiguousarray(np.asarray(I["L1_ffn_gate_w"]).T),
                "gate_b": np.asarray(I["L1_ffn_gate_b"], np.float32)[None, :],
                "w1T": np.ascontiguousarray(np.asarray(I["L1_ffn_w1"]).transpose(0, 2, 1)),
                "w3T": np.ascontiguousarray(np.asarray(I["L1_ffn_w3"]).transpose(0, 2, 1)),
                "w2T": np.ascontiguousarray(np.asarray(I["L1_ffn_w2"]).transpose(0, 2, 1)),
                "b1a": np.ascontiguousarray(np.asarray(I["L1_ffn_b1"], np.float32).reshape(8, 8, 128).transpose(0, 2, 1)),
                "b3a": np.ascontiguousarray(np.asarray(I["L1_ffn_b3"], np.float32).reshape(8, 8, 128).transpose(0, 2, 1)),
                "b2a": np.asarray(I["L1_ffn_b2"], np.float32),
                "ws1T": np.ascontiguousarray(np.asarray(I["L1_ffn_ws1"]).T),
                "ws3T": np.ascontiguousarray(np.asarray(I["L1_ffn_ws3"]).T),
                "ws2T": np.ascontiguousarray(np.asarray(I["L1_ffn_ws2"]).T),
                "bs1": np.ascontiguousarray(np.asarray(I["L1_ffn_bs1"], np.float32).reshape(16, 128).T),
                "bs3": np.ascontiguousarray(np.asarray(I["L1_ffn_bs3"], np.float32).reshape(16, 128).T),
                "bs2": np.asarray(I["L1_ffn_bs2"], np.float32)[None, :],
                "eyeb": C["eye_b"], "onesr": C["ones_r"]}
        for c in range(8):
            ins = dict(base)
            ins["x"] = xs[c]
            im.append(ins)
        r = _launch("s5", im)
        return gather(r, "x_o")

    X = moe_launch(X)

    xs = shards(X)
    im = [{"x": xs[c], "wq12": I["L1_attn_b_w_q12"], "wk12": I["L1_attn_b_w_k12"],
           "wv": I["L1_attn_b_w_v"], "eyeb": C["eye_b"]} for c in range(8)]
    ra = _launch("s6a", im)
    k12T = [np.concatenate([ra[g * 4 + i]["k12T_o"] for i in range(4)], axis=1) for g in range(2)]
    vv = [np.concatenate([ra[g * 4 + i]["v_o"] for i in range(4)], axis=0) for g in range(2)]
    lams = np.stack([np.asarray(I["L1_attn_b_lambda_q1"], np.float32),
                     np.asarray(I["L1_attn_b_lambda_k1"], np.float32),
                     np.asarray(I["L1_attn_b_lambda_q2"], np.float32),
                     np.asarray(I["L1_attn_b_lambda_k2"], np.float32)])
    im = []
    for c in range(8):
        g = c // 4
        im.append({"x": xs[c], "qT": ra[c]["q12T_o"], "kT": np.ascontiguousarray(k12T[g]),
                   "v": np.ascontiguousarray(vv[g]), "lams": lams, "eyeb": C["eye_b"], "onesr": C["ones_r"]})
    rb = _launch("s6b", im)
    X = gather(rb, "x_o")

    X = moe_launch(X)
    return np.ascontiguousarray(X)
